# version 13
# baseline (speedup 1.0000x reference)
import contextlib
import math
import numpy as np
import ml_dtypes
import concourse.bass as bass
import concourse.mybir as mybir
from concourse.bass_utils import run_bass_kernel_spmd
from concourse.ap import AP

F32 = mybir.dt.float32
BF16 = mybir.dt.bfloat16
I32 = mybir.dt.int32
I8 = mybir.dt.int8
U8 = mybir.dt.uint8
AF = mybir.ActivationFunctionType
ALU = mybir.AluOpType
AX = mybir.AxisListType

BF = ml_dtypes.bfloat16
NCORE = 8
B, S, D, H, HD = 2, 4096, 4096, 32, 128
TOK = 1024
EH = 1024
EPS = 1e-5
NEG = -1.0e30
ISQ = 1.0 / math.sqrt(HD)

ENGS = ("sync", "gpsimd", "vector", "scalar", "tensor")


class _Sem:
    __slots__ = ("h", "n")
    def __init__(self, h):
        self.h = h
        self.n = 0


class Sched:

    def __init__(self, chains=None):
        self.q = {k: [] for k in ENGS}
        self.chains = chains or {}

    def w(self, eng, s, v):
        self.q[eng].append(("w", s.h, int(v)))

    def o(self, eng, fn, s=None, k=1):
        if s is None and eng in self.chains:
            s, k = self.chains[eng], 1
        if s is not None:
            s.n += k
            self.q[eng].append(("o", fn, s.h, k))
            return s.n
        self.q[eng].append(("o", fn, None, 0))
        return None

    def emit(self, blk):
        barriered = set(self.chains)

        def runner2(eng, items):
            barrier = eng in barriered
            counts = {}
            resolved = []
            for it in items:
                if it[0] == "w":
                    resolved.append(it)
                else:
                    semh, k = it[2], it[3]
                    if semh is not None:
                        counts[semh] = counts.get(semh, 0) + k
                        resolved.append(("o", it[1], semh, k, (semh, counts[semh])))
                    else:
                        resolved.append(("o", it[1], None, 0, None))

            def _f(e):
                prev = None
                for it in resolved:
                    if it[0] == "w":
                        e.wait_ge(it[1], it[2])
                    else:
                        if barrier and prev is not None:
                            e.wait_ge(prev[0], prev[1])
                        inst = it[1](e)
                        if it[2] is not None:
                            inst.then_inc(it[2], it[3])
                        if it[4] is not None:
                            prev = it[4]
            return _f
        blk.sync(runner2("sync", self.q["sync"]))
        blk.gpsimd(runner2("gpsimd", self.q["gpsimd"]))
        blk.vector(runner2("vector", self.q["vector"]))
        blk.scalar(runner2("scalar", self.q["scalar"]))
        blk.tensor(runner2("tensor", self.q["tensor"]))



def build_prep():
    nc = bass.Bass("TRN2", target_bir_lowering=False)
    wq = nc.dram_tensor("wq", [512, D], I32, kind="ExternalInput").ap()
    zz = nc.dram_tensor("zz", [512, 32], F32, kind="ExternalInput").ap()
    ss = nc.dram_tensor("ss", [512, 32], F32, kind="ExternalInput").ap()
    s2 = nc.dram_tensor("s2", [512, 1], F32, kind="ExternalInput").ap()
    mk = nc.dram_tensor("mk", [512, D], I32, kind="ExternalInput").ap()
    q3 = nc.dram_tensor("q3", [65536, 12], I8, kind="ExternalInput").ap()
    wn = nc.dram_tensor("wn", [65536, 1], F32, kind="ExternalInput").ap()
    cin = nc.dram_tensor("cin", [D, 4096], F32, kind="ExternalInput").ap()
    w0d = nc.dram_tensor("w0d", [512, D], BF16, kind="ExternalOutput").ap()
    w1d = nc.dram_tensor("w1d", [512, D], BF16, kind="ExternalOutput").ap()
    cout = nc.dram_tensor("cout", [D, 4096], BF16, kind="ExternalOutput").ap()

    with contextlib.ExitStack() as st:
        E = st.enter_context
        QF = E(nc.sbuf_tensor("QF", [128, D], F32))
        MF = E(nc.sbuf_tensor("MF", [128, D], F32))
        T0 = E(nc.sbuf_tensor("T0", [128, D], F32))
        WD0 = E(nc.sbuf_tensor("WD0", [128, D], BF16))
        ZS = E(nc.sbuf_tensor("ZS", [128, 32], F32))
        SC = E(nc.sbuf_tensor("SC", [128, 32], F32))
        S2b = E(nc.sbuf_tensor("S2b", [128, 1], F32))
        SCC = E(nc.sbuf_tensor("SCC", [128, 32], F32))
        BY = E(nc.sbuf_tensor("BY", [128, 1536], I32))
        Q1 = E(nc.sbuf_tensor("Q1", [128, D], I32))
        QF2 = E(nc.sbuf_tensor("QF2", [128, D], F32))
        NR = E(nc.sbuf_tensor("NR", [128, 128], F32))
        NE = E(nc.sbuf_tensor("NE", [128, D], F32))
        TA = E(nc.sbuf_tensor("TA", [128, 512], I32))
        TB = E(nc.sbuf_tensor("TB", [128, 512], I32))
        T1 = E(nc.sbuf_tensor("T1", [128, D], F32))
        WD1 = E(nc.sbuf_tensor("WD1", [128, D], BF16))
        CA = E(nc.sbuf_tensor("CA", [128, 4096], BF16))
        CB = E(nc.sbuf_tensor("CB", [128, 4096], BF16))
        sL = _Sem(E(nc.semaphore("sL")))
        sV = _Sem(E(nc.semaphore("sV")))
        sO = _Sem(E(nc.semaphore("sO")))
        sC = _Sem(E(nc.semaphore("sC")))
        cV = _Sem(E(nc.semaphore("cV")))
        cS = _Sem(E(nc.semaphore("cS")))
        blk = E(nc.Block())
        sc = Sched(chains={"vector": cV, "scalar": cS})

        for r in range(4):
            r0, r1 = r * 128, (r + 1) * 128
            g0 = r * 16384
            if r > 0:
                sc.w("gpsimd", sV, 2 * r)
                sc.w("sync", sV, 2 * r)
            sc.o("gpsimd", lambda e, a=wq[r0:r1, :], o=QF: e.dma_start(out=o[:], in_=a), sL, 16)
            sc.o("gpsimd", lambda e, a=mk[r0:r1, :], o=MF: e.dma_start(out=o[:], in_=a), sL, 16)
            sc.o("gpsimd", lambda e, a=q3.bitcast(U8)[g0:g0 + 16384, :].rearrange("(p g) b -> p (g b)", p=128), o=BY: e.dma_start(out=o[:], in_=a), sL, 16)
            sc.o("sync", lambda e, a=zz[r0:r1, :], o=ZS: e.dma_start(out=o[:], in_=a), sL, 16)
            sc.o("sync", lambda e, a=ss[r0:r1, :], o=SC: e.dma_start(out=o[:], in_=a), sL, 16)
            sc.o("sync", lambda e, a=s2[r0:r1, :], o=S2b: e.dma_start(out=o[:], in_=a), sL, 16)
            sc.o("sync", lambda e, a=wn[g0:g0 + 16384, :].rearrange("(p g) o -> p (g o)", p=128), o=NR: e.dma_start(out=o[:], in_=a), sL, 16)
            sc.w("vector", sL, sL.n)
            if r > 0:
                sc.w("vector", sO, 32 * r)
            sc.o("vector", lambda e: e.tensor_scalar(SCC[:], SC[:], S2b[:, 0:1], None, op0=ALU.mult))
            for g in range(32):
                sc.o("vector", lambda e, g=g: e.tensor_scalar(
                    T0[:, g * 128:(g + 1) * 128], QF[:, g * 128:(g + 1) * 128],
                    ZS[:, g:g + 1], SCC[:, g:g + 1], op0=ALU.subtract, op1=ALU.mult))
            sc.o("vector", lambda e: e.tensor_tensor(WD0[:], T0[:], MF[:], op=ALU.mult), sV, 1)
            b0, b1, b2 = BY[:, 0:1536:3], BY[:, 1:1536:3], BY[:, 2:1536:3]
            qs = lambda v: Q1[:, v:4096:8]
            sc.o("vector", lambda e: e.tensor_single_scalar(qs(0), b0, 7, op=ALU.bitwise_and))
            sc.o("vector", lambda e: e.tensor_scalar(qs(1), b0, 3, 7, op0=ALU.logical_shift_right, op1=ALU.bitwise_and))
            sc.o("vector", lambda e: e.tensor_scalar(TA[:], b0, 6, 3, op0=ALU.logical_shift_right, op1=ALU.bitwise_and))
            sc.o("vector", lambda e: e.tensor_scalar(TB[:], b1, 1, 2, op0=ALU.bitwise_and, op1=ALU.logical_shift_left))
            sc.o("vector", lambda e: e.tensor_tensor(qs(2), TA[:], TB[:], op=ALU.add))
            sc.o("vector", lambda e: e.tensor_scalar(qs(3), b1, 1, 7, op0=ALU.logical_shift_right, op1=ALU.bitwise_and))
            sc.o("vector", lambda e: e.tensor_scalar(qs(4), b1, 4, 7, op0=ALU.logical_shift_right, op1=ALU.bitwise_and))
            sc.o("vector", lambda e: e.tensor_scalar(TA[:], b1, 7, 1, op0=ALU.logical_shift_right, op1=ALU.bitwise_and))
            sc.o("vector", lambda e: e.tensor_scalar(TB[:], b2, 3, 1, op0=ALU.bitwise_and, op1=ALU.logical_shift_left))
            sc.o("vector", lambda e: e.tensor_tensor(qs(5), TA[:], TB[:], op=ALU.add))
            sc.o("vector", lambda e: e.tensor_scalar(qs(6), b2, 2, 7, op0=ALU.logical_shift_right, op1=ALU.bitwise_and))
            sc.o("vector", lambda e: e.tensor_scalar(qs(7), b2, 5, 7, op0=ALU.logical_shift_right, op1=ALU.bitwise_and))
            sc.o("vector", lambda e: e.tensor_copy(QF2[:], Q1[:]))
            sc.o("vector", lambda e: e.tensor_copy(
                NE[:].rearrange("p (g j) -> p g j", j=32),
                NR[:].unsqueeze(2).broadcast_to((128, 128, 32))))
            sc.o("vector", lambda e: e.scalar_tensor_tensor(T1[:], QF2[:], 2.0 / 7.0, NE[:], op0=ALU.mult, op1=ALU.mult))
            sc.o("vector", lambda e: e.tensor_tensor(WD1[:], T1[:], NE[:], op=ALU.subtract), sV, 1)
            sc.w("sync", sV, sV.n)
            sc.o("sync", lambda e, o=w0d[r0:r1, :]: e.dma_start(out=o, in_=WD0[:]), sO, 16)
            sc.o("sync", lambda e, o=w1d[r0:r1, :]: e.dma_start(out=o, in_=WD1[:]), sO, 16)

        for i in range(32):
            buf = CA if i % 2 == 0 else CB
            if i >= 2:
                sc.w("gpsimd", sC, 16 * (i - 1))
            sc.o("gpsimd", lambda e, a=cin[i * 128:(i + 1) * 128, :], o=buf: e.dma_start(out=o[:], in_=a), sL, 16)
            sc.w("sync", sL, sL.n)
            sc.o("sync", lambda e, o=cout[i * 128:(i + 1) * 128, :], b=buf: e.dma_start(out=o, in_=b[:]), sC, 16)
        sc.w("sync", sO, sO.n)
        sc.w("sync", sC, sC.n)
        sc.emit(blk)
    return nc



def build_qkv(layer):
    nc = bass.Bass("TRN2", target_bir_lowering=False)
    if layer == 0:
        xT = nc.dram_tensor("xT", [D, S], F32, kind="ExternalInput").ap()
    else:
        xT = nc.dram_tensor("xT", [D, S], BF16, kind="ExternalInput").ap()
    wT = nc.dram_tensor("wT", [D, 3 * EH], BF16, kind="ExternalInput").ap()
    if layer == 1:
        b3 = nc.dram_tensor("b3", [1, 3 * EH], F32, kind="ExternalInput").ap()
    qdt = F32 if layer == 0 else BF16
    qo = nc.dram_tensor("qo", [S, EH], qdt, kind="ExternalOutput").ap()
    ko = nc.dram_tensor("ko", [S, EH], qdt, kind="ExternalOutput").ap()
    vo = nc.dram_tensor("vo", [S, EH], BF16, kind="ExternalOutput").ap()
    outs = [qo, ko, vo]

    with contextlib.ExitStack() as st:
        E = st.enter_context
        X16 = E(nc.sbuf_tensor("X16", [128, 32 * 1024], BF16))
        W16 = [E(nc.sbuf_tensor(f"W16{i}", [128, 32 * 512], BF16)) for i in range(2)]
        ONES1 = E(nc.sbuf_tensor("ONES1", [1, 128], BF16))
        B16 = E(nc.sbuf_tensor("B16", [1, 3 * EH], BF16))
        EVF = [E(nc.sbuf_tensor(f"EVF{i}", [128, 512], F32)) for i in range(4)]
        EVB = [E(nc.sbuf_tensor(f"EVB{i}", [128, 512], BF16)) for i in range(4)]
        PS = [E(nc.psum_tensor(f"PS{i}", [128, 512], F32)) for i in range(4)]
        sX = _Sem(E(nc.semaphore("sX")))
        sW = [_Sem(E(nc.semaphore(f"sW{i}"))) for i in range(2)]
        sMM = _Sem(E(nc.semaphore("sMM")))
        sEV = _Sem(E(nc.semaphore("sEV")))
        sST = [_Sem(E(nc.semaphore(f"sST{i}"))) for i in range(4)]
        cV = _Sem(E(nc.semaphore("cV")))
        cS = _Sem(E(nc.semaphore("cS")))
        blk = E(nc.Block())
        sc = Sched(chains={"vector": cV, "scalar": cS})

        sc.o("vector", lambda e: e.memset(ONES1[:], 1.0))
        if layer == 1:
            sc.o("gpsimd", lambda e: e.dma_start(out=B16[:], in_=b3), sX, 16)

        def wload(w):
            tg, et = divmod(w, 6)
            src = wT[:, et * 512:(et + 1) * 512].rearrange("(i p) e -> p i e", p=128)
            dst = W16[w % 2][:].rearrange("p (i e) -> p i e", i=32)
            if w >= 2:
                sc.w("sync", sMM, (w - 1) * 8)
            sc.o("sync", lambda e, s=src, d=dst: e.dma_start(out=d, in_=s), sW[w % 2], 16)

        wload(0)
        wload(1)
        bias_off = 16 if layer == 1 else 0
        n = 0
        for tg in range(4):
            if tg > 0:
                sc.w("gpsimd", sMM, 48 * tg)
            src = xT[:, tg * 1024:(tg + 1) * 1024].rearrange("(i p) t -> p i t", p=128)
            dst = X16[:].rearrange("p (i t) -> p i t", i=32)
            sc.o("gpsimd", lambda e, s=src, d=dst: e.dma_start(out=d, in_=s), sX, 16)
            for et in range(6):
                w = tg * 6 + et
                sc.w("tensor", sX, 16 * (tg + 1) + bias_off)
                sc.w("tensor", sW[w % 2], 16 * (w // 2 + 1))
                sec = et // 2
                ecol = (et % 2) * 512
                for tc in range(8):
                    n += 1
                    bank = (n - 1) % 4
                    if n > 4:
                        sc.w("tensor", sEV, n - 4)
                    for d in range(32):
                        last = (d == 31) and (layer == 0)
                        sc.o("tensor", lambda e, d=d, b=bank, tc=tc, st=(d == 0), sp=last, wb=w % 2:
                             e.matmul(PS[b][:], X16[:, d * 1024 + tc * 128: d * 1024 + tc * 128 + 128],
                                      W16[wb][:, d * 512:(d + 1) * 512], start=st, stop=sp),
                             sMM if last else None, 1)
                    if layer == 1:
                        sc.o("tensor", lambda e, b=bank, c=et * 512:
                             e.matmul(PS[b][:], ONES1[:], B16[0:1, c:c + 512], start=False, stop=True),
                             sMM, 1)
                    sc.w("vector", sMM, n)
                    if n > 4:
                        sc.w("vector", sST[bank], 16 * ((n - 4 - 1) // 4 + 1))
                    if layer == 0:
                        tgt = EVF[bank] if sec < 2 else EVB[bank]
                        sc.o("vector", lambda e, b=bank, t=tgt: e.tensor_scalar(
                            t[:], PS[b][:], -8.0, 8.0, op0=ALU.max, op1=ALU.min), sEV, 1)
                    else:
                        tgt = EVB[bank]
                        if sec == 0:
                            sc.o("vector", lambda e, b=bank, t=tgt: e.tensor_scalar(
                                t[:], PS[b][:], 1.0 / 128.0, None, op0=ALU.mult), sEV, 1)
                        else:
                            sc.o("vector", lambda e, b=bank, t=tgt: e.tensor_copy(t[:], PS[b][:]), sEV, 1)
                    sc.w("sync", sEV, sEV.n)
                    t0 = tg * 1024 + tc * 128
                    sc.o("sync", lambda e, o=outs[sec][t0:t0 + 128, ecol:ecol + 512], t=tgt:
                         e.dma_start(out=o, in_=t[:]), sST[bank], 16)
                if w + 2 < 24:
                    wload(w + 2)
        for b in range(4):
            sc.w("sync", sST[b], sST[b].n)
        sc.emit(blk)
    return nc



def build_qkln():
    nc = bass.Bass("TRN2", target_bir_lowering=False)
    q = nc.dram_tensor("q", [TOK, D], F32, kind="ExternalInput").ap()
    k = nc.dram_tensor("k", [TOK, D], F32, kind="ExternalInput").ap()
    gq = nc.dram_tensor("gq", [1, D], F32, kind="ExternalInput").ap()
    gk = nc.dram_tensor("gk", [1, D], F32, kind="ExternalInput").ap()
    q2 = nc.dram_tensor("q2", [TOK, D], BF16, kind="ExternalOutput").ap()
    k2 = nc.dram_tensor("k2", [TOK, D], BF16, kind="ExternalOutput").ap()

    with contextlib.ExitStack() as st:
        E = st.enter_context
        GQ = E(nc.sbuf_tensor("GQ", [1, D], F32))
        GK = E(nc.sbuf_tensor("GK", [1, D], F32))
        G1 = E(nc.sbuf_tensor("G1", [1, D], F32))
        G16 = E(nc.sbuf_tensor("G16", [1, D], BF16))
        G32 = E(nc.sbuf_tensor("G32", [128, D], F32))
        ONES1 = E(nc.sbuf_tensor("ONES1", [1, 128], BF16))
        QC = E(nc.sbuf_tensor("QC", [128, D], F32))
        KC = E(nc.sbuf_tensor("KC", [128, D], F32))
        T1 = E(nc.sbuf_tensor("T1", [128, D], F32))
        Q2T = E(nc.sbuf_tensor("Q2T", [128, D], BF16))
        K2T = E(nc.sbuf_tensor("K2T", [128, D], BF16))
        ST6 = E(nc.sbuf_tensor("ST6", [128, 48], F32))
        MV = E(nc.sbuf_tensor("MV", [128, 2], F32))
        MVK = E(nc.sbuf_tensor("MVK", [128, 2], F32))
        SD = E(nc.sbuf_tensor("SD", [128, 1], F32))
        RS = E(nc.sbuf_tensor("RS", [128, 1], F32))
        SDK = E(nc.sbuf_tensor("SDK", [128, 1], F32))
        RSK = E(nc.sbuf_tensor("RSK", [128, 1], F32))
        EPSC = E(nc.sbuf_tensor("EPSC", [128, 1], F32))
        PSG = E(nc.psum_tensor("PSG", [128, 512], F32))
        sL = _Sem(E(nc.semaphore("sL")))
        sV = _Sem(E(nc.semaphore("sV")))
        sSc = _Sem(E(nc.semaphore("sSc")))
        sT = _Sem(E(nc.semaphore("sT")))
        sO = _Sem(E(nc.semaphore("sO")))
        cV = _Sem(E(nc.semaphore("cV")))
        cS = _Sem(E(nc.semaphore("cS")))
        blk = E(nc.Block())
        sc = Sched(chains={"vector": cV, "scalar": cS})

        sc.o("sync", lambda e: e.dma_start(out=GQ[:], in_=gq), sL, 16)
        sc.o("sync", lambda e: e.dma_start(out=GK[:], in_=gk), sL, 16)
        sc.o("vector", lambda e: e.memset(ONES1[:], 1.0))
        sc.o("vector", lambda e: e.memset(EPSC[:], EPS), sV, 1)
        sc.w("vector", sL, 32)
        sc.o("vector", lambda e: e.scalar_tensor_tensor(G1[:], GQ[:], ISQ, GK[:], op0=ALU.mult, op1=ALU.mult))
        sc.o("vector", lambda e: e.tensor_copy(G16[:], G1[:]), sV, 1)
        for c in range(8):
            sc.w("tensor", sV, 2 + c)
            sc.o("tensor", lambda e, c=c: e.matmul(PSG[:], ONES1[:], G16[0:1, c * 512:(c + 1) * 512],
                                                   start=True, stop=True), sT, 1)
            sc.w("vector", sT, sT.n)
            sc.o("vector", lambda e, c=c: e.tensor_copy(G32[:, c * 512:(c + 1) * 512], PSG[:]), sV, 1)

        for tc in range(8):
            t0 = tc * 128
            sc.o("sync", lambda e, a=q[t0:t0 + 128, :]: e.dma_start(out=QC[:], in_=a), sL, 16)
            sc.o("sync", lambda e, a=k[t0:t0 + 128, :]: e.dma_start(out=KC[:], in_=a), sL, 16)
            sc.w("vector", sL, sL.n)
            if tc > 0:
                sc.w("vector", sO, 32 * tc)
            for cc in range(8):
                sc.o("vector", lambda e, cc=cc: e.bn_stats(ST6[:, cc * 6:(cc + 1) * 6], QC[:, cc * 512:(cc + 1) * 512]))
            sc.o("vector", lambda e: e.bn_aggr(MV[:], ST6[:]))
            for cc in range(8):
                sc.o("vector", lambda e, cc=cc: e.bn_stats(ST6[:, cc * 6:(cc + 1) * 6], KC[:, cc * 512:(cc + 1) * 512]))
            sc.o("vector", lambda e: e.bn_aggr(MVK[:], ST6[:]), sV, 1)
            sc.w("scalar", sV, sV.n)
            sc.o("scalar", lambda e: e.activation(SD[:], MV[:, 1:2], AF.Sqrt, bias=EPSC[:], scale=1.0))
            sc.o("scalar", lambda e: e.activation(SDK[:], MVK[:, 1:2], AF.Sqrt, bias=EPSC[:], scale=1.0), sSc, 1)
            sc.w("vector", sSc, sSc.n)
            sc.o("vector", lambda e: e.reciprocal(RS[:], SD[:]))
            sc.o("vector", lambda e: e.reciprocal(RSK[:], SDK[:]))
            sc.o("vector", lambda e: e.tensor_scalar(T1[:], QC[:], MV[:, 0:1], RS[:, 0:1], op0=ALU.subtract, op1=ALU.mult))
            sc.o("vector", lambda e: e.tensor_tensor(Q2T[:], T1[:], G32[:], op=ALU.mult), sV, 1)
            sc.o("vector", lambda e: e.tensor_scalar(K2T[:], KC[:], MVK[:, 0:1], RSK[:, 0:1], op0=ALU.subtract, op1=ALU.mult), sV, 1)
            sc.w("sync", sV, sV.n)
            sc.o("sync", lambda e, o=q2[t0:t0 + 128, :]: e.dma_start(out=o, in_=Q2T[:]), sO, 16)
            sc.o("sync", lambda e, o=k2[t0:t0 + 128, :]: e.dma_start(out=o, in_=K2T[:]), sO, 16)
        sc.w("sync", sO, sO.n)
        sc.emit(blk)
    return nc



def build_attn(debug=False):
    nc = bass.Bass("TRN2", target_bir_lowering=False)
    qf = nc.dram_tensor("qf", [8, HD, S], BF16, kind="ExternalInput").ap()
    kf = nc.dram_tensor("kf", [8, HD, S], BF16, kind="ExternalInput").ap()
    vt = nc.dram_tensor("vt", [8, S, HD], BF16, kind="ExternalInput").ap()
    ahi = nc.dram_tensor("ahi", [8, S], BF16, kind="ExternalInput").ap()
    ami = nc.dram_tensor("ami", [8, S], BF16, kind="ExternalInput").ap()
    alo = nc.dram_tensor("alo", [8, S], BF16, kind="ExternalInput").ap()
    acolt = nc.dram_tensor("acolt", [8, 128, 32], F32, kind="ExternalInput").ap()
    ub8 = nc.dram_tensor("ub8", [8, 128], F32, kind="ExternalInput").ap()
    uw8 = nc.dram_tensor("uw8", [8, 128], F32, kind="ExternalInput").ap()
    ident = nc.dram_tensor("ident", [128, 128], F32, kind="ExternalInput").ap()
    identb = nc.dram_tensor("identb", [128, 128], BF16, kind="ExternalInput").ap()
    negd = nc.dram_tensor("negd", [128, 128], F32, kind="ExternalInput").ap()
    trit = nc.dram_tensor("trit", [128, 896], F32, kind="ExternalInput").ap()
    oo = nc.dram_tensor("oo", [8, HD, S], BF16, kind="ExternalOutput").ap()
    if debug:
        dnm = nc.dram_tensor("dnm", [8, 128, 32], F32, kind="ExternalOutput").ap()
        dsg = nc.dram_tensor("dsg", [8, 128, 32], F32, kind="ExternalOutput").ap()
        dqrow = nc.dram_tensor("dqrow", [8, 3, S], BF16, kind="ExternalOutput").ap()
        dksc = nc.dram_tensor("dksc", [8, 128, 32], F32, kind="ExternalOutput").ap()
        dckr = nc.dram_tensor("dckr", [8, 1, S], F32, kind="ExternalOutput").ap()

    with contextlib.ExitStack() as st:
        E = st.enter_context
        QF = E(nc.sbuf_tensor("QF", [128, S], BF16))
        KF = E(nc.sbuf_tensor("KF", [128, S], BF16))
        VSB = E(nc.sbuf_tensor("VSB", [128, 32 * 128], BF16))
        KS5 = E(nc.sbuf_tensor("KS5", [5, S], BF16))
        QROW3 = E(nc.sbuf_tensor("QROW3", [3, S], BF16))
        ONES5 = E(nc.sbuf_tensor("ONES5", [5, 128], BF16))
        IDF = E(nc.sbuf_tensor("IDF", [128, 128], F32))
        IDB = E(nc.sbuf_tensor("IDB", [128, 128], BF16))
        NEGD = E(nc.sbuf_tensor("NEGD", [128, 128], F32))
        TRIT = E(nc.sbuf_tensor("TRIT", [128, 896], F32))
        UB = E(nc.sbuf_tensor("UB", [8, 128], F32))
        UW = E(nc.sbuf_tensor("UW", [8, 128], F32))
        U8 = E(nc.sbuf_tensor("U8", [8, 128], F32))
        UC = E(nc.sbuf_tensor("UC", [128, 8], F32))
        UCB = E(nc.sbuf_tensor("UCB", [128, 8], BF16))
        KSCA = E(nc.sbuf_tensor("KSCA", [128, 32], F32))
        KSC2 = E(nc.sbuf_tensor("KSC2", [128, 32], F32))
        CKR = E(nc.sbuf_tensor("CKR", [1, S], F32))
        CKH = E(nc.sbuf_tensor("CKH", [1, S], BF16))
        CKL = E(nc.sbuf_tensor("CKL", [1, S], BF16))
        ESC = E(nc.sbuf_tensor("ESC", [128, 2048], BF16))
        NM = [E(nc.sbuf_tensor(f"NM{i}", [128, 1], F32)) for i in range(2)]
        SG = [E(nc.sbuf_tensor(f"SG{i}", [128, 1], F32)) for i in range(2)]
        NMc = E(nc.sbuf_tensor("NMc", [128, 1], F32))
        SGc = E(nc.sbuf_tensor("SGc", [128, 1], F32))
        TC1 = E(nc.sbuf_tensor("TC1", [128, 1], F32))
        TC2 = E(nc.sbuf_tensor("TC2", [128, 1], F32))
        EX1 = E(nc.sbuf_tensor("EX1", [128, 1], F32))
        EX2 = E(nc.sbuf_tensor("EX2", [128, 1], F32))
        LNS = E(nc.sbuf_tensor("LNS", [128, 1], F32))
        QA = E(nc.sbuf_tensor("QA", [128, 1], F32))
        QA3B = E(nc.sbuf_tensor("QA3B", [128, 3], BF16))
        QH32 = E(nc.sbuf_tensor("QH32", [128, 1], F32))
        QR1 = E(nc.sbuf_tensor("QR1", [128, 1], F32))
        PTS = [E(nc.sbuf_tensor(f"PTS{i}", [128, 512], BF16)) for i in range(3)]
        NMall = E(nc.sbuf_tensor("NMall", [128, 32], F32))
        SGall = E(nc.sbuf_tensor("SGall", [128, 32], F32))
        OEV = [E(nc.sbuf_tensor(f"OEV{i}", [128, 512], BF16)) for i in range(2)]
        PROW = E(nc.psum_tensor("PROW", [128, 2048], F32))
        PT3 = E(nc.psum_tensor("PT3", [128, 1536], F32))
        PXB = E(nc.psum_tensor("PXB", [128, 1024], BF16))
        sL = _Sem(E(nc.semaphore("sL")))
        sT = _Sem(E(nc.semaphore("sT")))
        sV = _Sem(E(nc.semaphore("sV")))
        sSc = _Sem(E(nc.semaphore("sSc")))
        sE2 = _Sem(E(nc.semaphore("sE2")))
        sPV = _Sem(E(nc.semaphore("sPV")))
        sO = [_Sem(E(nc.semaphore(f"sO{i}"))) for i in range(2)]
        sR = _Sem(E(nc.semaphore("sR")))
        cV = _Sem(E(nc.semaphore("cV")))
        cS = _Sem(E(nc.semaphore("cS")))
        blk = E(nc.Block())
        sc = Sched(chains={"vector": cV, "scalar": cS})

        sc.o("sync", lambda e: e.dma_start(out=IDF[:], in_=ident), sL, 16)
        sc.o("sync", lambda e: e.dma_start(out=IDB[:], in_=identb), sL, 16)
        sc.o("sync", lambda e: e.dma_start(out=NEGD[:], in_=negd), sL, 16)
        sc.o("sync", lambda e: e.dma_start(out=TRIT[:], in_=trit), sL, 16)
        sc.o("sync", lambda e: e.dma_start(out=UB[:], in_=ub8), sL, 16)
        sc.o("sync", lambda e: e.dma_start(out=UW[:], in_=uw8), sL, 16)
        sc.o("vector", lambda e: e.memset(ONES5[:], 1.0))
        sc.w("vector", sL, 6 * 16)
        sc.o("vector", lambda e: e.scalar_tensor_tensor(U8[:], UB[:], ISQ, UW[:], op0=ALU.mult, op1=ALU.mult), sV, 1)
        sc.w("tensor", sV, sV.n)
        sc.o("tensor", lambda e: e.transpose(PT3[0:128, 0:8], U8[:], IDF[0:8, 0:8]), sT, 1)
        sc.w("vector", sT, sT.n)
        sc.o("vector", lambda e: e.tensor_copy(UC[:], PT3[0:128, 0:8]))
        sc.o("vector", lambda e: e.tensor_copy(UCB[:], UC[:]), sV, 1)

        for p in range(8):
            sc.o("sync", lambda e, p=p: e.dma_start(out=QF[:], in_=qf[p]), sL, 16)
            sc.o("sync", lambda e, p=p: e.dma_start(out=KF[:], in_=kf[p]), sL, 16)
            sc.o("sync", lambda e, p=p: e.dma_start(
                out=VSB[:].rearrange("p (kc d) -> p kc d", kc=32),
                in_=vt[p].rearrange("(kc p) d -> p kc d", p=128)), sL, 16)
            sc.o("sync", lambda e, p=p: e.dma_start(out=KSCA[:], in_=acolt[p]), sL, 16)
            sc.o("sync", lambda e, p=p: e.dma_start(out=KS5[0:1, :], in_=ahi[p:p + 1, :]), sR, 16)
            sc.o("sync", lambda e, p=p: e.dma_start(out=KS5[1:2, :], in_=ami[p:p + 1, :]), sR, 16)
            sc.o("sync", lambda e, p=p: e.dma_start(out=KS5[2:3, :], in_=alo[p:p + 1, :]), sR, 16)
            lded = sL.n
            rows0 = sR.n

            sc.w("tensor", sL, lded)
            sc.w("tensor", sV, sV.n)
            for c in range(8):
                dst = PT3[0:1, 1024:1536] if c % 2 == 0 else PROW[0:1, 0:512]
                sc.o("tensor", lambda e, c=c, d=dst, p=p: e.matmul(
                    d, UCB[:, p:p + 1], KF[:, c * 512:(c + 1) * 512], start=True, stop=True), sT, 1)
                sc.w("vector", sT, sT.n)
                sc.o("vector", lambda e, c=c, d=dst: e.tensor_copy(CKR[0:1, c * 512:(c + 1) * 512], d), sV, 1)
                sc.w("tensor", sV, sV.n)
            sc.o("vector", lambda e: e.tensor_copy(CKH[:], CKR[:]))
            sc.o("vector", lambda e: e.scalar_tensor_tensor(CKL[:], CKR[:], 1.0, CKH[:], op0=ALU.mult, op1=ALU.subtract), sV, 1)
            sc.w("sync", sV, sV.n)
            sc.o("sync", lambda e: e.dma_start(out=KS5[3:4, :], in_=CKH[:]), sR, 16)
            sc.o("sync", lambda e: e.dma_start(out=KS5[4:5, :], in_=CKL[:]), sR, 16)
            rows_done = sR.n
            for kc in range(32):
                dst = PT3[0:128, 1024 + (kc % 2):1024 + (kc % 2) + 1]
                sc.o("tensor", lambda e, kc=kc, d=dst, p=p: e.matmul(
                    d, KF[:, kc * 128:(kc + 1) * 128], UCB[:, p:p + 1], start=True, stop=True), sT, 1)
                sc.w("vector", sT, sT.n)
                sc.o("vector", lambda e, kc=kc, d=dst: e.tensor_tensor(
                    KSC2[:, kc:kc + 1], KSCA[:, kc:kc + 1], d, op=ALU.add), sV, 1)
                sc.w("tensor", sV, sV.n)
            ck_cols_done = sV.n

            sc.w("tensor", sR, rows_done)
            for qt in range(32):
                kext = (qt + 1) * 128
                blocks = [(0, min(kext, 2048))]
                if kext > 2048:
                    blocks.append((2048, kext))
                for bi, (b0, b1) in enumerate(blocks):
                    wb = b1 - b0
                    off = b0
                    tmm = None
                    while off < b1:
                        wch = min(512, b1 - off)
                        sc.o("tensor", lambda e, qt=qt, off=off, wch=wch, b0=b0: e.matmul(
                            PROW[:, off - b0:off - b0 + wch], QF[:, qt * 128:qt * 128 + 128],
                            KF[:, off:off + wch], start=True, stop=False))
                        lastch = off + wch >= b1
                        tmm = sc.o("tensor", lambda e, off=off, wch=wch, b0=b0: e.matmul(
                            PROW[:, off - b0:off - b0 + wch], ONES5[:], KS5[0:5, off:off + wch],
                            start=False, stop=True), sT if lastch else None, 1)
                        off += wch
                    if b1 == kext:
                        dloc = qt * 128 - b0
                        sc.w("vector", sT, tmm)
                        vmask = sc.o("vector", lambda e, dloc=dloc: e.tensor_tensor(
                            PROW[:, dloc:dloc + 128], PROW[:, dloc:dloc + 128], NEGD[:], op=ALU.add), sV, 1)
                        sc.w("vector", sV, vmask)
                    else:
                        sc.w("vector", sT, tmm)
                    vred = sc.o("vector", lambda e, wb=wb, bi=bi: e.reduce_max(
                        NM[bi][:], PROW[:, 0:wb], axis=AX.X, negate=True), sV, 1)
                    sc.w("scalar", sV, vred)
                    sexp = sc.o("scalar", lambda e, wb=wb, bi=bi: e.activation(
                        ESC[:, 0:wb], PROW[:, 0:wb], AF.Exp, bias=NM[bi][:], scale=1.0,
                        accum_out=SG[bi][:]), sSc, 1)
                    sc.w("tensor", sSc, sexp)
                if len(blocks) == 1:
                    nmf, sgf = NM[0], SG[0]
                    vs = None
                else:
                    sc.o("vector", lambda e: e.tensor_tensor(NMc[:], NM[0][:], NM[1][:], op=ALU.min))
                    sc.o("vector", lambda e: e.tensor_tensor(TC1[:], NMc[:], NM[0][:], op=ALU.subtract))
                    vq = sc.o("vector", lambda e: e.tensor_tensor(TC2[:], NMc[:], NM[1][:], op=ALU.subtract), sV, 1)
                    sc.w("scalar", sV, vq)
                    sq = sc.o("scalar", lambda e: (e.activation(EX1[:], TC1[:], AF.Exp),
                                                   e.activation(EX2[:], TC2[:], AF.Exp))[-1], sSc, 1)
                    sc.w("vector", sSc, sq)
                    sc.o("vector", lambda e: e.tensor_tensor(TC1[:], SG[0][:], EX1[:], op=ALU.mult))
                    sc.o("vector", lambda e: e.tensor_tensor(TC2[:], SG[1][:], EX2[:], op=ALU.mult))
                    vs = sc.o("vector", lambda e: e.tensor_tensor(SGc[:], TC1[:], TC2[:], op=ALU.add), sV, 1)
                    nmf, sgf = NMc, SGc
                if vs is not None:
                    sc.w("scalar", sV, vs)
                if debug:
                    sc.o("vector", lambda e, qt=qt, nmf=nmf: e.tensor_copy(NMall[:, qt:qt + 1], nmf[:]))
                    sc.o("vector", lambda e, qt=qt, sgf=sgf: e.tensor_copy(SGall[:, qt:qt + 1], sgf[:]))
                sln = sc.o("scalar", lambda e, sgf=sgf: e.activation(LNS[:], sgf[:], AF.Ln), sSc, 1)
                sc.w("vector", sSc, sln)
                sc.o("vector", lambda e, nmf=nmf: e.tensor_tensor(QA[:], nmf[:], LNS[:], op=ALU.subtract))
                sc.o("vector", lambda e: e.tensor_copy(QA3B[:, 0:1], QA[:]))
                sc.o("vector", lambda e: e.tensor_copy(QH32[:], QA3B[:, 0:1]))
                sc.o("vector", lambda e: e.tensor_tensor(QR1[:], QA[:], QH32[:], op=ALU.subtract))
                sc.o("vector", lambda e: e.tensor_copy(QA3B[:, 1:2], QR1[:]))
                sc.o("vector", lambda e: e.tensor_copy(QH32[:], QA3B[:, 1:2]))
                sc.o("vector", lambda e: e.tensor_tensor(QR1[:], QR1[:], QH32[:], op=ALU.subtract))
                vqa = sc.o("vector", lambda e: e.tensor_copy(QA3B[:, 2:3], QR1[:]), sV, 1)
                sc.w("tensor", sV, vqa)
                ttr = sc.o("tensor", lambda e: e.transpose(PXB[0:3, 0:128], QA3B[:], IDB[:]), sT, 1)
                sc.w("vector", sT, ttr)
                vevq = sc.o("vector", lambda e, qt=qt: e.tensor_copy(
                    QROW3[0:3, qt * 128:qt * 128 + 128], PXB[0:3, 0:128]), sV, 1)
                sc.w("tensor", sV, vevq)

            if debug:
                dv = sc.o("vector", lambda e: e.engine_nop(), sV, 1)
                sc.w("sync", sV, dv)
                sc.o("sync", lambda e, p=p: e.dma_start(out=dnm[p], in_=NMall[:]), sR, 16)
                sc.o("sync", lambda e, p=p: e.dma_start(out=dsg[p], in_=SGall[:]), sR, 16)
                sc.o("sync", lambda e, p=p: e.dma_start(out=dqrow[p], in_=QROW3[:]), sR, 16)
                sc.o("sync", lambda e, p=p: e.dma_start(out=dksc[p], in_=KSC2[:]), sR, 16)
                sc.o("sync", lambda e, p=p: e.dma_start(out=dckr[p], in_=CKR[:]), sR, 16)
                sc.w("sync", sR, sR.n)

            for qg in range(8):
                kcmax = qg * 4 + 4
                for kc in range(kcmax):
                    tb = kc % 3
                    sc.o("tensor", lambda e, kc=kc, qg=qg, tb=tb: e.matmul(
                        PT3[:, tb * 512:(tb + 1) * 512], KF[:, kc * 128:kc * 128 + 128],
                        QF[:, qg * 512:(qg + 1) * 512], start=True, stop=False))
                    tmm = sc.o("tensor", lambda e, qg=qg, tb=tb: e.matmul(
                        PT3[:, tb * 512:(tb + 1) * 512], ONES5[0:3, :],
                        QROW3[0:3, qg * 512:(qg + 1) * 512], start=False, stop=True), sT, 1)
                    c2 = kc * 128 - qg * 512
                    if c2 >= 0:
                        sc.w("vector", sT, tmm)
                        vm = sc.o("vector", lambda e, tb=tb, c2=c2: e.tensor_tensor(
                            PT3[:, tb * 512:(tb + 1) * 512], PT3[:, tb * 512:(tb + 1) * 512],
                            TRIT[:, 384 - c2:384 - c2 + 512], op=ALU.add), sV, 1)
                        sc.w("scalar", sV, vm)
                    else:
                        sc.w("scalar", sT, tmm)
                    sex = sc.o("scalar", lambda e, tb=tb, kc=kc: e.activation(
                        PTS[tb][:], PT3[:, tb * 512:(tb + 1) * 512], AF.Exp,
                        bias=KSC2[:, kc:kc + 1], scale=1.0), sE2, 1)
                    sc.w("tensor", sE2, sex)
                    sc.o("tensor", lambda e, kc=kc, tb=tb, st=(kc == 0), sp=(kc == kcmax - 1): e.matmul(
                        PROW[:, 0:512], VSB[:, kc * 128:kc * 128 + 128], PTS[tb][:],
                        start=st, stop=sp), sPV, 1)
                sc.w("vector", sPV, sPV.n)
                ob = qg % 2
                if sO[ob].n > 0:
                    sc.w("vector", sO[ob], sO[ob].n)
                vo_ = sc.o("vector", lambda e, ob=ob: e.tensor_copy(OEV[ob][:], PROW[:, 0:512]), sV, 1)
                sc.w("sync", sV, vo_)
                sc.o("sync", lambda e, p=p, qg=qg, ob=ob: e.dma_start(
                    out=oo[p][:, qg * 512:(qg + 1) * 512], in_=OEV[ob][:]), sO[ob], 16)
                sc.w("tensor", sV, vo_)
            sc.w("sync", sPV, sPV.n)
        sc.w("sync", sO[0], sO[0].n)
        sc.w("sync", sO[1], sO[1].n)
        sc.emit(blk)
    return nc



def build_proj():
    nc = bass.Bass("TRN2", target_bir_lowering=False)
    aT = nc.dram_tensor("aT", [D, TOK], BF16, kind="ExternalInput").ap()
    wT = nc.dram_tensor("wT", [D, D], BF16, kind="ExternalInput").ap()
    br = nc.dram_tensor("br", [1, D], F32, kind="ExternalInput").ap()
    resid = nc.dram_tensor("resid", [TOK, D], F32, kind="ExternalInput").ap()
    xo = nc.dram_tensor("xo", [TOK, D], F32, kind="ExternalOutput").ap()
    xob = nc.dram_tensor("xob", [TOK, D], BF16, kind="ExternalOutput").ap()
    Y = nc.dram_tensor("Y", [TOK, D], F32, kind="Internal").ap()

    with contextlib.ExitStack() as st:
        E = st.enter_context
        A16 = E(nc.sbuf_tensor("A16", [128, 32 * 1024], BF16))
        W16 = [E(nc.sbuf_tensor(f"W16{i}", [128, 32 * 512], BF16)) for i in range(2)]
        ONES1 = E(nc.sbuf_tensor("ONES1", [1, 128], BF16))
        B16 = E(nc.sbuf_tensor("B16", [1, D], BF16))
        EVF = [E(nc.sbuf_tensor(f"EVF{i}", [128, 512], F32)) for i in range(4)]
        YC = E(nc.sbuf_tensor("YC", [128, D], F32))
        RC = E(nc.sbuf_tensor("RC", [128, D], F32))
        XOB = E(nc.sbuf_tensor("XOB", [128, D], BF16))
        Y2 = YC
        XO = RC
        ST6 = E(nc.sbuf_tensor("ST6", [128, 48], F32))
        MV = E(nc.sbuf_tensor("MV", [128, 2], F32))
        SD = E(nc.sbuf_tensor("SD", [128, 1], F32))
        RS = E(nc.sbuf_tensor("RS", [128, 1], F32))
        EPSC = E(nc.sbuf_tensor("EPSC", [128, 1], F32))
        PS = [E(nc.psum_tensor(f"PS{i}", [128, 512], F32)) for i in range(4)]
        sX = _Sem(E(nc.semaphore("sX")))
        sW = [_Sem(E(nc.semaphore(f"sW{i}"))) for i in range(2)]
        sMM = _Sem(E(nc.semaphore("sMM")))
        sEV = _Sem(E(nc.semaphore("sEV")))
        sST = [_Sem(E(nc.semaphore(f"sST{i}"))) for i in range(4)]
        sL = _Sem(E(nc.semaphore("sL")))
        sV = _Sem(E(nc.semaphore("sV")))
        sSc = _Sem(E(nc.semaphore("sSc")))
        sO = _Sem(E(nc.semaphore("sO")))
        cV = _Sem(E(nc.semaphore("cV")))
        cS = _Sem(E(nc.semaphore("cS")))
        blk = E(nc.Block())
        sc = Sched(chains={"vector": cV, "scalar": cS})

        sc.o("vector", lambda e: e.memset(ONES1[:], 1.0))
        sc.o("vector", lambda e: e.memset(EPSC[:], EPS))
        sc.o("gpsimd", lambda e: e.dma_start(out=B16[:], in_=br), sX, 16)
        sc.o("gpsimd", lambda e: e.dma_start(
            out=A16[:].rearrange("p (i t) -> p i t", i=32),
            in_=aT.rearrange("(i p) t -> p i t", p=128)), sX, 16)

        def wload(w):
            src = wT[:, w * 512:(w + 1) * 512].rearrange("(i p) e -> p i e", p=128)
            dst = W16[w % 2][:].rearrange("p (i e) -> p i e", i=32)
            if w >= 2:
                sc.w("sync", sMM, (w - 1) * 8)
            sc.o("sync", lambda e, s=src, d=dst: e.dma_start(out=d, in_=s), sW[w % 2], 16)

        wload(0)
        wload(1)
        n = 0
        for et in range(8):
            sc.w("tensor", sX, 32)
            sc.w("tensor", sW[et % 2], 16 * (et // 2 + 1))
            for tc in range(8):
                n += 1
                bank = (n - 1) % 4
                if n > 4:
                    sc.w("tensor", sEV, n - 4)
                for d in range(32):
                    sc.o("tensor", lambda e, d=d, b=bank, tc=tc, wb=et % 2: e.matmul(
                        PS[b][:], A16[:, d * 1024 + tc * 128: d * 1024 + tc * 128 + 128],
                        W16[wb][:, d * 512:(d + 1) * 512], start=(d == 0), stop=False))
                sc.o("tensor", lambda e, b=bank, c=et * 512: e.matmul(
                    PS[b][:], ONES1[:], B16[0:1, c:c + 512], start=False, stop=True), sMM, 1)
                sc.w("vector", sMM, n)
                if n > 4:
                    sc.w("vector", sST[bank], 16 * ((n - 4 - 1) // 4 + 1))
                sc.o("vector", lambda e, b=bank: e.tensor_copy(EVF[b][:], PS[b][:]), sEV, 1)
                sc.w("sync", sEV, sEV.n)
                sc.o("sync", lambda e, t0=tc * 128, c=et * 512, b=bank: e.dma_start(
                    out=Y[t0:t0 + 128, c:c + 512], in_=EVF[b][:]), sST[bank], 16)
            if et + 2 < 8:
                wload(et + 2)
        for tc in range(8):
            t0 = tc * 128
            if tc == 0:
                for b in range(4):
                    sc.w("sync", sST[b], sST[b].n)
            else:
                sc.w("sync", sO, 32 * tc)
            sc.o("sync", lambda e, a=Y[t0:t0 + 128, :]: e.dma_start(out=YC[:], in_=a), sL, 16)
            sc.o("sync", lambda e, a=resid[t0:t0 + 128, :]: e.dma_start(out=RC[:], in_=a), sL, 16)
            sc.w("vector", sL, sL.n)
            if tc > 0:
                sc.w("vector", sO, 32 * tc)
            sc.o("vector", lambda e: e.tensor_tensor(Y2[:], YC[:], RC[:], op=ALU.add))
            for cc in range(8):
                sc.o("vector", lambda e, cc=cc: e.bn_stats(ST6[:, cc * 6:(cc + 1) * 6], Y2[:, cc * 512:(cc + 1) * 512]))
            sc.o("vector", lambda e: e.bn_aggr(MV[:], ST6[:]), sV, 1)
            sc.w("scalar", sV, sV.n)
            sc.o("scalar", lambda e: e.activation(SD[:], MV[:, 1:2], AF.Sqrt, bias=EPSC[:], scale=1.0), sSc, 1)
            sc.w("vector", sSc, sSc.n)
            sc.o("vector", lambda e: e.reciprocal(RS[:], SD[:]))
            sc.o("vector", lambda e: e.tensor_scalar(XO[:], Y2[:], MV[:, 0:1], RS[:, 0:1], op0=ALU.subtract, op1=ALU.mult))
            sc.o("vector", lambda e: e.tensor_copy(XOB[:], XO[:]), sV, 1)
            sc.w("sync", sV, sV.n)
            sc.o("sync", lambda e, o=xo[t0:t0 + 128, :]: e.dma_start(out=o, in_=XO[:]), sO, 16)
            sc.o("sync", lambda e, o=xob[t0:t0 + 128, :]: e.dma_start(out=o, in_=XOB[:]), sO, 16)
        sc.w("sync", sO, sO.n)
        sc.emit(blk)
    return nc



def _split3(a):
    a = a.astype(np.float64)
    hi = a.astype(BF)
    r = a - hi.astype(np.float64)
    mid = r.astype(BF)
    lo = (r - mid.astype(np.float64)).astype(BF)
    return hi, mid, lo


def _run(nc, in_maps):
    res = run_bass_kernel_spmd(nc, in_maps, core_ids=list(range(NCORE)))
    return res.results


_PROGS = {}

def _prog(name):
    if name not in _PROGS:
        _PROGS[name] = {"prep": build_prep, "qkv0": lambda: build_qkv(0),
                        "qkv1": lambda: build_qkv(1), "qkln": build_qkln,
                        "attn": build_attn, "attn_dbg": lambda: build_attn(True),
                        "proj": build_proj}[name]()
    return _PROGS[name]


_CONSTS = None

def _consts():
    global _CONSTS
    if _CONSTS is None:
        slopes = np.array([2.0 ** (-(h + 1) / 4.0) for h in range(H)], dtype=np.float64)
        kidx = np.arange(S, dtype=np.float64)
        ident = np.eye(128, dtype=np.float32)
        identb = np.eye(128, dtype=np.float32).astype(BF)
        ii, jj = np.meshgrid(np.arange(128), np.arange(128), indexing="ij")
        negd = np.where(jj > ii, np.float32(NEG), np.float32(0.0))
        iu, uu = np.meshgrid(np.arange(128), np.arange(896), indexing="ij")
        trit = np.where(iu <= uu - 384, np.float32(0.0), np.float32(NEG))
        AHI = np.empty((H, S), BF); AMI = np.empty((H, S), BF); ALO = np.empty((H, S), BF)
        ACOLT = np.empty((H, 128, 32), np.float32)
        for h in range(H):
            a = slopes[h] * kidx
            AHI[h], AMI[h], ALO[h] = _split3(a)
            ACOLT[h] = (slopes[h] * (np.arange(128)[:, None] + 128.0 * np.arange(32)[None, :])).astype(np.float32)
        _CONSTS = dict(ident=ident, identb=identb, negd=negd, trit=trit,
                       AHI=AHI, AMI=AMI, ALO=ALO, ACOLT=ACOLT)
    return _CONSTS


def run_prep(wqi, zz, ss, s2, mki, q3i, wni, castcat):
    maps = []
    for c in range(NCORE):
        r0, r1 = c * 512, (c + 1) * 512
        maps.append(dict(
            wq=wqi[r0:r1], zz=zz[r0:r1], ss=ss[r0:r1], s2=s2[r0:r1], mk=mki[r0:r1],
            q3=q3i[r0 * 128:r1 * 128], wn=wni[r0 * 128:r1 * 128],
            cin=np.ascontiguousarray(castcat[:, c * 4096:(c + 1) * 4096])))
    rp = _run(_prog("prep"), maps)
    W0 = np.concatenate([np.asarray(rp[c]["w0d"]) for c in range(NCORE)], axis=0)
    W1 = np.concatenate([np.asarray(rp[c]["w1d"]) for c in range(NCORE)], axis=0)
    castb = np.concatenate([np.asarray(rp[c]["cout"]) for c in range(NCORE)], axis=1)
    return W0, W1, castb


def run_qkv(layer, xbatches, Wfull, bias):
    maps = []
    for c in range(NCORE):
        b = c // 4
        hs = (c % 4) * 8
        wsl = np.concatenate([Wfull[:, hs * 128:hs * 128 + EH],
                              Wfull[:, D + hs * 128:D + hs * 128 + EH],
                              Wfull[:, 2 * D + hs * 128:2 * D + hs * 128 + EH]], axis=1)
        m = dict(xT=xbatches[b], wT=np.ascontiguousarray(wsl))
        if layer == 1:
            bsl = np.concatenate([bias[hs * 128:hs * 128 + EH],
                                  bias[D + hs * 128:D + hs * 128 + EH],
                                  bias[2 * D + hs * 128:2 * D + hs * 128 + EH]])[None, :]
            m["b3"] = np.ascontiguousarray(bsl.astype(np.float32))
        maps.append(m)
    rr = _run(_prog("qkv0" if layer == 0 else "qkv1"), maps)
    dt = np.float32 if layer == 0 else BF
    Qf = np.empty((2 * S, D), dt); Kf = np.empty((2 * S, D), dt); Vf = np.empty((2 * S, D), BF)
    for c in range(NCORE):
        b = c // 4
        hs = (c % 4) * 8
        Qf[b * S:(b + 1) * S, hs * 128:hs * 128 + EH] = np.asarray(rr[c]["qo"])
        Kf[b * S:(b + 1) * S, hs * 128:hs * 128 + EH] = np.asarray(rr[c]["ko"])
        Vf[b * S:(b + 1) * S, hs * 128:hs * 128 + EH] = np.asarray(rr[c]["vo"])
    return Qf, Kf, Vf


def run_qkln(Qr, Kr, gqv, gkv):
    gq = gqv[None, :].astype(np.float32)
    gk = gkv[None, :].astype(np.float32)
    maps = [dict(q=np.ascontiguousarray(Qr[c * TOK:(c + 1) * TOK]),
                 k=np.ascontiguousarray(Kr[c * TOK:(c + 1) * TOK]),
                 gq=gq, gk=gk) for c in range(NCORE)]
    rr = _run(_prog("qkln"), maps)
    Q2 = np.concatenate([np.asarray(rr[c]["q2"]) for c in range(NCORE)], axis=0)
    K2 = np.concatenate([np.asarray(rr[c]["k2"]) for c in range(NCORE)], axis=0)
    return Q2, K2


def run_attn(Q2, K2, Vf, ub, uw):
    C = _consts()
    maps = []
    for c in range(NCORE):
        b = c // 4
        hs = (c % 4) * 8
        qfs = np.empty((8, HD, S), BF); kfs = np.empty((8, HD, S), BF); vts = np.empty((8, S, HD), BF)
        for p in range(8):
            h = hs + p
            qfs[p] = Q2[b * S:(b + 1) * S, h * 128:(h + 1) * 128].T
            kfs[p] = K2[b * S:(b + 1) * S, h * 128:(h + 1) * 128].T
            vts[p] = Vf[b * S:(b + 1) * S, h * 128:(h + 1) * 128]
        maps.append(dict(
            qf=qfs, kf=kfs, vt=vts,
            ahi=C["AHI"][hs:hs + 8], ami=C["AMI"][hs:hs + 8], alo=C["ALO"][hs:hs + 8],
            acolt=C["ACOLT"][hs:hs + 8],
            ub8=np.ascontiguousarray(ub[hs * 128:(hs + 8) * 128].reshape(8, 128)),
            uw8=np.ascontiguousarray(uw[hs * 128:(hs + 8) * 128].reshape(8, 128)),
            ident=C["ident"], identb=C["identb"], negd=C["negd"], trit=C["trit"]))
    rr = _run(_prog("attn"), maps)
    Ofm = [np.empty((D, S), BF) for _ in range(2)]
    for c in range(NCORE):
        b = c // 4
        hs = (c % 4) * 8
        oc = np.asarray(rr[c]["oo"])
        for p in range(8):
            Ofm[b][(hs + p) * 128:(hs + p + 1) * 128, :] = oc[p]
    return Ofm


def run_proj(Ofm_or_aTs, wTb, bias, residfull, is_ofm=True):
    maps = []
    for c in range(NCORE):
        if is_ofm:
            b = c // 4
            aTc = np.ascontiguousarray(Ofm_or_aTs[b][:, (c % 4) * TOK:(c % 4 + 1) * TOK])
        else:
            aTc = Ofm_or_aTs[c]
        maps.append(dict(aT=aTc, wT=wTb, br=bias[None, :].astype(np.float32),
                         resid=np.ascontiguousarray(residfull[c * TOK:(c + 1) * TOK])))
    rr = _run(_prog("proj"), maps)
    xn = np.concatenate([np.asarray(rr[c]["xo"]) for c in range(NCORE)], axis=0)
    xnb = np.concatenate([np.asarray(rr[c]["xob"]) for c in range(NCORE)], axis=0)
    return xn, xnb


def kernel_unfused(L0_attn_hidden_states, L0_attn_Wqkv, L0_attn_q_ln_w, L0_attn_q_ln_b,
           L0_attn_k_ln_w, L0_attn_k_ln_b, L0_attn_Wout, L0_ffn_W_q, L0_ffn_scales,
           L0_ffn_zeros, L0_ffn_mask, L0_ffn_scale2, L0_ffn_bias, L1_attn_w_attn,
           L1_attn_b_attn, L1_attn_w_proj, L1_attn_b_proj, L1_ffn_weight_q3,
           L1_ffn_weight_norm, L1_ffn_bias):
    x = np.asarray(L0_attn_hidden_states, dtype=np.float32).reshape(2 * S, D)

    WqT = np.ascontiguousarray(np.asarray(L0_attn_Wqkv, np.float32).T)
    WoT = np.ascontiguousarray(np.asarray(L0_attn_Wout, np.float32).T)
    WA = np.asarray(L1_attn_w_attn, np.float32)
    WP = np.asarray(L1_attn_w_proj, np.float32)
    castcat = np.concatenate([WqT, WA, WoT, WP], axis=1)
    W0, W1, castb = run_prep(
        np.asarray(L0_ffn_W_q, np.int32),
        np.asarray(L0_ffn_zeros, np.float32).reshape(D, 32),
        np.asarray(L0_ffn_scales, np.float32).reshape(D, 32),
        np.asarray(L0_ffn_scale2, np.float32).reshape(D, 1),
        np.asarray(L0_ffn_mask, np.int32),
        np.asarray(L1_ffn_weight_q3, np.int8),
        np.asarray(L1_ffn_weight_norm, np.float32),
        castcat)
    WqTb = castb[:, :12288]
    WAb = castb[:, 12288:24576]
    WoTb = np.ascontiguousarray(castb[:, 24576:28672])
    WPb = np.ascontiguousarray(castb[:, 28672:])
    W0T = np.ascontiguousarray(W0.T)
    W1T = np.ascontiguousarray(W1.T)

    zeros_bias = np.zeros(D, np.float32)

    xb = [np.ascontiguousarray(x[b * S:(b + 1) * S].T) for b in range(2)]
    Qr, Kr, Vf = run_qkv(0, xb, WqTb, None)
    Q2, K2 = run_qkln(Qr, Kr, np.asarray(L0_attn_q_ln_w, np.float32),
                      np.asarray(L0_attn_k_ln_w, np.float32))
    Ofm = run_attn(Q2, K2, Vf,
                   np.asarray(L0_attn_q_ln_b, np.float32),
                   np.asarray(L0_attn_k_ln_w, np.float32))
    x1, x1b = run_proj(Ofm, WoTb, zeros_bias, x)
    aTs = [np.ascontiguousarray(x1b[c * TOK:(c + 1) * TOK].T) for c in range(NCORE)]
    x2, x2b = run_proj(aTs, W0T, np.asarray(L0_ffn_bias, np.float32), x1, is_ofm=False)

    xb2 = [np.ascontiguousarray(x2b[b * S:(b + 1) * S].T) for b in range(2)]
    Q1, K1, V1 = run_qkv(1, xb2, WAb, np.asarray(L1_attn_b_attn, np.float32))
    zz8 = np.zeros(D, np.float32)
    Ofm1 = run_attn(Q1, K1, V1, zz8, zz8)
    x3, x3b = run_proj(Ofm1, WPb, np.asarray(L1_attn_b_proj, np.float32), x2)
    aTs = [np.ascontiguousarray(x3b[c * TOK:(c + 1) * TOK].T) for c in range(NCORE)]
    x4, _ = run_proj(aTs, W1T, np.asarray(L1_ffn_bias, np.float32), x3, is_ofm=False)

    return x4.reshape(2, S, D).astype(np.float32)




GROUPS_B = [[0, 1, 2, 3], [4, 5, 6, 7]]
GROUPS_ALL = [[0, 1, 2, 3, 4, 5, 6, 7]]


class _Ctx:
    pass


def _mk_sems(nc, E, names):
    return {n: _Sem(E(nc.semaphore(n))) for n in names}


SEM_NAMES = ["sL", "sR", "sT", "sV", "sSc", "sE2", "sPV", "sO0", "sO1", "sMM",
             "sEV", "sW0", "sW1", "sS0", "sS1", "sS2", "sS3", "sX", "sC",
             "gW0", "gW1", "sAGW", "sAGX", "sAR", "sAGO", "cV", "cS",
             "sL2", "sR2", "sVa", "sVb"]


def _barrier(cx, exclude=()):
    snap = [(m.h, m.n) for nm, m in cx.sm.items() if m.n > 0 and nm not in exclude]
    for eng in ENGS:
        for h, v in snap:
            cx.sc.q[eng].append(("w", h, v))


def emit_cast_list(cx, items):
    nc, sc, E, sm = cx.nc, cx.sc, cx.E, cx.sm
    CW = [E(nc.sbuf_tensor(f"{cx.tag}CW{i}", [128, 32 * 512], BF16)) for i in range(2)]
    b_st = [sm["sS0"].n, sm["sS1"].n]
    for i, (src, dst) in enumerate(items):
        pr = i % 2
        sw = sm["gW0"] if pr == 0 else sm["gW1"]
        ss = sm["sS0"] if pr == 0 else sm["sS1"]
        if i >= 2:
            sc.w("gpsimd", ss, b_st[pr] + 16 * (i // 2))
        sc.o("gpsimd", lambda e, s=src, d=CW[pr]: e.dma_start(
            out=d[:].rearrange("p (i t) -> p i t", i=32),
            in_=s.rearrange("(i p) t -> p i t", p=128)), sw, 16)
        sc.w("sync", sw, sw.n)
        sc.o("sync", lambda e, d=dst, b=CW[pr]: e.dma_start(
            out=d.rearrange("(i p) t -> p i t", p=128),
            in_=b[:].rearrange("p (i t) -> p i t", i=32)), ss, 16)


def emit_dequant_fused(cx, io, do_tr=True, do_dq=True, do_st=True, do_pe=True):
    nc, sc, E, sm = cx.nc, cx.sc, cx.E, cx.sm
    t = cx.tag
    QF = E(nc.sbuf_tensor(t + "QF", [128, D], F32))
    MF = E(nc.sbuf_tensor(t + "MF", [128, D], F32))
    T0 = E(nc.sbuf_tensor(t + "T0", [128, D], F32))
    WD0 = E(nc.sbuf_tensor(t + "WD0", [128, D], BF16))
    ZS = E(nc.sbuf_tensor(t + "ZS", [128, 32], F32))
    SC_ = E(nc.sbuf_tensor(t + "SC", [128, 32], F32))
    S2b = E(nc.sbuf_tensor(t + "S2b", [128, 1], F32))
    SCC = E(nc.sbuf_tensor(t + "SCC", [128, 32], F32))
    BY = E(nc.sbuf_tensor(t + "BY", [128, 1536], I32))
    Q1 = E(nc.sbuf_tensor(t + "Q1", [128, D], I32))
    QF2 = E(nc.sbuf_tensor(t + "QF2", [128, D], F32))
    NR = E(nc.sbuf_tensor(t + "NR", [128, 128], F32))
    NE = E(nc.sbuf_tensor(t + "NE", [128, D], F32))
    TA = E(nc.sbuf_tensor(t + "TA", [128, 512], I32))
    TB = E(nc.sbuf_tensor(t + "TB", [128, 512], I32))
    T1 = E(nc.sbuf_tensor(t + "T1", [128, D], F32))
    WD1 = E(nc.sbuf_tensor(t + "WD1", [128, D], BF16))
    sL, sV, sT, sEV, sC = sm["sL"], sm["sV"], sm["sT"], sm["sEV"], sm["sC"]
    wq, zz, ss, s2, mk, q3, wn = (io["wq"], io["zz"], io["ss"], io["s2"],
                                  io["mk"], io["q3"], io["wn"])
    agw_in = io["agw_in"]

    vbase = sV.n
    import os as _os
    for r in range(int(_os.environ.get('DEQR', '4'))):
        r0, r1 = r * 128, (r + 1) * 128
        g0 = r * 16384
        if r > 0:
            sc.w("gpsimd", sV, vbase + 2 * r)
            sc.w("sync", sV, vbase + 2 * r)
        sX_ = sm["sX"]
        sc.o("gpsimd", lambda e, a=wq[r0:r1, :], o=QF: e.dma_start(out=o[:], in_=a), sX_, 16)
        sc.o("gpsimd", lambda e, a=mk[r0:r1, :], o=MF: e.dma_start(out=o[:], in_=a), sX_, 16)
        sc.o("gpsimd", lambda e, a=q3.bitcast(U8)[g0:g0 + 16384, :].rearrange("(p g) b -> p (g b)", p=128), o=BY: e.dma_start(out=o[:], in_=a), sX_, 16)
        sc.o("sync", lambda e, a=zz[r0:r1, :], o=ZS: e.dma_start(out=o[:], in_=a), sL, 16)
        sc.o("sync", lambda e, a=ss[r0:r1, :], o=SC_: e.dma_start(out=o[:], in_=a), sL, 16)
        sc.o("sync", lambda e, a=s2[r0:r1, :], o=S2b: e.dma_start(out=o[:], in_=a), sL, 16)
        sc.o("sync", lambda e, a=wn[g0:g0 + 16384, :].rearrange("(p g) o -> p (g o)", p=128), o=NR: e.dma_start(out=o[:], in_=a), sL, 16)
        sc.w("vector", sL, sL.n)
        sc.w("vector", sX_, sX_.n)
        if r > 0:
            sc.w("vector", sm["sC"], sm["sC"].n)
        if not do_dq:
            sc.o("vector", lambda e: e.tensor_copy(WD0[:], QF[:]), sV, 1)
            sc.o("vector", lambda e: e.tensor_copy(WD1[:], MF[:]), sV, 1)
        else:
            _dq_marker = 1
        if not do_dq:
            pass
        if do_dq:
            sc.o("vector", lambda e: e.tensor_scalar(SCC[:], SC_[:], S2b[:, 0:1], None, op0=ALU.mult))
        for g in range(32 if do_dq else 0):
            sc.o("vector", lambda e, g=g: e.tensor_scalar(
                T0[:, g * 128:(g + 1) * 128], QF[:, g * 128:(g + 1) * 128],
                ZS[:, g:g + 1], SCC[:, g:g + 1], op0=ALU.subtract, op1=ALU.mult))
        if do_dq:
            sc.o("vector", lambda e: e.tensor_tensor(WD0[:], T0[:], MF[:], op=ALU.mult), sV, 1)
        b0_, b1_, b2_ = BY[:, 0:1536:3], BY[:, 1:1536:3], BY[:, 2:1536:3]
        qs = lambda v: Q1[:, v:4096:8]
        if not do_dq:
            qs = None
        if do_dq:
            sc.o("vector", lambda e: e.tensor_single_scalar(qs(0), b0_, 7, op=ALU.bitwise_and))
            sc.o("vector", lambda e: e.tensor_scalar(qs(1), b0_, 3, 7, op0=ALU.logical_shift_right, op1=ALU.bitwise_and))
            sc.o("vector", lambda e: e.tensor_scalar(TA[:], b0_, 6, 3, op0=ALU.logical_shift_right, op1=ALU.bitwise_and))
            sc.o("vector", lambda e: e.tensor_scalar(TB[:], b1_, 1, 2, op0=ALU.bitwise_and, op1=ALU.logical_shift_left))
            sc.o("vector", lambda e: e.tensor_tensor(qs(2), TA[:], TB[:], op=ALU.add))
            sc.o("vector", lambda e: e.tensor_scalar(qs(3), b1_, 1, 7, op0=ALU.logical_shift_right, op1=ALU.bitwise_and))
            sc.o("vector", lambda e: e.tensor_scalar(qs(4), b1_, 4, 7, op0=ALU.logical_shift_right, op1=ALU.bitwise_and))
            sc.o("vector", lambda e: e.tensor_scalar(TA[:], b1_, 7, 1, op0=ALU.logical_shift_right, op1=ALU.bitwise_and))
            sc.o("vector", lambda e: e.tensor_scalar(TB[:], b2_, 3, 1, op0=ALU.bitwise_and, op1=ALU.logical_shift_left))
            sc.o("vector", lambda e: e.tensor_tensor(qs(5), TA[:], TB[:], op=ALU.add))
            sc.o("vector", lambda e: e.tensor_scalar(qs(6), b2_, 2, 7, op0=ALU.logical_shift_right, op1=ALU.bitwise_and))
            sc.o("vector", lambda e: e.tensor_scalar(qs(7), b2_, 5, 7, op0=ALU.logical_shift_right, op1=ALU.bitwise_and))
            sc.o("vector", lambda e: e.tensor_copy(QF2[:], Q1[:]))
            sc.o("vector", lambda e: e.tensor_copy(
                NE[:].rearrange("p (g j) -> p g j", j=32),
                NR[:].unsqueeze(2).broadcast_to((128, 128, 32))))
            sc.o("vector", lambda e: e.scalar_tensor_tensor(T1[:], QF2[:], 2.0 / 7.0, NE[:], op0=ALU.mult, op1=ALU.mult))
            sc.o("vector", lambda e: e.tensor_tensor(WD1[:], T1[:], NE[:], op=ALU.subtract), sV, 1)
        sc.w("sync", sV, sV.n)
        sc.o("sync", lambda e, d=io["wdq0"][r0:r1, :]: e.dma_start(out=d, in_=WD0[:]), sC, 16)
        sc.o("sync", lambda e, d=io["wdq1"][r0:r1, :]: e.dma_start(out=d, in_=WD1[:]), sC, 16)

    XT = [E(nc.sbuf_tensor(t + f"XT{i}", [128, 512], BF16)) for i in range(2)]
    sc.w("sync", sC, sC.n)
    sW = [sm["sW0"], sm["sW1"]]
    sS = [sm["sS0"], sm["sS1"]]
    sb = [sm["sS0"].n, sm["sS1"].n]
    idx = 0
    for mi, wsrc in ((0, io["wdq0"]), (1, io["wdq1"])):
        for ncb in range(32):
            pr = idx % 2
            if idx >= 2:
                sc.w("sync", sS[pr], sb[pr] + 16 * (idx // 2))
            sc.o("sync", lambda e, a=wsrc[0:512, ncb * 128:(ncb + 1) * 128], o=XT[pr]:
                 e.dma_start(out=o[:], in_=a, transpose=True), sW[pr], 16)
            sc.w("sync", sW[pr], sW[pr].n)
            dstp = agw_in[(2 + mi) * 4096 + ncb * 128:(2 + mi) * 4096 + (ncb + 1) * 128, :]
            sc.o("sync", lambda e, d=dstp, o=XT[pr]: e.dma_start(out=d, in_=o[:]), sS[pr], 16)
            idx += 1


def emit_tm2fm(cx, src, dst, R, C, cast, pre_bounced=False, bounce_only=False):
    nc, sc, E, sm = cx.nc, cx.sc, cx.E, cx.sm
    t = cx.tag
    nch = R // 128
    ndc = C // 128
    XB = [E(nc.sbuf_tensor(t + f"XB{i}", [128, C], BF16)) for i in range(2)]
    XT = [E(nc.sbuf_tensor(t + f"XT{i}", [128, R], BF16)) for i in range(2)]
    sX, sC = sm["sX"], sm["sC"]
    sW = [sm["sW0"], sm["sW1"]]
    sS = [sm["sS0"], sm["sS1"]]
    bm = cx.tmb
    cb0 = sC.n
    if pre_bounced:
        nch = 0
    for tc in range(nch):
        pr = tc % 2
        if tc >= 2:
            sc.w("gpsimd", sC, cb0 + 16 * (tc - 1))
        sc.o("gpsimd", lambda e, a=src[tc * 128:(tc + 1) * 128, :], o=XB[pr]:
             e.dma_start(out=o[:], in_=a), sX, 16)
        sc.w("sync", sX, sX.n)
        sc.o("sync", lambda e, d=bm[tc * 128:(tc + 1) * 128, 0:C], o=XB[pr]:
             e.dma_start(out=d, in_=o[:]), sC, 16)
    sc.w("sync", sC, sC.n)
    if bounce_only:
        ndc = 0
    sb = [sm["sS0"].n, sm["sS1"].n]
    for dc in range(ndc):
        pr = dc % 2
        if dc >= 2:
            sc.w("sync", sS[pr], sb[pr] + 16 * (dc // 2))
        sc.o("sync", lambda e, a=bm[0:R, dc * 128:(dc + 1) * 128], o=XT[pr]:
             e.dma_start(out=o[:, 0:R], in_=a, transpose=True), sW[pr], 16)
        sc.w("sync", sW[pr], sW[pr].n)
        sc.o("sync", lambda e, d=dst[dc * 128:(dc + 1) * 128, 0:R], o=XT[pr]:
             e.dma_start(out=d, in_=o[:, 0:R]), sS[pr], 16)


def emit_qkv_fused(cx, layer, io):
    nc, sc, E, sm = cx.nc, cx.sc, cx.E, cx.sm
    t = cx.tag
    XH = [E(nc.sbuf_tensor(t + f"XH{i}", [128, 16 * 1024], BF16)) for i in range(3)]
    W16 = [E(nc.sbuf_tensor(t + f"W16{i}", [128, 32 * 512], BF16)) for i in range(2)]
    B16 = E(nc.sbuf_tensor(t + "B16", [1, 3 * EH], BF16))
    EVF = [E(nc.sbuf_tensor(t + f"EVF{i}", [128, 512], F32)) for i in range(4)]
    EVB = [E(nc.sbuf_tensor(t + f"EVB{i}", [128, 512], BF16)) for i in range(4)]
    PS = [E(nc.psum_tensor(t + f"PS{i}", [128, 512], F32)) for i in range(4)]
    sX, sMM, sEV = sm["sX"], sm["sMM"], sm["sEV"]
    sW = [sm["sW0"], sm["sW1"]]
    sST = [sm["sS0"], sm["sS1"], sm["sS2"], sm["sS3"]]
    sLq = [sm["sL"], sm["sL2"]]
    agxt, wT = io["agxt"], io["wT"]
    outs = io["outs"]
    stats = layer == 0
    if stats:
        STATS = E(nc.sbuf_tensor(t + "STATS", [128, 128], F32))
        SQS = E(nc.sbuf_tensor(t + "SQS", [128, 512], F32))
        ACC1 = E(nc.sbuf_tensor(t + "ACC1", [128, 1], F32))
        ACC2 = E(nc.sbuf_tensor(t + "ACC2", [128, 1], F32))
        sc.o("vector", lambda e: e.memset(STATS[:], 0.0))

    xb = sX.n
    bias_off = 0
    if layer == 1:
        sc.o("gpsimd", lambda e: e.dma_start(out=B16[:], in_=io["b3"]), sX, 16)
        bias_off = 16
    wb_base = [sm["sW0"].n, sm["sW1"].n]
    mmb = sMM.n
    evb = sEV.n
    stb = [x.n for x in sST]
    scb = sm["sSc"].n

    def wload(w):
        tg, et = divmod(w, 6)
        src = wT[:, et * 512:(et + 1) * 512].rearrange("(i p) e -> p i e", p=128)
        dstw = W16[w % 2][:].rearrange("p (i e) -> p i e", i=32)
        if w >= 2:
            sc.w("sync", sMM, mmb + (w - 1) * 8)
        sc.o("sync", lambda e, s=src, d=dstw: e.dma_start(out=d, in_=s), sW[w % 2], 16)

    xcnt = {}

    def xload(s):
        tg_, h_ = divmod(s, 2)
        buf = s % 3
        if s >= 3:
            sc.w("sync", sMM, mmb + 48 * ((s - 3) // 2 + 1))
        for dd in range(16):
            d_ = h_ * 16 + dd
            sc.o("sync", lambda e, tg_=tg_, d_=d_, buf=buf, dd=dd: e.dma_start(
                out=XH[buf][:, dd * 1024:(dd + 1) * 1024],
                in_=agxt[tg_ * 1024:(tg_ + 1) * 1024, d_ * 128:(d_ + 1) * 128],
                transpose=True), sLq[s % 2], 16)
        xcnt[s] = sLq[s % 2].n

    wload(0)
    wload(1)
    xload(0)
    xload(1)
    xload(2)
    n = 0
    for tg in range(4):
        for et in range(6):
            w = tg * 6 + et
            if bias_off:
                sc.w("tensor", sX, xb + bias_off)
            sc.w("tensor", sLq[(2 * tg) % 2], xcnt[2 * tg])
            if 2 * tg + 1 in xcnt:
                sc.w("tensor", sLq[(2 * tg + 1) % 2], xcnt[2 * tg + 1])
            sc.w("tensor", sW[w % 2], wb_base[w % 2] + 16 * (w // 2 + 1))
            if et == 4 and 2 * tg + 2 <= 7:
                xload(2 * tg + 2)
            if et == 5 and 2 * tg + 3 <= 7:
                xload(2 * tg + 3)
            sec = et // 2
            ecol = (et % 2) * 512
            for tc in range(8):
                n += 1
                bank = (n - 1) % 4
                if n > 4:
                    sc.w("tensor", sEV, evb + n - 4)
                for d in range(32):
                    last = (d == 31) and (layer == 0)
                    hb = (2 * tg + d // 16) % 3
                    sc.o("tensor", lambda e, d=d, b=bank, tc=tc, st=(d == 0), sp=last, wp=w % 2, hb=hb:
                         e.matmul(PS[b][:], XH[hb][:, (d % 16) * 1024 + tc * 128: (d % 16) * 1024 + tc * 128 + 128],
                                  W16[wp][:, d * 512:(d + 1) * 512], start=st, stop=sp),
                         sMM if last else None, 1)
                if layer == 1:
                    sc.o("tensor", lambda e, b=bank, c=et * 512:
                         e.matmul(PS[b][:], cx.ONESB[0:1, :], B16[0:1, c:c + 512], start=False, stop=True),
                         sMM, 1)
                sc.w("vector", sMM, mmb + n)
                if n > 4:
                    sc.w("vector", sST[bank], stb[bank] + 16 * ((n - 4 - 1) // 4 + 1))
                    if stats:
                        sc.w("vector", sm["sSc"], scb + max(0, _nsq(n - 4)))
                if layer == 0:
                    tgt = EVF[bank] if sec < 2 else EVB[bank]
                    sc.o("vector", lambda e, b=bank, tt=tgt: e.tensor_scalar(
                        tt[:], PS[b][:], -8.0, 8.0, op0=ALU.max, op1=ALU.min), sEV, 1)
                    if sec < 2 and stats:
                        sc.o("vector", lambda e, tt=tgt: e.reduce_sum(
                            ACC1[:], tt[:], axis=AX.X))
                else:
                    tgt = EVB[bank]
                    if sec == 0:
                        sc.o("vector", lambda e, b=bank, tt=tgt: e.tensor_scalar(
                            tt[:], PS[b][:], 1.0 / 128.0, None, op0=ALU.mult), sEV, 1)
                    else:
                        sc.o("vector", lambda e, b=bank, tt=tgt: e.tensor_copy(tt[:], PS[b][:]), sEV, 1)
                if stats and sec < 2:
                    tcg = tg * 8 + tc
                    c0 = tcg * 4 + (0 if sec == 0 else 2)
                    sc.o("vector", lambda e, c0=c0: e.tensor_tensor(
                        STATS[:, c0:c0 + 1], STATS[:, c0:c0 + 1], ACC1[:], op=ALU.add))
                    sc.w("scalar", sEV, evb + n)
                    sq = sc.o("scalar", lambda e, b=bank, tt=tgt: e.activation(
                        SQS[:], tt[:], AF.Square, accum_out=ACC2[:]), sm["sSc"], 1)
                    sc.w("vector", sm["sSc"], sq)
                    sc.o("vector", lambda e, c0=c0: e.tensor_tensor(
                        STATS[:, c0 + 1:c0 + 2], STATS[:, c0 + 1:c0 + 2], ACC2[:], op=ALU.add))
                sc.w("sync", sEV, sEV.n)
                t0 = tg * 1024 + tc * 128
                sc.o("sync", lambda e, o=outs[sec][t0:t0 + 128, ecol:ecol + 512], tt=tgt:
                     e.dma_start(out=o, in_=tt[:]), sST[bank], 16)
            if w + 2 < 24:
                wload(w + 2)
    if stats:
        vs = sc.o("vector", lambda e: e.engine_nop(), sm["sV"], 1)
        sc.w("sync", sm["sV"], vs)
        sc.o("sync", lambda e: e.dma_start(out=io["ar_in"], in_=STATS[:]), sm["sL"], 16)
        sc.w("gpsimd", sm["sL"], sm["sL"].n)
        sc.o("gpsimd", lambda e: e.collective_compute(
            "AllReduce", ALU.add, GROUPS_B, ins=[io["ar_in"]], outs=[io["ar_out"]]),
            sm["sAR"], 1)


def _nsq(k):
    full, rem = divmod(k, 48)
    cnt = full * 32
    cnt += min(rem, 32)
    return cnt


def emit_qkfin_fused(cx, layer, io):
    nc, sc, E, sm = cx.nc, cx.sc, cx.E, cx.sm
    t = cx.tag
    sL, sT, sV, sEV = sm["sL"], sm["sT"], sm["sV"], sm["sEV"]
    sW = [sm["sW0"], sm["sW1"]]
    sS = [sm["sS0"], sm["sS1"]]
    sC = sm["sC"]
    qtm, ktm, q2fm, k2fm = io["qtm"], io["ktm"], io["q2fm"], io["k2fm"]

    if layer == 0:
        ARS = E(nc.sbuf_tensor(t + "ARS", [128, 128], F32))
        GQ = E(nc.sbuf_tensor(t + "GQ", [1, EH], F32))
        GK = E(nc.sbuf_tensor(t + "GK", [1, EH], F32))
        G1 = E(nc.sbuf_tensor(t + "G1", [1, EH], F32))
        G16 = E(nc.sbuf_tensor(t + "G16", [1, EH], BF16))
        GBC = E(nc.sbuf_tensor(t + "GBC", [128, EH], F32))
        MU = E(nc.sbuf_tensor(t + "MU", [128, 1], F32))
        VAR = E(nc.sbuf_tensor(t + "VAR", [128, 1], F32))
        TT1 = E(nc.sbuf_tensor(t + "TT1", [128, 1], F32))
        SD = E(nc.sbuf_tensor(t + "SD", [128, 1], F32))
        RSD = E(nc.sbuf_tensor(t + "RSD", [128, 1], F32))
        MUK = E(nc.sbuf_tensor(t + "MUK", [128, 1], F32))
        RSDK = E(nc.sbuf_tensor(t + "RSDK", [128, 1], F32))
        QC = E(nc.sbuf_tensor(t + "QC", [128, EH], F32))
        KC = E(nc.sbuf_tensor(t + "KC", [128, EH], F32))
        TAP = E(nc.sbuf_tensor(t + "TAP", [128, EH], F32))
        Q2C = E(nc.sbuf_tensor(t + "Q2C", [128, EH], BF16))
        K2C = E(nc.sbuf_tensor(t + "K2C", [128, EH], BF16))
        PSG = E(nc.psum_tensor(t + "PSG", [128, 512], F32))
        qbm, kbm = io["qbm"], io["kbm"]

        sc.w("sync", sm["sAR"], sm["sAR"].n)
        sc.o("sync", lambda e: e.dma_start(out=ARS[:], in_=io["ar_out"]), sL, 16)
        sc.o("sync", lambda e: e.dma_start(out=GQ[:], in_=io["gq"]), sL, 16)
        sc.o("sync", lambda e: e.dma_start(out=GK[:], in_=io["gk"]), sL, 16)
        sc.w("vector", sL, sL.n)
        sc.o("vector", lambda e: e.scalar_tensor_tensor(G1[:], GQ[:], ISQ, GK[:], op0=ALU.mult, op1=ALU.mult))
        sc.o("vector", lambda e: e.tensor_copy(G16[:], G1[:]), sV, 1)
        for c_ in range(2):
            sc.w("tensor", sV, sV.n)
            sc.o("tensor", lambda e, c_=c_: e.matmul(PSG[:], cx.ONESB[0:1, :], G16[0:1, c_ * 512:(c_ + 1) * 512],
                                                     start=True, stop=True), sT, 1)
            sc.w("vector", sT, sT.n)
            sc.o("vector", lambda e, c_=c_: e.tensor_copy(GBC[:, c_ * 512:(c_ + 1) * 512], PSG[:]), sV, 1)

        cb0 = sC.n
        prev_vq = None
        for tcg in range(32):
            r0 = tcg * 128
            if prev_vq is not None:
                sc.w("sync", sV, prev_vq)
            if tcg >= 1:
                sc.w("sync", sC, cb0 + 32 * tcg)
            sc.o("sync", lambda e, a=qtm[r0:r0 + 128, :]: e.dma_start(out=QC[:], in_=a), sL, 16)
            sc.o("sync", lambda e, a=ktm[r0:r0 + 128, :]: e.dma_start(out=KC[:], in_=a), sL, 16)
            sc.w("vector", sL, sL.n)
            c4 = tcg * 4
            sc.o("vector", lambda e, c4=c4: e.tensor_scalar(MU[:], ARS[:, c4:c4 + 1], 1.0 / 4096.0, None, op0=ALU.mult))
            sc.o("vector", lambda e, c4=c4: e.tensor_scalar(VAR[:], ARS[:, c4 + 1:c4 + 2], 1.0 / 4096.0, None, op0=ALU.mult))
            sc.o("vector", lambda e: e.tensor_tensor(TT1[:], MU[:], MU[:], op=ALU.mult))
            vv = sc.o("vector", lambda e: e.tensor_tensor(VAR[:], VAR[:], TT1[:], op=ALU.subtract), sV, 1)
            sc.w("scalar", sV, vv)
            sq = sc.o("scalar", lambda e: e.activation(SD[:], VAR[:], AF.Sqrt, bias=cx.EPSC[:], scale=1.0), sm["sSc"], 1)
            sc.w("vector", sm["sSc"], sq)
            sc.o("vector", lambda e: e.reciprocal(RSD[:], SD[:]))
            sc.o("vector", lambda e, c4=c4: e.tensor_scalar(MUK[:], ARS[:, c4 + 2:c4 + 3], 1.0 / 4096.0, None, op0=ALU.mult))
            sc.o("vector", lambda e, c4=c4: e.tensor_scalar(VAR[:], ARS[:, c4 + 3:c4 + 4], 1.0 / 4096.0, None, op0=ALU.mult))
            sc.o("vector", lambda e: e.tensor_tensor(TT1[:], MUK[:], MUK[:], op=ALU.mult))
            vv = sc.o("vector", lambda e: e.tensor_tensor(VAR[:], VAR[:], TT1[:], op=ALU.subtract), sV, 1)
            sc.w("scalar", sV, vv)
            sq = sc.o("scalar", lambda e: e.activation(SD[:], VAR[:], AF.Sqrt, bias=cx.EPSC[:], scale=1.0), sm["sSc"], 1)
            sc.w("vector", sm["sSc"], sq)
            sc.o("vector", lambda e: e.reciprocal(RSDK[:], SD[:]))
            sc.o("vector", lambda e: e.tensor_scalar(TAP[:], QC[:], MU[:, 0:1], RSD[:, 0:1], op0=ALU.subtract, op1=ALU.mult))
            sc.o("vector", lambda e: e.tensor_tensor(Q2C[:], TAP[:], GBC[:], op=ALU.mult))
            vq = sc.o("vector", lambda e: e.tensor_scalar(K2C[:], KC[:], MUK[:, 0:1], RSDK[:, 0:1], op0=ALU.subtract, op1=ALU.mult), sV, 1)
            prev_vq = vq
            sc.w("sync", sV, vq)
            sc.o("sync", lambda e, d=qbm[r0:r0 + 128, :]: e.dma_start(out=d, in_=Q2C[:]), sC, 16)
            sc.o("sync", lambda e, d=kbm[r0:r0 + 128, :]: e.dma_start(out=d, in_=K2C[:]), sC, 16)
        sc.w("sync", sC, sC.n)


def emit_heat(cx, ps, n):
    for _ in range(n):
        cx.sc.o("tensor", lambda e, ps=ps: e.matmul(ps, cx.IDB[:], cx.HEATR[:], start=True, stop=True))


def emit_attn_fused(cx, layer, io):
    nc, sc, E, sm = cx.nc, cx.sc, cx.E, cx.sm
    t = cx.tag
    QF = [E(nc.sbuf_tensor(t + f"QF{i}", [128, S], BF16)) for i in range(3)]
    KF = [E(nc.sbuf_tensor(t + f"KF{i}", [128, S], BF16)) for i in range(3)]
    VSB = [E(nc.sbuf_tensor(t + f"VSB{i}", [128, S], BF16)) for i in range(2)]
    KS5 = [E(nc.sbuf_tensor(t + f"KS5{i}", [5, S], BF16)) for i in range(2)]
    QROW3 = [E(nc.sbuf_tensor(t + f"QROW3{i}", [3, S], BF16)) for i in range(2)]
    KSCA = [E(nc.sbuf_tensor(t + f"KSCA{i}", [128, 32], F32)) for i in range(2)]
    KSC2 = [E(nc.sbuf_tensor(t + f"KSC2{i}", [128, 32], F32)) for i in range(3)]
    KSCF = [E(nc.sbuf_tensor(t + f"KSCF{i}", [128, 32], F32)) for i in range(2)]
    CKHc = [E(nc.sbuf_tensor(t + f"CKHc{i}", [128, 32], BF16)) for i in range(2)]
    CKLc = [E(nc.sbuf_tensor(t + f"CKLc{i}", [128, 32], BF16)) for i in range(2)]
    THs = [E(nc.sbuf_tensor(t + f"THs{i}", [128, 128], BF16)) for i in range(2)]
    TLs = [E(nc.sbuf_tensor(t + f"TLs{i}", [128, 128], BF16)) for i in range(2)]
    UB = E(nc.sbuf_tensor(t + "UB", [8, 128], F32))
    UW = E(nc.sbuf_tensor(t + "UW", [8, 128], F32))
    U8 = E(nc.sbuf_tensor(t + "U8", [8, 128], F32))
    UCF = E(nc.sbuf_tensor(t + "UCF", [128, 8], F32))
    UCB = E(nc.sbuf_tensor(t + "UCB", [128, 8], BF16))
    NMB = [E(nc.sbuf_tensor(t + f"NMB{i}", [128, 4], F32)) for i in range(2)]
    SGB = [E(nc.sbuf_tensor(t + f"SGB{i}", [128, 4], F32)) for i in range(2)]
    SGW = [E(nc.sbuf_tensor(t + f"SGW{i}", [128, 4], F32)) for i in range(2)]
    EXB = [E(nc.sbuf_tensor(t + f"EXB{i}", [128, 4], F32)) for i in range(2)]
    NMc = [E(nc.sbuf_tensor(t + f"NMc{i}", [128, 1], F32)) for i in range(2)]
    SGc = [E(nc.sbuf_tensor(t + f"SGc{i}", [128, 1], F32)) for i in range(2)]
    LNS = [E(nc.sbuf_tensor(t + f"LNS{i}", [128, 1], F32)) for i in range(2)]
    QAB = [E(nc.sbuf_tensor(t + f"QAB{i}", [128, 4], F32)) for i in range(2)]
    QA3G = [E(nc.sbuf_tensor(t + f"QA3G{i}", [128, 12], F32)) for i in range(2)]
    QH4 = [E(nc.sbuf_tensor(t + f"QH4{i}", [128, 4], BF16)) for i in range(2)]
    QR4 = [E(nc.sbuf_tensor(t + f"QR4{i}", [128, 4], F32)) for i in range(2)]
    ESC = E(nc.sbuf_tensor(t + "ESC", [128, 1024], BF16))
    Z4 = E(nc.sbuf_tensor(t + "Z4", [128, 4], F32))
    ZPAD = E(nc.sbuf_tensor(t + "ZPAD", [128, 96], BF16))
    PTS = [E(nc.sbuf_tensor(t + f"PTS{i}", [128, 512], BF16)) for i in range(4)]
    OEV = [E(nc.sbuf_tensor(t + f"OEV{i}", [128, 512], BF16)) for i in range(2)]
    PALL = E(nc.psum_tensor(t + "PALL", [128, 4096], F32))
    sT, sV, sSc = sm["sT"], sm["sV"], sm["sSc"]
    sE2, sPV = sm["sE2"], sm["sPV"]
    sO = [sm["sO0"], sm["sO1"]]
    sLq = [sm["sL"], sm["sL2"]]
    sRr = [sm["sR"], sm["sR2"]]
    sVv = [sm["sVa"], sm["sVb"]]
    q2fm, k2fm, vtm = io["q2fm"], io["k2fm"], io["vtm"]
    agoi, ago8 = io["agoi"], io["ago8"]
    ckd = io["ckd"]
    ahi, ami, alo, acolt = io["ahi"], io["ami"], io["alo"], io["acolt"]

    b7 = {"v": 0}
    ldQ = [0] * 8
    ldR = [0] * 8
    ldV = [0] * 8
    ckst = [0] * 8
    ckst2 = [0] * 8
    ksAdd = [0] * 8
    exps_last = [0] * 8
    oa = [[0, 0] for _ in range(8)]
    pv_after = [0] * 8
    e2_after = [0] * 8
    qrow_ev = [[0] * 8 for _ in range(8)]

    sc.o("vector", lambda e: e.memset(Z4[:], 0.0))
    zv = sc.o("vector", lambda e: e.memset(ZPAD[:], 0.0), sV, 1)
    sc.w("sync", sV, zv)
    for pi_ in range(2):
        sc.o("sync", lambda e, pi_=pi_: e.dma_start(out=ckd[pi_]["h1"][:, 32:128], in_=ZPAD[:]), sRr[pi_], 16)
        sc.o("sync", lambda e, pi_=pi_: e.dma_start(out=ckd[pi_]["l1"][:, 32:128], in_=ZPAD[:]), sRr[pi_], 16)
    sc.o("sync", lambda e: e.dma_start(out=UB[:], in_=io["ub8"]), sLq[0], 16)
    sc.o("sync", lambda e: e.dma_start(out=UW[:], in_=io["uw8"]), sLq[0], 16)
    sc.w("vector", sLq[0], sLq[0].n)
    sc.o("vector", lambda e: e.scalar_tensor_tensor(
        U8[:], UB[:], ISQ, UW[:], op0=ALU.mult, op1=ALU.mult), sV, 1)
    sc.w("tensor", sV, sV.n)
    sc.o("tensor", lambda e: e.transpose(PALL[0:128, 3584:3592], U8[:], cx.IDF[0:8, 0:8]), sT, 1)
    sc.w("vector", sT, sT.n)
    sc.o("vector", lambda e: e.tensor_copy(UCF[:], PALL[0:128, 3584:3592]))
    b7["v"] = sc.o("vector", lambda e: e.tensor_copy(UCB[:], UCF[:]), sV, 1)

    def loads_qk(p, gates=()):
        pi = p % 2
        for s_, v_ in gates:
            sc.w("sync", s_, v_)
        sc.o("sync", lambda e, p=p: e.dma_start(
            out=QF[p % 3][:], in_=q2fm[0:S, p * 128:(p + 1) * 128], transpose=True), sLq[pi], 16)
        sc.o("sync", lambda e, p=p: e.dma_start(
            out=KF[p % 3][:], in_=k2fm[0:S, p * 128:(p + 1) * 128], transpose=True), sLq[pi], 16)
        sc.o("sync", lambda e, p=p: e.dma_start(out=KSCA[pi][:], in_=acolt[p]), sLq[pi], 16)
        ldQ[p] = sLq[pi].n
        sc.o("sync", lambda e, p=p: e.dma_start(out=KS5[pi][0:1, :], in_=ahi[p:p + 1, :]), sRr[pi], 16)
        sc.o("sync", lambda e, p=p: e.dma_start(out=KS5[pi][1:2, :], in_=ami[p:p + 1, :]), sRr[pi], 16)
        sc.o("sync", lambda e, p=p: e.dma_start(out=KS5[pi][2:3, :], in_=alo[p:p + 1, :]), sRr[pi], 16)

    def loads_v(p, gates=()):
        pi = p % 2
        for s_, v_ in gates:
            sc.w("sync", s_, v_)
        sc.o("sync", lambda e, p=p: e.dma_start(
            out=VSB[pi][:].rearrange("p (kc d) -> p kc d", kc=32),
            in_=vtm[:, p * 128:(p + 1) * 128].rearrange("(kc p) d -> p kc d", p=128)), sVv[pi], 16)
        ldV[p] = sVv[pi].n

    def setup(p):
        pi = p % 2
        sc.w("tensor", sLq[pi], ldQ[p])
        sc.w("tensor", sV, b7["v"])
        for kc in range(32):
            sc.o("tensor", lambda e, kc=kc, p=p: e.matmul(
                PALL[0:128, 3840 + kc:3841 + kc], KF[p % 3][:, kc * 128:(kc + 1) * 128],
                UCB[:, p:p + 1], start=True, stop=True),
                sT if kc == 31 else None, 1)
        tpe = sT.n
        sc.w("vector", sT, tpe)
        if p >= 2:
            sc.w("vector", sRr[pi], ckst[p - 2])
        sc.o("vector", lambda e, pi=pi: e.tensor_copy(KSCF[pi][:], PALL[0:128, 3840:3872]))
        if p >= 3:
            sc.w("vector", sE2, e2_after[p - 3])
        vts = sc.o("vector", lambda e, p=p, pi=pi: e.tensor_tensor(
            KSC2[p % 3][:], PALL[0:128, 3840:3872], KSCA[pi][:], op=ALU.add), sV, 1)
        b7["v"] = vts
        ksAdd[p] = vts
        sc.o("vector", lambda e, pi=pi: e.tensor_copy(CKHc[pi][:], KSCF[pi][:]))
        vck = sc.o("vector", lambda e, pi=pi: e.scalar_tensor_tensor(
            CKLc[pi][:], KSCF[pi][:], 1.0, CKHc[pi][:], op0=ALU.mult, op1=ALU.subtract), sV, 1)
        sc.w("sync", sV, vck)
        if p >= 2:
            sc.w("sync", sRr[pi], ckst2[p - 2])
        sc.o("sync", lambda e, pi=pi: e.dma_start(out=ckd[pi]["h1"][:, 0:32], in_=CKHc[pi][:]), sRr[pi], 16)
        sc.o("sync", lambda e, pi=pi: e.dma_start(out=ckd[pi]["l1"][:, 0:32], in_=CKLc[pi][:]), sRr[pi], 16)
        ckst[p] = sRr[pi].n
        sc.w("sync", sRr[pi], sRr[pi].n)
        sc.o("sync", lambda e, pi=pi: e.dma_start(out=THs[pi][:], in_=ckd[pi]["h1"], transpose=True), sRr[pi], 16)
        sc.o("sync", lambda e, pi=pi: e.dma_start(out=TLs[pi][:], in_=ckd[pi]["l1"], transpose=True), sRr[pi], 16)
        sc.w("sync", sRr[pi], sRr[pi].n)
        sc.o("sync", lambda e, pi=pi: e.dma_start(out=ckd[pi]["h2"], in_=THs[pi][0:32, :]), sRr[pi], 16)
        sc.o("sync", lambda e, pi=pi: e.dma_start(out=ckd[pi]["l2"], in_=TLs[pi][0:32, :]), sRr[pi], 16)
        sc.w("sync", sRr[pi], sRr[pi].n)
        sc.o("sync", lambda e, pi=pi: e.dma_start(
            out=KS5[pi][3:4, :], in_=ckd[pi]["h2"].rearrange("(o c) w -> o (c w)", o=1)), sRr[pi], 16)
        sc.o("sync", lambda e, pi=pi: e.dma_start(
            out=KS5[pi][4:5, :], in_=ckd[pi]["l2"].rearrange("(o c) w -> o (c w)", o=1)), sRr[pi], 16)
        ldR[p] = sRr[pi].n
        ckst2[p] = sRr[pi].n

    p2 = {"t": 0, "pvc": [], "e2c": [], "vo": 0, "pend": None, "pvq": []}

    def p2_evac_flush():
        if p2["pend"] is None:
            return
        ph, qg, ob, pvend = p2["pend"]
        p2["pend"] = None
        sc.w("vector", sPV, pvend)
        if sO[ob].n > 0:
            sc.w("vector", sO[ob], sO[ob].n)
        vo = sc.o("vector", lambda e, ob=ob: e.tensor_copy(OEV[ob][:], PALL[:, 3072:3584]), sV, 1)
        p2["vo"] = vo
        sc.w("sync", sV, vo)
        sc.o("sync", lambda e, ph=ph, qg=qg, ob=ob: e.dma_start(
            out=agoi[ph][:, qg * 512:(qg + 1) * 512], in_=OEV[ob][:]), sO[ob], 16)
        oa[ph] = [sO[0].n, sO[1].n]

    def p2_score(ph, qg, kc):
        if kc == 0:
            sc.w("tensor", sV, qrow_ev[ph][qg])
            if qg == 0:
                sc.w("scalar", sV, ksAdd[ph])
        tg = p2["t"]
        p2["t"] += 1
        slot = tg % 2
        pslot = tg % 4
        po = 2048 + slot * 512
        if tg >= 2:
            sc.w("tensor", sE2, p2["e2c"][tg - 2])
        bq, bk, br = ph % 3, ph % 3, ph % 2
        sc.o("tensor", lambda e, kc=kc, qg=qg, po=po, bk=bk, bq=bq: e.matmul(
            PALL[:, po:po + 512], KF[bk][:, kc * 128:kc * 128 + 128],
            QF[bq][:, qg * 512:(qg + 1) * 512], start=True, stop=False))
        c2 = kc * 128 - qg * 512
        if c2 >= 0:
            sc.o("tensor", lambda e, po=po, c2=c2: e.matmul(
                PALL[:, po:po + 512], cx.IDB[:],
                cx.TRITB[:, 384 - c2:384 - c2 + 512], start=False, stop=False))
        tmm = sc.o("tensor", lambda e, qg=qg, po=po, br=br: e.matmul(
            PALL[:, po:po + 512], cx.ONESB[0:3, :],
            QROW3[br][0:3, qg * 512:(qg + 1) * 512], start=False, stop=True), sT, 1)
        sc.w("scalar", sT, tmm)
        if tg >= 4:
            sc.w("scalar", sPV, p2["pvc"][tg - 4])
        sex = sc.o("scalar", lambda e, pslot=pslot, po=po, kc=kc, ph=ph: e.activation(
            PTS[pslot][:], PALL[:, po:po + 512], AF.Exp,
            bias=KSC2[ph % 3][:, kc:kc + 1], scale=1.0), sE2, 1)
        p2["e2c"].append(sex)
        p2["pvq"].append((ph, qg, kc, pslot, sex))

    def p2_pop_pv():
        ph, qg, kc, pslot, e2c = p2["pvq"].pop(0)
        kcmax = qg * 4 + 4
        if kc == 0:
            p2_evac_flush()
            if p2["vo"]:
                sc.w("tensor", sV, p2["vo"])
            sc.w("tensor", sVv[ph % 2], ldV[ph])
        sc.w("tensor", sE2, e2c)
        p2["pvc"].append(sc.o("tensor", lambda e, kc=kc, pslot=pslot, bv2=ph % 2,
                              st=(kc == 0), sp=(kc == kcmax - 1): e.matmul(
            PALL[:, 3072:3584], VSB[bv2][:, kc * 128:kc * 128 + 128],
            PTS[pslot][:], start=st, stop=sp), sPV, 1))
        if kc == kcmax - 1:
            p2["pend"] = (ph, qg, qg % 2, sPV.n)
            if qg == 7:
                pv_after[ph] = sPV.n
                e2_after[ph] = sE2.n

    def fire_ag(ph):
        sc.w("gpsimd", sO[0], sO[0].n)
        sc.w("gpsimd", sO[1], sO[1].n)
        sc.o("gpsimd", lambda e, ph=ph: e.collective_compute(
            "AllGather", ALU.bypass, GROUPS_B, ins=[agoi[ph]], outs=[ago8[ph]]),
            sm["sAGO"], 1)

    loads_qk(0)
    loads_v(0)
    setup(0)
    nmb_scgate = [0, 0]
    cmb_dvgate = [0, 0]
    qa3_pe = [0, 0]
    for p in range(9):
        if p < 8:
            if p + 1 < 8:
                g_ = []
                if p - 2 >= 0:
                    g_ += [(sO[0], oa[p - 2][0]), (sO[1], oa[p - 2][1]),
                           (sSc, exps_last[p - 2])]
                if p - 1 >= 0:
                    g_ += [(sSc, exps_last[p - 1]), (sV, ksAdd[p - 1])]
                loads_qk(p + 1, g_)
            if p >= 1:
                loads_v(p, [(sPV, pv_after[p - 2])] if p >= 2 else ())
            exps = []
            u = 0
            pendg = None
            tlist = [(qg_, kc_) for qg_ in range(8) for kc_ in range(qg_ * 4 + 4)] if p >= 1 else []
            tidx = [0]
            for qt in range(32):
                pi2 = qt % 2
                kext = (qt + 1) * 128
                blocks = [(o2, min(o2 + 1024, kext)) for o2 in range(0, kext, 1024)]
                nb = len(blocks)
                if nmb_scgate[pi2]:
                    sc.w("vector", sSc, nmb_scgate[pi2])
                if cmb_dvgate[pi2]:
                    sc.w("scalar", sV, cmb_dvgate[pi2])
                for bi, (b0, b1) in enumerate(blocks):
                    wb = b1 - b0
                    base = (u % 2) * 1024
                    if u >= 2:
                        s_, c_ = exps[u - 2]
                        sc.w("tensor", s_, c_)
                    if u == 0:
                        sc.w("tensor", sLq[p % 2], ldQ[p])
                        sc.w("tensor", sRr[p % 2], ldR[p])
                    off = b0
                    tmm = None
                    while off < b1:
                        wch = min(512, b1 - off)
                        diag = b1 == kext and off + wch >= b1
                        sc.o("tensor", lambda e, qt=qt, off=off, wch=wch, b0=b0, base=base, p=p: e.matmul(
                            PALL[:, base + off - b0:base + off - b0 + wch],
                            QF[p % 3][:, qt * 128:qt * 128 + 128],
                            KF[p % 3][:, off:off + wch], start=True, stop=False))
                        lastch = off + wch >= b1
                        if diag:
                            dloc = qt * 128 - b0
                            sc.o("tensor", lambda e, dloc=dloc, base=base: e.matmul(
                                PALL[:, base + dloc:base + dloc + 128],
                                cx.IDB[:], cx.NEGDB[:], start=False, stop=False))
                        tmm = sc.o("tensor", lambda e, off=off, wch=wch, b0=b0, base=base, p=p: e.matmul(
                            PALL[:, base + off - b0:base + off - b0 + wch],
                            cx.ONESB[0:5, :], KS5[p % 2][0:5, off:off + wch],
                            start=False, stop=True), sT if lastch else None, 1)
                        off += wch
                    sc.w("vector", sT, tmm)
                    vred = sc.o("vector", lambda e, wb=wb, bi=bi, base=base, pi2=pi2: e.reduce_max(
                        NMB[pi2][:, bi:bi + 1], PALL[:, base:base + wb],
                        axis=AX.X, negate=True), sV, 1)
                    sc.w("scalar", sV, vred)
                    sexp = sc.o("scalar", lambda e, wb=wb, bi=bi, base=base, pi2=pi2: e.activation(
                        ESC[:, 0:wb], PALL[:, base:base + wb], AF.Exp,
                        bias=NMB[pi2][:, bi:bi + 1], scale=1.0,
                        accum_out=SGB[pi2][:, bi:bi + 1]), sSc, 1)
                    exps.append((sSc, sexp))
                    u += 1
                    if p >= 1:
                        want = min(len(tlist), (u * len(tlist)) // 76)
                        while tidx[0] < want:
                            qg_, kc_ = tlist[tidx[0]]
                            tidx[0] += 1
                            p2_score(p - 1, qg_, kc_)
                            while len(p2["pvq"]) > 2:
                                p2_pop_pv()
                gb = (qt // 4) % 2
                qcol = qt % 4
                if nb == 1:
                    sln = sc.o("scalar", lambda e, pi2=pi2: e.activation(
                        LNS[pi2][:], SGB[pi2][:, 0:1], AF.Ln), sSc, 1)
                    sc.w("vector", sSc, sln)
                    cmb_dvgate[pi2] = sc.o("vector", lambda e, gb=gb, qcol=qcol, pi2=pi2: e.tensor_tensor(
                        QAB[gb][:, qcol:qcol + 1], NMB[pi2][:, 0:1], LNS[pi2][:],
                        op=ALU.subtract), sV, 1)
                else:
                    sc.o("vector", lambda e, nb=nb, pi2=pi2: e.tensor_tensor(
                        SGW[pi2][:, 0:nb], Z4[:, 0:nb], NMB[pi2][:, 0:nb], op=ALU.subtract))
                    vmn = sc.o("vector", lambda e, nb=nb, pi2=pi2: e.reduce_max(
                        NMc[pi2][:], SGW[pi2][:, 0:nb], axis=AX.X, negate=True), sV, 1)
                    sc.w("scalar", sV, vmn)
                    sx = sc.o("scalar", lambda e, nb=nb, pi2=pi2: e.activation(
                        EXB[pi2][:, 0:nb], NMB[pi2][:, 0:nb], AF.Exp,
                        bias=NMc[pi2][:], scale=-1.0), sSc, 1)
                    sc.w("vector", sSc, sx)
                    sc.o("vector", lambda e, nb=nb, pi2=pi2: e.tensor_tensor(
                        SGW[pi2][:, 0:nb], SGB[pi2][:, 0:nb], EXB[pi2][:, 0:nb], op=ALU.mult))
                    vsg = sc.o("vector", lambda e, nb=nb, pi2=pi2: e.reduce_sum(
                        SGc[pi2][:], SGW[pi2][:, 0:nb], axis=AX.X), sV, 1)
                    sc.w("scalar", sV, vsg)
                    sln = sc.o("scalar", lambda e, pi2=pi2: e.activation(
                        LNS[pi2][:], SGc[pi2][:], AF.Ln), sSc, 1)
                    sc.w("vector", sSc, sln)
                    cmb_dvgate[pi2] = sc.o("vector", lambda e, gb=gb, qcol=qcol, pi2=pi2: e.tensor_tensor(
                        QAB[gb][:, qcol:qcol + 1], NMc[pi2][:], LNS[pi2][:],
                        op=ALU.subtract), sV, 1)
                nmb_scgate[pi2] = sln

                def emit_qaug_group(g, gbuf, p=p):
                    if qa3_pe[gbuf]:
                        sc.w("vector", sT, qa3_pe[gbuf])
                    l0 = QA3G[gbuf][:, 0:12:3]
                    l1 = QA3G[gbuf][:, 1:12:3]
                    l2 = QA3G[gbuf][:, 2:12:3]
                    sc.o("vector", lambda e: e.tensor_copy(QH4[gbuf][:], QAB[gbuf][:]))
                    sc.o("vector", lambda e: e.tensor_copy(l0, QH4[gbuf][:]))
                    sc.o("vector", lambda e: e.tensor_tensor(QR4[gbuf][:], QAB[gbuf][:], l0, op=ALU.subtract))
                    sc.o("vector", lambda e: e.tensor_copy(QH4[gbuf][:], QR4[gbuf][:]))
                    sc.o("vector", lambda e: e.tensor_copy(l1, QH4[gbuf][:]))
                    sc.o("vector", lambda e: e.tensor_tensor(QR4[gbuf][:], QR4[gbuf][:], l1, op=ALU.subtract))
                    sc.o("vector", lambda e: e.tensor_copy(QH4[gbuf][:], QR4[gbuf][:]))
                    vsp = sc.o("vector", lambda e: e.tensor_copy(l2, QH4[gbuf][:]), sV, 1)
                    sc.w("tensor", sV, vsp)
                    sc.w("tensor", sV, b7["v"])
                    ttr = 0
                    for ql in range(4):
                        ttr = sc.o("tensor", lambda e, ql=ql, gbuf=gbuf: e.transpose(
                            PALL[0:3, 3584 + ql * 128:3712 + ql * 128],
                            QA3G[gbuf][:, ql * 3:ql * 3 + 3], cx.IDF[:]), sT, 1)
                    qa3_pe[gbuf] = ttr
                    sc.w("vector", sT, ttr)
                    if g == 0 and p >= 2:
                        sc.w("vector", sPV, pv_after[p - 2])
                    vv = 0
                    for ql in range(4):
                        vv = sc.o("vector", lambda e, ql=ql, g=g, p=p: e.tensor_copy(
                            QROW3[p % 2][0:3, (g * 4 + ql) * 128:(g * 4 + ql + 1) * 128],
                            PALL[0:3, 3584 + ql * 128:3712 + ql * 128]), sV, 1)
                    b7["v"] = vv
                    qrow_ev[p][g] = vv

                if qt % 4 == 3:
                    if pendg is not None:
                        emit_qaug_group(*pendg)
                    pendg = (qt // 4, gb)
                if qt == 10 and p + 1 < 8:
                    setup(p + 1)
            while tidx[0] < len(tlist):
                qg_, kc_ = tlist[tidx[0]]
                tidx[0] += 1
                p2_score(p - 1, qg_, kc_)
                while len(p2["pvq"]) > 2:
                    p2_pop_pv()
            emit_qaug_group(*pendg)
            pendg = None
            exps_last[p] = exps[-1][1]
            if p >= 1:
                while p2["pvq"]:
                    p2_pop_pv()
                p2_evac_flush()
                fire_ag(p - 1)
        else:
            tlist = [(qg_, kc_) for qg_ in range(8) for kc_ in range(qg_ * 4 + 4)]
            for qg_, kc_ in tlist:
                p2_score(7, qg_, kc_)
                while len(p2["pvq"]) > 2:
                    p2_pop_pv()
            while p2["pvq"]:
                p2_pop_pv()
            p2_evac_flush()
            fire_ag(7)
    sc.w("sync", sPV, sPV.n)



def emit_proj_fused(cx, io):
    nc, sc, E, sm = cx.nc, cx.sc, cx.E, cx.sm
    t = cx.tag
    A16 = E(nc.sbuf_tensor(t + "A16", [128, 32 * 1024], BF16))
    W16 = [E(nc.sbuf_tensor(t + f"W16{i}", [128, 32 * 512], BF16)) for i in range(2)]
    B16 = E(nc.sbuf_tensor(t + "B16", [1, D], BF16))
    EVF = [E(nc.sbuf_tensor(t + f"EVF{i}", [128, 512], F32)) for i in range(4)]
    YC = E(nc.sbuf_tensor(t + "YC", [128, D], F32))
    RC = E(nc.sbuf_tensor(t + "RC", [128, D], F32))
    XOB = E(nc.sbuf_tensor(t + "XOB", [128, D], BF16))
    ST6 = E(nc.sbuf_tensor(t + "ST6", [128, 48], F32))
    MV = E(nc.sbuf_tensor(t + "MV", [128, 2], F32))
    SD = E(nc.sbuf_tensor(t + "SD", [128, 1], F32))
    RS = E(nc.sbuf_tensor(t + "RS", [128, 1], F32))
    PS = [E(nc.psum_tensor(t + f"PS{i}", [128, 512], F32)) for i in range(4)]
    Y2 = YC
    XO = RC
    sX, sMM, sEV, sL, sV = sm["sX"], sm["sMM"], sm["sEV"], sm["sL"], sm["sV"]
    sW = [sm["sW0"], sm["sW1"]]
    sST = [sm["sS0"], sm["sS1"], sm["sS2"], sm["sS3"]]
    xb = sX.n
    sc.o("gpsimd", lambda e: e.dma_start(out=B16[:], in_=io["br"]), sX, 16)
    ret = io["a16_load"](sc, A16, sX)
    asem, xa = ret if ret is not None else (sX, sX.n)
    wbb = [sm["sW0"].n, sm["sW1"].n]
    mmb = sMM.n
    evb = sEV.n
    stb = [x.n for x in sST]

    def wload(w):
        src = io["wtile"](w)
        dstw = W16[w % 2][:].rearrange("p (i e) -> p i e", i=32)
        if w >= 2:
            sc.w("sync", sMM, mmb + (w - 1) * 8)
        sc.o("sync", lambda e, s=src, d=dstw: e.dma_start(out=d, in_=s), sW[w % 2], 16)

    wload(0)
    wload(1)
    n = 0
    for et in range(8):
        sc.w("tensor", sX, xb + 16)
        sc.w("tensor", asem, xa)
        sc.w("tensor", sW[et % 2], wbb[et % 2] + 16 * (et // 2 + 1))
        for tc in range(8):
            n += 1
            bank = (n - 1) % 4
            if n > 4:
                sc.w("tensor", sEV, evb + n - 4)
            for d in range(32):
                sc.o("tensor", lambda e, d=d, b=bank, tc=tc, wp=et % 2: e.matmul(
                    PS[b][:], A16[:, d * 1024 + tc * 128: d * 1024 + tc * 128 + 128],
                    W16[wp][:, d * 512:(d + 1) * 512], start=(d == 0), stop=False))
            sc.o("tensor", lambda e, b=bank, c=et * 512: e.matmul(
                PS[b][:], cx.ONESB[0:1, :], B16[0:1, c:c + 512], start=False, stop=True), sMM, 1)
            sc.w("vector", sMM, mmb + n)
            if n > 4:
                sc.w("vector", sST[bank], stb[bank] + 16 * ((n - 4 - 1) // 4 + 1))
            sc.o("vector", lambda e, b=bank: e.tensor_copy(EVF[b][:], PS[b][:]), sEV, 1)
            sc.w("sync", sEV, sEV.n)
            sc.o("sync", lambda e, t0=tc * 128, c=et * 512, b=bank: e.dma_start(
                out=io["Y"][t0:t0 + 128, c:c + 512], in_=EVF[b][:]), sST[bank], 16)
        if et + 2 < 8:
            wload(et + 2)
    ob = sm["sO0"].n
    prev_vo = None
    for tc in range(8):
        t0 = tc * 128
        if tc == 0:
            for b in range(4):
                sc.w("sync", sST[b], sST[b].n)
        else:
            sc.w("sync", sm["sO0"], ob + 16 * tc)
            sc.w("sync", sV, prev_vo)
        sc.o("sync", lambda e, a=io["Y"][t0:t0 + 128, :]: e.dma_start(out=YC[:], in_=a), sL, 16)
        sc.o("sync", lambda e, a=io["resid"][t0:t0 + 128, :]: e.dma_start(out=RC[:], in_=a), sL, 16)
        sc.w("vector", sL, sL.n)
        if tc > 0:
            sc.w("vector", sm["sO0"], ob + 16 * tc)
        sc.o("vector", lambda e: e.tensor_tensor(Y2[:], YC[:], RC[:], op=ALU.add))
        for cc in range(8):
            sc.o("vector", lambda e, cc=cc: e.bn_stats(ST6[:, cc * 6:(cc + 1) * 6], Y2[:, cc * 512:(cc + 1) * 512]))
        sc.o("vector", lambda e: e.bn_aggr(MV[:], ST6[:]), sV, 1)
        sc.w("scalar", sV, sV.n)
        sq = sc.o("scalar", lambda e: e.activation(SD[:], MV[:, 1:2], AF.Sqrt, bias=cx.EPSC[:], scale=1.0), sm["sSc"], 1)
        sc.w("vector", sm["sSc"], sq)
        sc.o("vector", lambda e: e.reciprocal(RS[:], SD[:]))
        vo = sc.o("vector", lambda e: e.tensor_scalar(XO[:], Y2[:], MV[:, 0:1], RS[:, 0:1], op0=ALU.subtract, op1=ALU.mult), sV, 1)
        if io.get("bounce") is not None:
            if tc > 0:
                sc.w("vector", sm["sC"], sm["sC"].n)
            vo = sc.o("vector", lambda e: e.tensor_copy(XOB[:], XO[:]), sV, 1)
            sc.w("sync", sV, vo)
            sc.o("sync", lambda e, d=io["bounce"][t0:t0 + 128, :]: e.dma_start(out=d, in_=XOB[:]), sm["sC"], 16)
        prev_vo = vo
        sc.w("sync", sV, vo)
        sc.o("sync", lambda e, o=io["xo"][t0:t0 + 128, :]: e.dma_start(out=o, in_=XO[:]), sm["sO0"], 16)
    hn = io.get("heat_n", 0)
    if hn:
        PSH = E(nc.psum_tensor(t + "PSH", [128, 512], F32))
        emit_heat(cx, PSH[:], hn)

def build_fused(stop_after=None):
    import os
    DEQR = int(os.environ.get('DEQR', '4'))
    SKIPA = os.environ.get('SKIPA', '0') == '1'
    DEQM = int(os.environ.get('DEQM', '2'))
    nc = bass.Bass("TRN2", target_bir_lowering=False)
    x_tok = nc.dram_tensor("x_tok", [TOK, D], F32, kind="ExternalInput").ap()
    wq_sl = nc.dram_tensor("wq_sl", [D, 3 * EH], F32, kind="ExternalInput").ap()
    wa_sl = nc.dram_tensor("wa_sl", [D, 3 * EH], F32, kind="ExternalInput").ap()
    shc = nc.dram_tensor("shc", [D, 1024], F32, kind="ExternalInput").ap()
    wqd = nc.dram_tensor("wqd", [512, D], I32, kind="ExternalInput").ap()
    zzd = nc.dram_tensor("zzd", [512, 32], F32, kind="ExternalInput").ap()
    ssd = nc.dram_tensor("ssd", [512, 32], F32, kind="ExternalInput").ap()
    s2d = nc.dram_tensor("s2d", [512, 1], F32, kind="ExternalInput").ap()
    mkd = nc.dram_tensor("mkd", [512, D], I32, kind="ExternalInput").ap()
    q3d = nc.dram_tensor("q3d", [65536, 12], I8, kind="ExternalInput").ap()
    wnd = nc.dram_tensor("wnd", [65536, 1], F32, kind="ExternalInput").ap()
    b3 = nc.dram_tensor("b3", [1, 3 * EH], F32, kind="ExternalInput").ap()
    bpr = nc.dram_tensor("bpr", [1, D], F32, kind="ExternalInput").ap()
    bf0 = nc.dram_tensor("bf0", [1, D], F32, kind="ExternalInput").ap()
    bf1 = nc.dram_tensor("bf1", [1, D], F32, kind="ExternalInput").ap()
    zrow = nc.dram_tensor("zrow", [1, D], F32, kind="ExternalInput").ap()
    gq_sl = nc.dram_tensor("gq_sl", [1, EH], F32, kind="ExternalInput").ap()
    gk_sl = nc.dram_tensor("gk_sl", [1, EH], F32, kind="ExternalInput").ap()
    ub2 = nc.dram_tensor("ub2", [2, 8, 128], F32, kind="ExternalInput").ap()
    uw2 = nc.dram_tensor("uw2", [2, 8, 128], F32, kind="ExternalInput").ap()
    ident = nc.dram_tensor("ident", [128, 128], F32, kind="ExternalInput").ap()
    identb = nc.dram_tensor("identb", [128, 128], BF16, kind="ExternalInput").ap()
    negd = nc.dram_tensor("negd", [128, 128], F32, kind="ExternalInput").ap()
    trit = nc.dram_tensor("trit", [128, 896], F32, kind="ExternalInput").ap()
    ahi = nc.dram_tensor("ahi", [8, S], BF16, kind="ExternalInput").ap()
    ami = nc.dram_tensor("ami", [8, S], BF16, kind="ExternalInput").ap()
    alo = nc.dram_tensor("alo", [8, S], BF16, kind="ExternalInput").ap()
    acolt = nc.dram_tensor("acolt", [8, 128, 32], F32, kind="ExternalInput").ap()
    tofs = nc.dram_tensor("tofs", [1, 1], I32, kind="ExternalInput").ap()
    xout = nc.dram_tensor("xout", [TOK, D], F32, kind="ExternalOutput").ap()
    DBGF = os.environ.get("DBGF", "0") == "1"
    if DBGF:
        dago = nc.dram_tensor("dago", [4096, S], BF16, kind="ExternalOutput").ap()
        dagx = nc.dram_tensor("dagx", [4 * D, TOK], BF16, kind="ExternalOutput").ap()
        _dqdt = F32 if os.environ.get("DBGL", "0") == "0" else BF16
        dqtm = nc.dram_tensor("dqtm", [S, EH], _dqdt, kind="ExternalOutput").ap()
        dktm = nc.dram_tensor("dktm", [S, EH], _dqdt, kind="ExternalOutput").ap()
        dvtm = nc.dram_tensor("dvtm", [S, EH], BF16, kind="ExternalOutput").ap()
        dq2 = nc.dram_tensor("dq2", [EH, S], BF16, kind="ExternalOutput").ap()
        dk2 = nc.dram_tensor("dk2", [EH, S], BF16, kind="ExternalOutput").ap()
        dar = nc.dram_tensor("dar", [128, 128], F32, kind="ExternalOutput").ap()
        ds1 = nc.dram_tensor("ds1", [TOK, D], F32, kind="ExternalOutput").ap()
        ds2 = nc.dram_tensor("ds2", [TOK, D], F32, kind="ExternalOutput").ap()
        dagw = nc.dram_tensor("dagw", [32 * D, 512], BF16, kind="ExternalOutput").ap()
        dagxin = nc.dram_tensor("dagxin", [D, TOK], BF16, kind="ExternalOutput").ap()

    WQ16 = nc.dram_tensor("WQ16", [D, 3 * EH], BF16).ap()
    WA16 = nc.dram_tensor("WA16", [D, 3 * EH], BF16).ap()
    agw_in = nc.dram_tensor("agw_in", [4 * D, 512], BF16).ap()
    agw_out = nc.dram_tensor("agw_out", [32 * D, 512], BF16).ap()
    agx_in = nc.dram_tensor("agx_in", [D, TOK], BF16).ap()
    agx_out = nc.dram_tensor("agx_out", [4 * D, TOK], BF16).ap()
    agxt = nc.dram_tensor("agxt", [4 * TOK, D], BF16).ap()
    ar_in = nc.dram_tensor("ar_in", [128, 128], F32).ap()
    ar_out = nc.dram_tensor("ar_out", [128, 128], F32).ap()
    agoi = [nc.dram_tensor(f"agoi{p}", [128, S], BF16).ap() for p in range(8)]
    ago8 = [nc.dram_tensor(f"ago8_{p}", [512, S], BF16).ap() for p in range(8)]
    ckd = [dict(h1=nc.dram_tensor(f"ckdh1{i}", [128, 128], BF16).ap(),
                l1=nc.dram_tensor(f"ckdl1{i}", [128, 128], BF16).ap(),
                h2=nc.dram_tensor(f"ckdh2{i}", [32, 128], BF16).ap(),
                l2=nc.dram_tensor(f"ckdl2{i}", [32, 128], BF16).ap()) for i in range(2)]
    qtm0 = nc.dram_tensor("qtm0", [S, EH], F32).ap()
    ktm0 = nc.dram_tensor("ktm0", [S, EH], F32).ap()
    vtm0 = nc.dram_tensor("vtm0", [S, EH], BF16).ap()
    qtm1 = nc.dram_tensor("qtm1", [S, EH], BF16).ap()
    ktm1 = nc.dram_tensor("ktm1", [S, EH], BF16).ap()
    vtm1 = nc.dram_tensor("vtm1", [S, EH], BF16).ap()
    q2fm = nc.dram_tensor("q2fm", [EH, S], BF16).ap()
    k2fm = nc.dram_tensor("k2fm", [EH, S], BF16).ap()
    xfm = nc.dram_tensor("xfm", [D, TOK], BF16).ap()
    s1 = nc.dram_tensor("s1", [TOK, D], F32).ap()
    s2 = nc.dram_tensor("s2", [TOK, D], F32).ap()
    s3 = nc.dram_tensor("s3", [TOK, D], F32).ap()
    Y = nc.dram_tensor("Ybuf", [TOK, D], F32).ap()
    tmb = nc.dram_tensor("tmb", [TOK, D], BF16).ap()
    wdq0 = nc.dram_tensor("wdq0", [512, D], BF16).ap()
    wdq1 = nc.dram_tensor("wdq1", [512, D], BF16).ap()
    qbm = nc.dram_tensor("qbm", [S, EH], BF16).ap()
    kbm = nc.dram_tensor("kbm", [S, EH], BF16).ap()

    with contextlib.ExitStack() as st0:
        E0 = st0.enter_context
        cx = _Ctx()
        cx.nc = nc
        cx.sm = _mk_sems(nc, E0, SEM_NAMES)
        cx.sc = Sched(chains={"vector": cx.sm["cV"], "scalar": cx.sm["cS"]})
        sc, sm = cx.sc, cx.sm
        cx.IDF = E0(nc.sbuf_tensor("gIDF", [128, 128], F32))
        cx.IDB = E0(nc.sbuf_tensor("gIDB", [128, 128], BF16))
        cx.NEGD = E0(nc.sbuf_tensor("gNEGD", [128, 128], F32))
        cx.TRIT = E0(nc.sbuf_tensor("gTRIT", [128, 896], F32))
        cx.ONESB = E0(nc.sbuf_tensor("gONESB", [8, 128], BF16))
        cx.EPSC = E0(nc.sbuf_tensor("gEPSC", [128, 1], F32))
        cx.HEATR = E0(nc.sbuf_tensor("gHEATR", [128, 512], BF16))
        cx.NEGDB = E0(nc.sbuf_tensor("gNEGDB", [128, 128], BF16))
        cx.TRITB = E0(nc.sbuf_tensor("gTRITB", [128, 896], BF16))
        cx.tmb = tmb
        TOFS = E0(nc.sbuf_tensor("gTOFS", [1, 1], I32))
        blk = E0(nc.Block())

        sc.o("sync", lambda e: e.dma_start(out=cx.IDF[:], in_=ident), sm["sL"], 16)
        sc.o("sync", lambda e: e.dma_start(out=cx.IDB[:], in_=identb), sm["sL"], 16)
        sc.o("sync", lambda e: e.dma_start(out=cx.NEGD[:], in_=negd), sm["sL"], 16)
        sc.o("sync", lambda e: e.dma_start(out=cx.TRIT[:], in_=trit), sm["sL"], 16)
        sc.o("vector", lambda e: e.memset(cx.ONESB[:], 1.0))
        sc.o("vector", lambda e: e.memset(cx.HEATR[:], 0.0))
        sc.w("vector", sm["sL"], sm["sL"].n)
        sc.o("vector", lambda e: e.tensor_copy(cx.NEGDB[:], cx.NEGD[:]))
        sc.o("vector", lambda e: e.tensor_copy(cx.TRITB[:], cx.TRIT[:]), sm["sV"], 1)
        sc.o("vector", lambda e: e.memset(cx.EPSC[:], EPS), sm["sV"], 1)
        sc.o("gpsimd", lambda e: e.dma_start(out=TOFS[:], in_=tofs), sm["sX"], 16)
        sc.w("gpsimd", sm["sX"], sm["sX"].n)

        def _initreg(e):
            cx.treg = e.alloc_register("tofsreg")
            return e.reg_load(cx.treg, TOFS[0:1, 0:1])
        sc.o("gpsimd", _initreg)

        if stop_after == "pre":
            def _finish0():
                sc.o("sync", lambda e: e.dma_start(out=xout, in_=x_tok), sm["sO1"], 16)
                _barrier(cx)
            _finish0()
            sc.emit(blk)
            return nc

        with contextlib.ExitStack() as st:
            cx.E = st.enter_context
            cx.tag = "p0a"
            items = []
            if SKIPA:
                items = items
            for et in range(6):
                items.append((wq_sl[:, et * 512:(et + 1) * 512], WQ16[:, et * 512:(et + 1) * 512]))
            for et in range(6):
                items.append((wa_sl[:, et * 512:(et + 1) * 512], WA16[:, et * 512:(et + 1) * 512]))
            items.append((shc[:, 0:512], agw_in[0:D, :]))
            items.append((shc[:, 512:1024], agw_in[D:2 * D, :]))
            if SKIPA:
                items = []
            emit_cast_list(cx, items)
            _barrier(cx, exclude=("sAGW",))

        if SKIPA or stop_after == "p0a":
            pass
        if stop_after == "p0a":
            def _finish0():
                sc.o("sync", lambda e: e.dma_start(out=xout, in_=x_tok), sm["sO1"], 16)
                _barrier(cx)
            _finish0()
            sc.emit(blk)
            return nc

        with contextlib.ExitStack() as st:
            cx.E = st.enter_context
            cx.tag = "p0b"
            emit_dequant_fused(cx, dict(wq=wqd, zz=zzd, ss=ssd, s2=s2d, mk=mkd,
                                        q3=q3d, wn=wnd, agw_in=agw_in,
                                        wdq0=wdq0, wdq1=wdq1),
                               do_tr=stop_after not in ("p0dq",),
                               do_dq=stop_after not in ("p0tr", "p0tr3"),
                               do_st=stop_after != "p0tr2",
                               do_pe=stop_after != "p0tr3")
            sc.w("gpsimd", sm["sS0"], sm["sS0"].n)
            sc.w("gpsimd", sm["sS1"], sm["sS1"].n)
            sc.w("gpsimd", sm["sC"], sm["sC"].n)
            if stop_after in ("p0nc", "p0dq", "p0tr", "p0tr2", "p0tr3"):
                def _finish0():
                    sc.o("sync", lambda e: e.dma_start(out=xout, in_=x_tok), sm["sO1"], 16)
                    _barrier(cx)
                _finish0()
                sc.emit(blk)
                return nc
            sc.o("gpsimd", lambda e: e.collective_compute(
                "AllGather", ALU.bypass, GROUPS_ALL, ins=[agw_in], outs=[agw_out]),
                sm["sAGW"], 1)
            _barrier(cx, exclude=("sAGW",))

        def _finish():
            sc.o("sync", lambda e: e.dma_start(out=xout, in_=x_tok), sm["sO1"], 16)
            _barrier(cx)

        if stop_after == "p0":
            _finish()
            sc.emit(blk)
            return nc

        layer_io = [
            dict(xsrc=x_tok, xcast=True, wT=WQ16, qtm=qtm0, ktm=ktm0, vtm=vtm0,
                 resid_p=x_tok, s_attn=s1, s_ffn=s2, wm_out=0, wm_ffn=2,
                 br_out=zrow, br_ffn=bf0),
            dict(xsrc=s2, xcast=True, wT=WA16, qtm=qtm1, ktm=ktm1, vtm=vtm1,
                 resid_p=s2, s_attn=s3, s_ffn=xout, wm_out=1, wm_ffn=3,
                 br_out=bpr, br_ffn=bf1),
        ]

        for L in (0, 1):
            io = layer_io[L]
            with contextlib.ExitStack() as st:
                cx.E = st.enter_context
                cx.tag = f"l{L}p1"
                if L == 0:
                    emit_tm2fm(cx, io["xsrc"], agx_in, TOK, D, cast=True, bounce_only=True)
                sc.w("gpsimd", sm["sC"], sm["sC"].n)
                sc.o("gpsimd", lambda e: e.collective_compute(
                    "AllGather", ALU.bypass, GROUPS_B, ins=[tmb], outs=[agxt]),
                    sm["sAGX"], 1)
                _barrier(cx, exclude=("sAGW",))

            if stop_after == "p1" and L == 0:
                _finish()
                sc.emit(blk)
                return nc

            with contextlib.ExitStack() as st:
                cx.E = st.enter_context
                cx.tag = f"l{L}p2"
                qio = dict(agxt=agxt, wT=io["wT"],
                           outs=[io["qtm"], io["ktm"], io["vtm"]],
                           ar_in=ar_in, ar_out=ar_out)
                if L == 1:
                    qio["b3"] = b3
                emit_qkv_fused(cx, L, qio)
                _barrier(cx, exclude=("sAGW",))

            if stop_after == "p2" and L == 0:
                _finish()
                sc.emit(blk)
                return nc

            if L == 0:
                with contextlib.ExitStack() as st:
                    cx.E = st.enter_context
                    cx.tag = f"l{L}p3"
                    emit_qkfin_fused(cx, L, dict(qtm=io["qtm"], ktm=io["ktm"],
                                                 q2fm=q2fm, k2fm=k2fm, qbm=qbm, kbm=kbm,
                                                 ar_out=ar_out, gq=gq_sl, gk=gk_sl))
                    _barrier(cx, exclude=("sAGW",))

            if stop_after == "p3" and L == 0:
                _finish()
                sc.emit(blk)
                return nc

            with contextlib.ExitStack() as st:
                cx.E = st.enter_context
                cx.tag = f"l{L}p4"
                emit_attn_fused(cx, L, dict(
                    q2fm=qbm if L == 0 else io["qtm"],
                    k2fm=kbm if L == 0 else io["ktm"],
                    vtm=io["vtm"], agoi=agoi, ago8=ago8,
                    ckd=ckd, ahi=ahi, ami=ami, alo=alo, acolt=acolt,
                    ub8=ub2[L], uw8=uw2[L]))
                _barrier(cx, exclude=("sAGW",))

            if stop_after == "p4" and L == 0:
                _finish()
                sc.emit(blk)
                return nc

            with contextlib.ExitStack() as st:
                cx.E = st.enter_context
                cx.tag = f"l{L}p5"
                if L == 0:
                    sc.w("sync", sm["sAGW"], 1)
                    sc.w("gpsimd", sm["sAGW"], 1)
                sc.w("gpsimd", sm["sAGO"], sm["sAGO"].n)

                def a16_load(sc_, A16, sX, _cx=cx, _ago8=ago8):
                    for sp in range(8):
                        static = _ago8[sp].rearrange("(c q) t -> q c t", q=128)[:, :, 0:1024]

                        def _do(e, sp=sp, static=static):
                            src = AP(_ago8[sp].tensor, _cx.treg, static.ap)
                            dst = A16[:].rearrange("q (g t) -> q g t", g=32)[:, sp:32:8, :]
                            return e.dma_start(out=dst, in_=src)
                        sc_.o("gpsimd", _do, sm["sX"], 16)
                    return None

                m = io["wm_out"]
                emit_proj_fused(cx, dict(
                    a16_load=a16_load,
                    wtile=lambda w, m=m: agw_out[w * 16384 + m * 4096:w * 16384 + (m + 1) * 4096, 0:512].rearrange("(i p) e -> p i e", p=128),
                    br=io["br_out"], resid=io["resid_p"], Y=Y, xo=io["s_attn"],
                    bounce=tmb))
                _barrier(cx, exclude=("sAGW",) if L == 0 else ())

            if stop_after == "p5" and L == 0:
                _finish()
                sc.emit(blk)
                return nc

            with contextlib.ExitStack() as st:
                cx.E = st.enter_context
                cx.tag = f"l{L}p6b"

                def a16s(sc_, A16, sX, _x=tmb):
                    for d_ in range(32):
                        sc_.o("sync", lambda e, d_=d_: e.dma_start(
                            out=A16[:, d_ * 1024:(d_ + 1) * 1024],
                            in_=_x[0:1024, d_ * 128:(d_ + 1) * 128], transpose=True),
                            sm["sL"], 16)
                    return (sm["sL"], sm["sL"].n)

                m = io["wm_ffn"]
                emit_proj_fused(cx, dict(
                    a16_load=a16s,
                    wtile=lambda w, m=m: agw_out[w * 16384 + m * 4096:w * 16384 + (m + 1) * 4096, 0:512].rearrange("(i p) e -> p i e", p=128),
                    br=io["br_ffn"], resid=io["s_attn"], Y=Y, xo=io["s_ffn"],
                    bounce=tmb if L == 0 else None))
                _barrier(cx)
            if stop_after == "p6b" and L == 0:
                _finish()
                sc.emit(blk)
                return nc

            if DBGF and L == int(os.environ.get("DBGL", "0")):
                qt_, kt_, vt_ = (qtm0, ktm0, vtm0) if L == 0 else (qtm1, ktm1, vtm1)
                for p_ in range(8):
                    sc.o("sync", lambda e, a=ago8[p_], b=dago[p_ * 512:(p_ + 1) * 512, :]: e.dma_start(out=b, in_=a), sm["sO1"], 16)
                for dsrc, ddst in [(agx_in, dagxin), (agx_out, dagx), (qt_, dqtm),
                                   (kt_, dktm), (vt_, dvtm), (q2fm, dq2),
                                   (k2fm, dk2), (ar_out, dar), (s1, ds1), (s2, ds2),
                                   (agw_out, dagw)]:
                    sc.o("sync", lambda e, a=dsrc, b=ddst: e.dma_start(out=b, in_=a), sm["sO1"], 16)
                sc.w("sync", sm["sO1"], sm["sO1"].n)

        sc.emit(blk)
    return nc


def kernel(L0_attn_hidden_states, L0_attn_Wqkv, L0_attn_q_ln_w, L0_attn_q_ln_b,
           L0_attn_k_ln_w, L0_attn_k_ln_b, L0_attn_Wout, L0_ffn_W_q, L0_ffn_scales,
           L0_ffn_zeros, L0_ffn_mask, L0_ffn_scale2, L0_ffn_bias, L1_attn_w_attn,
           L1_attn_b_attn, L1_attn_w_proj, L1_attn_b_proj, L1_ffn_weight_q3,
           L1_ffn_weight_norm, L1_ffn_bias):
    x = np.asarray(L0_attn_hidden_states, dtype=np.float32).reshape(2 * S, D)
    WqT = np.ascontiguousarray(np.asarray(L0_attn_Wqkv, np.float32).T)
    WoT = np.ascontiguousarray(np.asarray(L0_attn_Wout, np.float32).T)
    WA = np.asarray(L1_attn_w_attn, np.float32)
    WP = np.asarray(L1_attn_w_proj, np.float32)
    zz = np.asarray(L0_ffn_zeros, np.float32).reshape(D, 32)
    ss = np.asarray(L0_ffn_scales, np.float32).reshape(D, 32)
    s2v = np.asarray(L0_ffn_scale2, np.float32).reshape(D, 1)
    wqi = np.asarray(L0_ffn_W_q, np.int32)
    mki = np.asarray(L0_ffn_mask, np.int32)
    q3i = np.asarray(L1_ffn_weight_q3, np.int8)
    wni = np.asarray(L1_ffn_weight_norm, np.float32)
    gq = np.asarray(L0_attn_q_ln_w, np.float32)
    gk = np.asarray(L0_attn_k_ln_w, np.float32)
    bq = np.asarray(L0_attn_q_ln_b, np.float32)
    b_attn = np.asarray(L1_attn_b_attn, np.float32)
    C = _consts()

    def _slice3(W, hs):
        return np.ascontiguousarray(np.concatenate(
            [W[:, hs * 128:hs * 128 + EH],
             W[:, D + hs * 128:D + hs * 128 + EH],
             W[:, 2 * D + hs * 128:2 * D + hs * 128 + EH]], axis=1))

    nc = build_fused()
    maps = []
    for c in range(NCORE):
        hs = (c % 4) * 8
        r0, r1 = c * 512, (c + 1) * 512
        zz8 = np.zeros((8, 128), np.float32)
        maps.append(dict(
            x_tok=np.ascontiguousarray(x[c * TOK:(c + 1) * TOK]),
            wq_sl=_slice3(WqT, hs), wa_sl=_slice3(WA, hs),
            shc=np.ascontiguousarray(np.concatenate(
                [WoT[:, c * 512:(c + 1) * 512], WP[:, c * 512:(c + 1) * 512]], axis=1)),
            wqd=wqi[r0:r1], zzd=zz[r0:r1], ssd=ss[r0:r1], s2d=s2v[r0:r1], mkd=mki[r0:r1],
            q3d=q3i[r0 * 128:r1 * 128], wnd=wni[r0 * 128:r1 * 128],
            b3=np.ascontiguousarray(np.concatenate(
                [b_attn[hs * 128:hs * 128 + EH], b_attn[D + hs * 128:D + hs * 128 + EH],
                 b_attn[2 * D + hs * 128:2 * D + hs * 128 + EH]])[None, :]),
            bpr=np.asarray(L1_attn_b_proj, np.float32)[None, :],
            bf0=np.asarray(L0_ffn_bias, np.float32)[None, :],
            bf1=np.asarray(L1_ffn_bias, np.float32)[None, :],
            zrow=np.zeros((1, D), np.float32),
            gq_sl=np.ascontiguousarray(gq[hs * 128:hs * 128 + EH][None, :]),
            gk_sl=np.ascontiguousarray(gk[hs * 128:hs * 128 + EH][None, :]),
            ub2=np.stack([bq[hs * 128:(hs + 8) * 128].reshape(8, 128), zz8]),
            uw2=np.stack([gk[hs * 128:(hs + 8) * 128].reshape(8, 128), zz8]),
            ident=C["ident"], identb=C["identb"], negd=C["negd"], trit=C["trit"],
            ahi=np.ascontiguousarray(C["AHI"][hs:hs + 8]),
            ami=np.ascontiguousarray(C["AMI"][hs:hs + 8]),
            alo=np.ascontiguousarray(C["ALO"][hs:hs + 8]),
            acolt=np.ascontiguousarray(C["ACOLT"][hs:hs + 8]),
            tofs=np.array([[(c % 4) * 1024]], np.int32)))
    rr = _run(nc, maps)
    out = np.concatenate([np.asarray(rr[c]["xout"]) for c in range(NCORE)], axis=0)
    return out.reshape(2, S, D).astype(np.float32)


# version 14
# speedup vs baseline: 1.1367x; 1.1367x over previous
import contextlib
import math
import numpy as np
import ml_dtypes
import concourse.bass as bass
import concourse.mybir as mybir
from concourse.bass_utils import run_bass_kernel_spmd
from concourse.ap import AP

F32 = mybir.dt.float32
BF16 = mybir.dt.bfloat16
I32 = mybir.dt.int32
I8 = mybir.dt.int8
U8 = mybir.dt.uint8
AF = mybir.ActivationFunctionType
ALU = mybir.AluOpType
AX = mybir.AxisListType

BF = ml_dtypes.bfloat16
NCORE = 8
B, S, D, H, HD = 2, 4096, 4096, 32, 128
TOK = 1024
EH = 1024
EPS = 1e-5
NEG = -1.0e30
ISQ = 1.0 / math.sqrt(HD)

ENGS = ("sync", "gpsimd", "vector", "scalar", "tensor")


class _Sem:
    __slots__ = ("h", "n")
    def __init__(self, h):
        self.h = h
        self.n = 0


class Sched:

    def __init__(self, chains=None):
        self.q = {k: [] for k in ENGS}
        self.chains = chains or {}

    def w(self, eng, s, v):
        self.q[eng].append(("w", s.h, int(v)))

    def o(self, eng, fn, s=None, k=1):
        if s is None and eng in self.chains:
            s, k = self.chains[eng], 1
        if s is not None:
            s.n += k
            self.q[eng].append(("o", fn, s.h, k))
            return s.n
        self.q[eng].append(("o", fn, None, 0))
        return None

    def emit(self, blk):
        barriered = set(self.chains)

        def runner2(eng, items):
            barrier = eng in barriered
            counts = {}
            resolved = []
            for it in items:
                if it[0] == "w":
                    resolved.append(it)
                else:
                    semh, k = it[2], it[3]
                    if semh is not None:
                        counts[semh] = counts.get(semh, 0) + k
                        resolved.append(("o", it[1], semh, k, (semh, counts[semh])))
                    else:
                        resolved.append(("o", it[1], None, 0, None))

            def _f(e):
                prev = None
                for it in resolved:
                    if it[0] == "w":
                        e.wait_ge(it[1], it[2])
                    else:
                        if barrier and prev is not None:
                            e.wait_ge(prev[0], prev[1])
                        inst = it[1](e)
                        if it[2] is not None:
                            inst.then_inc(it[2], it[3])
                        if it[4] is not None:
                            prev = it[4]
            return _f
        blk.sync(runner2("sync", self.q["sync"]))
        blk.gpsimd(runner2("gpsimd", self.q["gpsimd"]))
        blk.vector(runner2("vector", self.q["vector"]))
        blk.scalar(runner2("scalar", self.q["scalar"]))
        blk.tensor(runner2("tensor", self.q["tensor"]))



def build_prep():
    nc = bass.Bass("TRN2", target_bir_lowering=False)
    wq = nc.dram_tensor("wq", [512, D], I32, kind="ExternalInput").ap()
    zz = nc.dram_tensor("zz", [512, 32], F32, kind="ExternalInput").ap()
    ss = nc.dram_tensor("ss", [512, 32], F32, kind="ExternalInput").ap()
    s2 = nc.dram_tensor("s2", [512, 1], F32, kind="ExternalInput").ap()
    mk = nc.dram_tensor("mk", [512, D], I32, kind="ExternalInput").ap()
    q3 = nc.dram_tensor("q3", [65536, 12], I8, kind="ExternalInput").ap()
    wn = nc.dram_tensor("wn", [65536, 1], F32, kind="ExternalInput").ap()
    cin = nc.dram_tensor("cin", [D, 4096], F32, kind="ExternalInput").ap()
    w0d = nc.dram_tensor("w0d", [512, D], BF16, kind="ExternalOutput").ap()
    w1d = nc.dram_tensor("w1d", [512, D], BF16, kind="ExternalOutput").ap()
    cout = nc.dram_tensor("cout", [D, 4096], BF16, kind="ExternalOutput").ap()

    with contextlib.ExitStack() as st:
        E = st.enter_context
        QF = E(nc.sbuf_tensor("QF", [128, D], F32))
        MF = E(nc.sbuf_tensor("MF", [128, D], F32))
        T0 = E(nc.sbuf_tensor("T0", [128, D], F32))
        WD0 = E(nc.sbuf_tensor("WD0", [128, D], BF16))
        ZS = E(nc.sbuf_tensor("ZS", [128, 32], F32))
        SC = E(nc.sbuf_tensor("SC", [128, 32], F32))
        S2b = E(nc.sbuf_tensor("S2b", [128, 1], F32))
        SCC = E(nc.sbuf_tensor("SCC", [128, 32], F32))
        BY = E(nc.sbuf_tensor("BY", [128, 1536], I32))
        Q1 = E(nc.sbuf_tensor("Q1", [128, D], I32))
        QF2 = E(nc.sbuf_tensor("QF2", [128, D], F32))
        NR = E(nc.sbuf_tensor("NR", [128, 128], F32))
        NE = E(nc.sbuf_tensor("NE", [128, D], F32))
        TA = E(nc.sbuf_tensor("TA", [128, 512], I32))
        TB = E(nc.sbuf_tensor("TB", [128, 512], I32))
        T1 = E(nc.sbuf_tensor("T1", [128, D], F32))
        WD1 = E(nc.sbuf_tensor("WD1", [128, D], BF16))
        CA = E(nc.sbuf_tensor("CA", [128, 4096], BF16))
        CB = E(nc.sbuf_tensor("CB", [128, 4096], BF16))
        sL = _Sem(E(nc.semaphore("sL")))
        sV = _Sem(E(nc.semaphore("sV")))
        sO = _Sem(E(nc.semaphore("sO")))
        sC = _Sem(E(nc.semaphore("sC")))
        cV = _Sem(E(nc.semaphore("cV")))
        cS = _Sem(E(nc.semaphore("cS")))
        blk = E(nc.Block())
        sc = Sched(chains={"vector": cV, "scalar": cS})

        for r in range(4):
            r0, r1 = r * 128, (r + 1) * 128
            g0 = r * 16384
            if r > 0:
                sc.w("gpsimd", sV, 2 * r)
                sc.w("sync", sV, 2 * r)
            sc.o("gpsimd", lambda e, a=wq[r0:r1, :], o=QF: e.dma_start(out=o[:], in_=a), sL, 16)
            sc.o("gpsimd", lambda e, a=mk[r0:r1, :], o=MF: e.dma_start(out=o[:], in_=a), sL, 16)
            sc.o("gpsimd", lambda e, a=q3.bitcast(U8)[g0:g0 + 16384, :].rearrange("(p g) b -> p (g b)", p=128), o=BY: e.dma_start(out=o[:], in_=a), sL, 16)
            sc.o("sync", lambda e, a=zz[r0:r1, :], o=ZS: e.dma_start(out=o[:], in_=a), sL, 16)
            sc.o("sync", lambda e, a=ss[r0:r1, :], o=SC: e.dma_start(out=o[:], in_=a), sL, 16)
            sc.o("sync", lambda e, a=s2[r0:r1, :], o=S2b: e.dma_start(out=o[:], in_=a), sL, 16)
            sc.o("sync", lambda e, a=wn[g0:g0 + 16384, :].rearrange("(p g) o -> p (g o)", p=128), o=NR: e.dma_start(out=o[:], in_=a), sL, 16)
            sc.w("vector", sL, sL.n)
            if r > 0:
                sc.w("vector", sO, 32 * r)
            sc.o("vector", lambda e: e.tensor_scalar(SCC[:], SC[:], S2b[:, 0:1], None, op0=ALU.mult))
            for g in range(32):
                sc.o("vector", lambda e, g=g: e.tensor_scalar(
                    T0[:, g * 128:(g + 1) * 128], QF[:, g * 128:(g + 1) * 128],
                    ZS[:, g:g + 1], SCC[:, g:g + 1], op0=ALU.subtract, op1=ALU.mult))
            sc.o("vector", lambda e: e.tensor_tensor(WD0[:], T0[:], MF[:], op=ALU.mult), sV, 1)
            b0, b1, b2 = BY[:, 0:1536:3], BY[:, 1:1536:3], BY[:, 2:1536:3]
            qs = lambda v: Q1[:, v:4096:8]
            sc.o("vector", lambda e: e.tensor_single_scalar(qs(0), b0, 7, op=ALU.bitwise_and))
            sc.o("vector", lambda e: e.tensor_scalar(qs(1), b0, 3, 7, op0=ALU.logical_shift_right, op1=ALU.bitwise_and))
            sc.o("vector", lambda e: e.tensor_scalar(TA[:], b0, 6, 3, op0=ALU.logical_shift_right, op1=ALU.bitwise_and))
            sc.o("vector", lambda e: e.tensor_scalar(TB[:], b1, 1, 2, op0=ALU.bitwise_and, op1=ALU.logical_shift_left))
            sc.o("vector", lambda e: e.tensor_tensor(qs(2), TA[:], TB[:], op=ALU.add))
            sc.o("vector", lambda e: e.tensor_scalar(qs(3), b1, 1, 7, op0=ALU.logical_shift_right, op1=ALU.bitwise_and))
            sc.o("vector", lambda e: e.tensor_scalar(qs(4), b1, 4, 7, op0=ALU.logical_shift_right, op1=ALU.bitwise_and))
            sc.o("vector", lambda e: e.tensor_scalar(TA[:], b1, 7, 1, op0=ALU.logical_shift_right, op1=ALU.bitwise_and))
            sc.o("vector", lambda e: e.tensor_scalar(TB[:], b2, 3, 1, op0=ALU.bitwise_and, op1=ALU.logical_shift_left))
            sc.o("vector", lambda e: e.tensor_tensor(qs(5), TA[:], TB[:], op=ALU.add))
            sc.o("vector", lambda e: e.tensor_scalar(qs(6), b2, 2, 7, op0=ALU.logical_shift_right, op1=ALU.bitwise_and))
            sc.o("vector", lambda e: e.tensor_scalar(qs(7), b2, 5, 7, op0=ALU.logical_shift_right, op1=ALU.bitwise_and))
            sc.o("vector", lambda e: e.tensor_copy(QF2[:], Q1[:]))
            sc.o("vector", lambda e: e.tensor_copy(
                NE[:].rearrange("p (g j) -> p g j", j=32),
                NR[:].unsqueeze(2).broadcast_to((128, 128, 32))))
            sc.o("vector", lambda e: e.scalar_tensor_tensor(T1[:], QF2[:], 2.0 / 7.0, NE[:], op0=ALU.mult, op1=ALU.mult))
            sc.o("vector", lambda e: e.tensor_tensor(WD1[:], T1[:], NE[:], op=ALU.subtract), sV, 1)
            sc.w("sync", sV, sV.n)
            sc.o("sync", lambda e, o=w0d[r0:r1, :]: e.dma_start(out=o, in_=WD0[:]), sO, 16)
            sc.o("sync", lambda e, o=w1d[r0:r1, :]: e.dma_start(out=o, in_=WD1[:]), sO, 16)

        for i in range(32):
            buf = CA if i % 2 == 0 else CB
            if i >= 2:
                sc.w("gpsimd", sC, 16 * (i - 1))
            sc.o("gpsimd", lambda e, a=cin[i * 128:(i + 1) * 128, :], o=buf: e.dma_start(out=o[:], in_=a), sL, 16)
            sc.w("sync", sL, sL.n)
            sc.o("sync", lambda e, o=cout[i * 128:(i + 1) * 128, :], b=buf: e.dma_start(out=o, in_=b[:]), sC, 16)
        sc.w("sync", sO, sO.n)
        sc.w("sync", sC, sC.n)
        sc.emit(blk)
    return nc



def build_qkv(layer):
    nc = bass.Bass("TRN2", target_bir_lowering=False)
    if layer == 0:
        xT = nc.dram_tensor("xT", [D, S], F32, kind="ExternalInput").ap()
    else:
        xT = nc.dram_tensor("xT", [D, S], BF16, kind="ExternalInput").ap()
    wT = nc.dram_tensor("wT", [D, 3 * EH], BF16, kind="ExternalInput").ap()
    if layer == 1:
        b3 = nc.dram_tensor("b3", [1, 3 * EH], F32, kind="ExternalInput").ap()
    qdt = F32 if layer == 0 else BF16
    qo = nc.dram_tensor("qo", [S, EH], qdt, kind="ExternalOutput").ap()
    ko = nc.dram_tensor("ko", [S, EH], qdt, kind="ExternalOutput").ap()
    vo = nc.dram_tensor("vo", [S, EH], BF16, kind="ExternalOutput").ap()
    outs = [qo, ko, vo]

    with contextlib.ExitStack() as st:
        E = st.enter_context
        X16 = E(nc.sbuf_tensor("X16", [128, 32 * 1024], BF16))
        W16 = [E(nc.sbuf_tensor(f"W16{i}", [128, 32 * 512], BF16)) for i in range(2)]
        ONES1 = E(nc.sbuf_tensor("ONES1", [1, 128], BF16))
        B16 = E(nc.sbuf_tensor("B16", [1, 3 * EH], BF16))
        EVF = [E(nc.sbuf_tensor(f"EVF{i}", [128, 512], F32)) for i in range(4)]
        EVB = [E(nc.sbuf_tensor(f"EVB{i}", [128, 512], BF16)) for i in range(4)]
        PS = [E(nc.psum_tensor(f"PS{i}", [128, 512], F32)) for i in range(4)]
        sX = _Sem(E(nc.semaphore("sX")))
        sW = [_Sem(E(nc.semaphore(f"sW{i}"))) for i in range(2)]
        sMM = _Sem(E(nc.semaphore("sMM")))
        sEV = _Sem(E(nc.semaphore("sEV")))
        sST = [_Sem(E(nc.semaphore(f"sST{i}"))) for i in range(4)]
        cV = _Sem(E(nc.semaphore("cV")))
        cS = _Sem(E(nc.semaphore("cS")))
        blk = E(nc.Block())
        sc = Sched(chains={"vector": cV, "scalar": cS})

        sc.o("vector", lambda e: e.memset(ONES1[:], 1.0))
        if layer == 1:
            sc.o("gpsimd", lambda e: e.dma_start(out=B16[:], in_=b3), sX, 16)

        def wload(w):
            tg, et = divmod(w, 6)
            src = wT[:, et * 512:(et + 1) * 512].rearrange("(i p) e -> p i e", p=128)
            dst = W16[w % 2][:].rearrange("p (i e) -> p i e", i=32)
            if w >= 2:
                sc.w("sync", sMM, (w - 1) * 8)
            sc.o("sync", lambda e, s=src, d=dst: e.dma_start(out=d, in_=s), sW[w % 2], 16)

        wload(0)
        wload(1)
        bias_off = 16 if layer == 1 else 0
        n = 0
        for tg in range(4):
            if tg > 0:
                sc.w("gpsimd", sMM, 48 * tg)
            src = xT[:, tg * 1024:(tg + 1) * 1024].rearrange("(i p) t -> p i t", p=128)
            dst = X16[:].rearrange("p (i t) -> p i t", i=32)
            sc.o("gpsimd", lambda e, s=src, d=dst: e.dma_start(out=d, in_=s), sX, 16)
            for et in range(6):
                w = tg * 6 + et
                sc.w("tensor", sX, 16 * (tg + 1) + bias_off)
                sc.w("tensor", sW[w % 2], 16 * (w // 2 + 1))
                sec = et // 2
                ecol = (et % 2) * 512
                for tc in range(8):
                    n += 1
                    bank = (n - 1) % 4
                    if n > 4:
                        sc.w("tensor", sEV, n - 4)
                    for d in range(32):
                        last = (d == 31) and (layer == 0)
                        sc.o("tensor", lambda e, d=d, b=bank, tc=tc, st=(d == 0), sp=last, wb=w % 2:
                             e.matmul(PS[b][:], X16[:, d * 1024 + tc * 128: d * 1024 + tc * 128 + 128],
                                      W16[wb][:, d * 512:(d + 1) * 512], start=st, stop=sp),
                             sMM if last else None, 1)
                    if layer == 1:
                        sc.o("tensor", lambda e, b=bank, c=et * 512:
                             e.matmul(PS[b][:], ONES1[:], B16[0:1, c:c + 512], start=False, stop=True),
                             sMM, 1)
                    sc.w("vector", sMM, n)
                    if n > 4:
                        sc.w("vector", sST[bank], 16 * ((n - 4 - 1) // 4 + 1))
                    if layer == 0:
                        tgt = EVF[bank] if sec < 2 else EVB[bank]
                        sc.o("vector", lambda e, b=bank, t=tgt: e.tensor_scalar(
                            t[:], PS[b][:], -8.0, 8.0, op0=ALU.max, op1=ALU.min), sEV, 1)
                    else:
                        tgt = EVB[bank]
                        if sec == 0:
                            sc.o("vector", lambda e, b=bank, t=tgt: e.tensor_scalar(
                                t[:], PS[b][:], 1.0 / 128.0, None, op0=ALU.mult), sEV, 1)
                        else:
                            sc.o("vector", lambda e, b=bank, t=tgt: e.tensor_copy(t[:], PS[b][:]), sEV, 1)
                    sc.w("sync", sEV, sEV.n)
                    t0 = tg * 1024 + tc * 128
                    sc.o("sync", lambda e, o=outs[sec][t0:t0 + 128, ecol:ecol + 512], t=tgt:
                         e.dma_start(out=o, in_=t[:]), sST[bank], 16)
                if w + 2 < 24:
                    wload(w + 2)
        for b in range(4):
            sc.w("sync", sST[b], sST[b].n)
        sc.emit(blk)
    return nc



def build_qkln():
    nc = bass.Bass("TRN2", target_bir_lowering=False)
    q = nc.dram_tensor("q", [TOK, D], F32, kind="ExternalInput").ap()
    k = nc.dram_tensor("k", [TOK, D], F32, kind="ExternalInput").ap()
    gq = nc.dram_tensor("gq", [1, D], F32, kind="ExternalInput").ap()
    gk = nc.dram_tensor("gk", [1, D], F32, kind="ExternalInput").ap()
    q2 = nc.dram_tensor("q2", [TOK, D], BF16, kind="ExternalOutput").ap()
    k2 = nc.dram_tensor("k2", [TOK, D], BF16, kind="ExternalOutput").ap()

    with contextlib.ExitStack() as st:
        E = st.enter_context
        GQ = E(nc.sbuf_tensor("GQ", [1, D], F32))
        GK = E(nc.sbuf_tensor("GK", [1, D], F32))
        G1 = E(nc.sbuf_tensor("G1", [1, D], F32))
        G16 = E(nc.sbuf_tensor("G16", [1, D], BF16))
        G32 = E(nc.sbuf_tensor("G32", [128, D], F32))
        ONES1 = E(nc.sbuf_tensor("ONES1", [1, 128], BF16))
        QC = E(nc.sbuf_tensor("QC", [128, D], F32))
        KC = E(nc.sbuf_tensor("KC", [128, D], F32))
        T1 = E(nc.sbuf_tensor("T1", [128, D], F32))
        Q2T = E(nc.sbuf_tensor("Q2T", [128, D], BF16))
        K2T = E(nc.sbuf_tensor("K2T", [128, D], BF16))
        ST6 = E(nc.sbuf_tensor("ST6", [128, 48], F32))
        MV = E(nc.sbuf_tensor("MV", [128, 2], F32))
        MVK = E(nc.sbuf_tensor("MVK", [128, 2], F32))
        SD = E(nc.sbuf_tensor("SD", [128, 1], F32))
        RS = E(nc.sbuf_tensor("RS", [128, 1], F32))
        SDK = E(nc.sbuf_tensor("SDK", [128, 1], F32))
        RSK = E(nc.sbuf_tensor("RSK", [128, 1], F32))
        EPSC = E(nc.sbuf_tensor("EPSC", [128, 1], F32))
        PSG = E(nc.psum_tensor("PSG", [128, 512], F32))
        sL = _Sem(E(nc.semaphore("sL")))
        sV = _Sem(E(nc.semaphore("sV")))
        sSc = _Sem(E(nc.semaphore("sSc")))
        sT = _Sem(E(nc.semaphore("sT")))
        sO = _Sem(E(nc.semaphore("sO")))
        cV = _Sem(E(nc.semaphore("cV")))
        cS = _Sem(E(nc.semaphore("cS")))
        blk = E(nc.Block())
        sc = Sched(chains={"vector": cV, "scalar": cS})

        sc.o("sync", lambda e: e.dma_start(out=GQ[:], in_=gq), sL, 16)
        sc.o("sync", lambda e: e.dma_start(out=GK[:], in_=gk), sL, 16)
        sc.o("vector", lambda e: e.memset(ONES1[:], 1.0))
        sc.o("vector", lambda e: e.memset(EPSC[:], EPS), sV, 1)
        sc.w("vector", sL, 32)
        sc.o("vector", lambda e: e.scalar_tensor_tensor(G1[:], GQ[:], ISQ, GK[:], op0=ALU.mult, op1=ALU.mult))
        sc.o("vector", lambda e: e.tensor_copy(G16[:], G1[:]), sV, 1)
        for c in range(8):
            sc.w("tensor", sV, 2 + c)
            sc.o("tensor", lambda e, c=c: e.matmul(PSG[:], ONES1[:], G16[0:1, c * 512:(c + 1) * 512],
                                                   start=True, stop=True), sT, 1)
            sc.w("vector", sT, sT.n)
            sc.o("vector", lambda e, c=c: e.tensor_copy(G32[:, c * 512:(c + 1) * 512], PSG[:]), sV, 1)

        for tc in range(8):
            t0 = tc * 128
            sc.o("sync", lambda e, a=q[t0:t0 + 128, :]: e.dma_start(out=QC[:], in_=a), sL, 16)
            sc.o("sync", lambda e, a=k[t0:t0 + 128, :]: e.dma_start(out=KC[:], in_=a), sL, 16)
            sc.w("vector", sL, sL.n)
            if tc > 0:
                sc.w("vector", sO, 32 * tc)
            for cc in range(8):
                sc.o("vector", lambda e, cc=cc: e.bn_stats(ST6[:, cc * 6:(cc + 1) * 6], QC[:, cc * 512:(cc + 1) * 512]))
            sc.o("vector", lambda e: e.bn_aggr(MV[:], ST6[:]))
            for cc in range(8):
                sc.o("vector", lambda e, cc=cc: e.bn_stats(ST6[:, cc * 6:(cc + 1) * 6], KC[:, cc * 512:(cc + 1) * 512]))
            sc.o("vector", lambda e: e.bn_aggr(MVK[:], ST6[:]), sV, 1)
            sc.w("scalar", sV, sV.n)
            sc.o("scalar", lambda e: e.activation(SD[:], MV[:, 1:2], AF.Sqrt, bias=EPSC[:], scale=1.0))
            sc.o("scalar", lambda e: e.activation(SDK[:], MVK[:, 1:2], AF.Sqrt, bias=EPSC[:], scale=1.0), sSc, 1)
            sc.w("vector", sSc, sSc.n)
            sc.o("vector", lambda e: e.reciprocal(RS[:], SD[:]))
            sc.o("vector", lambda e: e.reciprocal(RSK[:], SDK[:]))
            sc.o("vector", lambda e: e.tensor_scalar(T1[:], QC[:], MV[:, 0:1], RS[:, 0:1], op0=ALU.subtract, op1=ALU.mult))
            sc.o("vector", lambda e: e.tensor_tensor(Q2T[:], T1[:], G32[:], op=ALU.mult), sV, 1)
            sc.o("vector", lambda e: e.tensor_scalar(K2T[:], KC[:], MVK[:, 0:1], RSK[:, 0:1], op0=ALU.subtract, op1=ALU.mult), sV, 1)
            sc.w("sync", sV, sV.n)
            sc.o("sync", lambda e, o=q2[t0:t0 + 128, :]: e.dma_start(out=o, in_=Q2T[:]), sO, 16)
            sc.o("sync", lambda e, o=k2[t0:t0 + 128, :]: e.dma_start(out=o, in_=K2T[:]), sO, 16)
        sc.w("sync", sO, sO.n)
        sc.emit(blk)
    return nc



def build_attn(debug=False):
    nc = bass.Bass("TRN2", target_bir_lowering=False)
    qf = nc.dram_tensor("qf", [8, HD, S], BF16, kind="ExternalInput").ap()
    kf = nc.dram_tensor("kf", [8, HD, S], BF16, kind="ExternalInput").ap()
    vt = nc.dram_tensor("vt", [8, S, HD], BF16, kind="ExternalInput").ap()
    ahi = nc.dram_tensor("ahi", [8, S], BF16, kind="ExternalInput").ap()
    ami = nc.dram_tensor("ami", [8, S], BF16, kind="ExternalInput").ap()
    alo = nc.dram_tensor("alo", [8, S], BF16, kind="ExternalInput").ap()
    acolt = nc.dram_tensor("acolt", [8, 128, 32], F32, kind="ExternalInput").ap()
    ub8 = nc.dram_tensor("ub8", [8, 128], F32, kind="ExternalInput").ap()
    uw8 = nc.dram_tensor("uw8", [8, 128], F32, kind="ExternalInput").ap()
    ident = nc.dram_tensor("ident", [128, 128], F32, kind="ExternalInput").ap()
    identb = nc.dram_tensor("identb", [128, 128], BF16, kind="ExternalInput").ap()
    negd = nc.dram_tensor("negd", [128, 128], F32, kind="ExternalInput").ap()
    trit = nc.dram_tensor("trit", [128, 896], F32, kind="ExternalInput").ap()
    oo = nc.dram_tensor("oo", [8, HD, S], BF16, kind="ExternalOutput").ap()
    if debug:
        dnm = nc.dram_tensor("dnm", [8, 128, 32], F32, kind="ExternalOutput").ap()
        dsg = nc.dram_tensor("dsg", [8, 128, 32], F32, kind="ExternalOutput").ap()
        dqrow = nc.dram_tensor("dqrow", [8, 3, S], BF16, kind="ExternalOutput").ap()
        dksc = nc.dram_tensor("dksc", [8, 128, 32], F32, kind="ExternalOutput").ap()
        dckr = nc.dram_tensor("dckr", [8, 1, S], F32, kind="ExternalOutput").ap()

    with contextlib.ExitStack() as st:
        E = st.enter_context
        QF = E(nc.sbuf_tensor("QF", [128, S], BF16))
        KF = E(nc.sbuf_tensor("KF", [128, S], BF16))
        VSB = E(nc.sbuf_tensor("VSB", [128, 32 * 128], BF16))
        KS5 = E(nc.sbuf_tensor("KS5", [5, S], BF16))
        QROW3 = E(nc.sbuf_tensor("QROW3", [3, S], BF16))
        ONES5 = E(nc.sbuf_tensor("ONES5", [5, 128], BF16))
        IDF = E(nc.sbuf_tensor("IDF", [128, 128], F32))
        IDB = E(nc.sbuf_tensor("IDB", [128, 128], BF16))
        NEGD = E(nc.sbuf_tensor("NEGD", [128, 128], F32))
        TRIT = E(nc.sbuf_tensor("TRIT", [128, 896], F32))
        UB = E(nc.sbuf_tensor("UB", [8, 128], F32))
        UW = E(nc.sbuf_tensor("UW", [8, 128], F32))
        U8 = E(nc.sbuf_tensor("U8", [8, 128], F32))
        UC = E(nc.sbuf_tensor("UC", [128, 8], F32))
        UCB = E(nc.sbuf_tensor("UCB", [128, 8], BF16))
        KSCA = E(nc.sbuf_tensor("KSCA", [128, 32], F32))
        KSC2 = E(nc.sbuf_tensor("KSC2", [128, 32], F32))
        CKR = E(nc.sbuf_tensor("CKR", [1, S], F32))
        CKH = E(nc.sbuf_tensor("CKH", [1, S], BF16))
        CKL = E(nc.sbuf_tensor("CKL", [1, S], BF16))
        ESC = E(nc.sbuf_tensor("ESC", [128, 2048], BF16))
        NM = [E(nc.sbuf_tensor(f"NM{i}", [128, 1], F32)) for i in range(2)]
        SG = [E(nc.sbuf_tensor(f"SG{i}", [128, 1], F32)) for i in range(2)]
        NMc = E(nc.sbuf_tensor("NMc", [128, 1], F32))
        SGc = E(nc.sbuf_tensor("SGc", [128, 1], F32))
        TC1 = E(nc.sbuf_tensor("TC1", [128, 1], F32))
        TC2 = E(nc.sbuf_tensor("TC2", [128, 1], F32))
        EX1 = E(nc.sbuf_tensor("EX1", [128, 1], F32))
        EX2 = E(nc.sbuf_tensor("EX2", [128, 1], F32))
        LNS = E(nc.sbuf_tensor("LNS", [128, 1], F32))
        QA = E(nc.sbuf_tensor("QA", [128, 1], F32))
        QA3B = E(nc.sbuf_tensor("QA3B", [128, 3], BF16))
        QH32 = E(nc.sbuf_tensor("QH32", [128, 1], F32))
        QR1 = E(nc.sbuf_tensor("QR1", [128, 1], F32))
        PTS = [E(nc.sbuf_tensor(f"PTS{i}", [128, 512], BF16)) for i in range(3)]
        NMall = E(nc.sbuf_tensor("NMall", [128, 32], F32))
        SGall = E(nc.sbuf_tensor("SGall", [128, 32], F32))
        OEV = [E(nc.sbuf_tensor(f"OEV{i}", [128, 512], BF16)) for i in range(2)]
        PROW = E(nc.psum_tensor("PROW", [128, 2048], F32))
        PT3 = E(nc.psum_tensor("PT3", [128, 1536], F32))
        PXB = E(nc.psum_tensor("PXB", [128, 1024], BF16))
        sL = _Sem(E(nc.semaphore("sL")))
        sT = _Sem(E(nc.semaphore("sT")))
        sV = _Sem(E(nc.semaphore("sV")))
        sSc = _Sem(E(nc.semaphore("sSc")))
        sE2 = _Sem(E(nc.semaphore("sE2")))
        sPV = _Sem(E(nc.semaphore("sPV")))
        sO = [_Sem(E(nc.semaphore(f"sO{i}"))) for i in range(2)]
        sR = _Sem(E(nc.semaphore("sR")))
        cV = _Sem(E(nc.semaphore("cV")))
        cS = _Sem(E(nc.semaphore("cS")))
        blk = E(nc.Block())
        sc = Sched(chains={"vector": cV, "scalar": cS})

        sc.o("sync", lambda e: e.dma_start(out=IDF[:], in_=ident), sL, 16)
        sc.o("sync", lambda e: e.dma_start(out=IDB[:], in_=identb), sL, 16)
        sc.o("sync", lambda e: e.dma_start(out=NEGD[:], in_=negd), sL, 16)
        sc.o("sync", lambda e: e.dma_start(out=TRIT[:], in_=trit), sL, 16)
        sc.o("sync", lambda e: e.dma_start(out=UB[:], in_=ub8), sL, 16)
        sc.o("sync", lambda e: e.dma_start(out=UW[:], in_=uw8), sL, 16)
        sc.o("vector", lambda e: e.memset(ONES5[:], 1.0))
        sc.w("vector", sL, 6 * 16)
        sc.o("vector", lambda e: e.scalar_tensor_tensor(U8[:], UB[:], ISQ, UW[:], op0=ALU.mult, op1=ALU.mult), sV, 1)
        sc.w("tensor", sV, sV.n)
        sc.o("tensor", lambda e: e.transpose(PT3[0:128, 0:8], U8[:], IDF[0:8, 0:8]), sT, 1)
        sc.w("vector", sT, sT.n)
        sc.o("vector", lambda e: e.tensor_copy(UC[:], PT3[0:128, 0:8]))
        sc.o("vector", lambda e: e.tensor_copy(UCB[:], UC[:]), sV, 1)

        for p in range(8):
            sc.o("sync", lambda e, p=p: e.dma_start(out=QF[:], in_=qf[p]), sL, 16)
            sc.o("sync", lambda e, p=p: e.dma_start(out=KF[:], in_=kf[p]), sL, 16)
            sc.o("sync", lambda e, p=p: e.dma_start(
                out=VSB[:].rearrange("p (kc d) -> p kc d", kc=32),
                in_=vt[p].rearrange("(kc p) d -> p kc d", p=128)), sL, 16)
            sc.o("sync", lambda e, p=p: e.dma_start(out=KSCA[:], in_=acolt[p]), sL, 16)
            sc.o("sync", lambda e, p=p: e.dma_start(out=KS5[0:1, :], in_=ahi[p:p + 1, :]), sR, 16)
            sc.o("sync", lambda e, p=p: e.dma_start(out=KS5[1:2, :], in_=ami[p:p + 1, :]), sR, 16)
            sc.o("sync", lambda e, p=p: e.dma_start(out=KS5[2:3, :], in_=alo[p:p + 1, :]), sR, 16)
            lded = sL.n
            rows0 = sR.n

            sc.w("tensor", sL, lded)
            sc.w("tensor", sV, sV.n)
            for c in range(8):
                dst = PT3[0:1, 1024:1536] if c % 2 == 0 else PROW[0:1, 0:512]
                sc.o("tensor", lambda e, c=c, d=dst, p=p: e.matmul(
                    d, UCB[:, p:p + 1], KF[:, c * 512:(c + 1) * 512], start=True, stop=True), sT, 1)
                sc.w("vector", sT, sT.n)
                sc.o("vector", lambda e, c=c, d=dst: e.tensor_copy(CKR[0:1, c * 512:(c + 1) * 512], d), sV, 1)
                sc.w("tensor", sV, sV.n)
            sc.o("vector", lambda e: e.tensor_copy(CKH[:], CKR[:]))
            sc.o("vector", lambda e: e.scalar_tensor_tensor(CKL[:], CKR[:], 1.0, CKH[:], op0=ALU.mult, op1=ALU.subtract), sV, 1)
            sc.w("sync", sV, sV.n)
            sc.o("sync", lambda e: e.dma_start(out=KS5[3:4, :], in_=CKH[:]), sR, 16)
            sc.o("sync", lambda e: e.dma_start(out=KS5[4:5, :], in_=CKL[:]), sR, 16)
            rows_done = sR.n
            for kc in range(32):
                dst = PT3[0:128, 1024 + (kc % 2):1024 + (kc % 2) + 1]
                sc.o("tensor", lambda e, kc=kc, d=dst, p=p: e.matmul(
                    d, KF[:, kc * 128:(kc + 1) * 128], UCB[:, p:p + 1], start=True, stop=True), sT, 1)
                sc.w("vector", sT, sT.n)
                sc.o("vector", lambda e, kc=kc, d=dst: e.tensor_tensor(
                    KSC2[:, kc:kc + 1], KSCA[:, kc:kc + 1], d, op=ALU.add), sV, 1)
                sc.w("tensor", sV, sV.n)
            ck_cols_done = sV.n

            sc.w("tensor", sR, rows_done)
            for qt in range(32):
                kext = (qt + 1) * 128
                blocks = [(0, min(kext, 2048))]
                if kext > 2048:
                    blocks.append((2048, kext))
                for bi, (b0, b1) in enumerate(blocks):
                    wb = b1 - b0
                    off = b0
                    tmm = None
                    while off < b1:
                        wch = min(512, b1 - off)
                        sc.o("tensor", lambda e, qt=qt, off=off, wch=wch, b0=b0: e.matmul(
                            PROW[:, off - b0:off - b0 + wch], QF[:, qt * 128:qt * 128 + 128],
                            KF[:, off:off + wch], start=True, stop=False))
                        lastch = off + wch >= b1
                        tmm = sc.o("tensor", lambda e, off=off, wch=wch, b0=b0: e.matmul(
                            PROW[:, off - b0:off - b0 + wch], ONES5[:], KS5[0:5, off:off + wch],
                            start=False, stop=True), sT if lastch else None, 1)
                        off += wch
                    if b1 == kext:
                        dloc = qt * 128 - b0
                        sc.w("vector", sT, tmm)
                        vmask = sc.o("vector", lambda e, dloc=dloc: e.tensor_tensor(
                            PROW[:, dloc:dloc + 128], PROW[:, dloc:dloc + 128], NEGD[:], op=ALU.add), sV, 1)
                        sc.w("vector", sV, vmask)
                    else:
                        sc.w("vector", sT, tmm)
                    vred = sc.o("vector", lambda e, wb=wb, bi=bi: e.reduce_max(
                        NM[bi][:], PROW[:, 0:wb], axis=AX.X, negate=True), sV, 1)
                    sc.w("scalar", sV, vred)
                    sexp = sc.o("scalar", lambda e, wb=wb, bi=bi: e.activation(
                        ESC[:, 0:wb], PROW[:, 0:wb], AF.Exp, bias=NM[bi][:], scale=1.0,
                        accum_out=SG[bi][:]), sSc, 1)
                    sc.w("tensor", sSc, sexp)
                if len(blocks) == 1:
                    nmf, sgf = NM[0], SG[0]
                    vs = None
                else:
                    sc.o("vector", lambda e: e.tensor_tensor(NMc[:], NM[0][:], NM[1][:], op=ALU.min))
                    sc.o("vector", lambda e: e.tensor_tensor(TC1[:], NMc[:], NM[0][:], op=ALU.subtract))
                    vq = sc.o("vector", lambda e: e.tensor_tensor(TC2[:], NMc[:], NM[1][:], op=ALU.subtract), sV, 1)
                    sc.w("scalar", sV, vq)
                    sq = sc.o("scalar", lambda e: (e.activation(EX1[:], TC1[:], AF.Exp),
                                                   e.activation(EX2[:], TC2[:], AF.Exp))[-1], sSc, 1)
                    sc.w("vector", sSc, sq)
                    sc.o("vector", lambda e: e.tensor_tensor(TC1[:], SG[0][:], EX1[:], op=ALU.mult))
                    sc.o("vector", lambda e: e.tensor_tensor(TC2[:], SG[1][:], EX2[:], op=ALU.mult))
                    vs = sc.o("vector", lambda e: e.tensor_tensor(SGc[:], TC1[:], TC2[:], op=ALU.add), sV, 1)
                    nmf, sgf = NMc, SGc
                if vs is not None:
                    sc.w("scalar", sV, vs)
                if debug:
                    sc.o("vector", lambda e, qt=qt, nmf=nmf: e.tensor_copy(NMall[:, qt:qt + 1], nmf[:]))
                    sc.o("vector", lambda e, qt=qt, sgf=sgf: e.tensor_copy(SGall[:, qt:qt + 1], sgf[:]))
                sln = sc.o("scalar", lambda e, sgf=sgf: e.activation(LNS[:], sgf[:], AF.Ln), sSc, 1)
                sc.w("vector", sSc, sln)
                sc.o("vector", lambda e, nmf=nmf: e.tensor_tensor(QA[:], nmf[:], LNS[:], op=ALU.subtract))
                sc.o("vector", lambda e: e.tensor_copy(QA3B[:, 0:1], QA[:]))
                sc.o("vector", lambda e: e.tensor_copy(QH32[:], QA3B[:, 0:1]))
                sc.o("vector", lambda e: e.tensor_tensor(QR1[:], QA[:], QH32[:], op=ALU.subtract))
                sc.o("vector", lambda e: e.tensor_copy(QA3B[:, 1:2], QR1[:]))
                sc.o("vector", lambda e: e.tensor_copy(QH32[:], QA3B[:, 1:2]))
                sc.o("vector", lambda e: e.tensor_tensor(QR1[:], QR1[:], QH32[:], op=ALU.subtract))
                vqa = sc.o("vector", lambda e: e.tensor_copy(QA3B[:, 2:3], QR1[:]), sV, 1)
                sc.w("tensor", sV, vqa)
                ttr = sc.o("tensor", lambda e: e.transpose(PXB[0:3, 0:128], QA3B[:], IDB[:]), sT, 1)
                sc.w("vector", sT, ttr)
                vevq = sc.o("vector", lambda e, qt=qt: e.tensor_copy(
                    QROW3[0:3, qt * 128:qt * 128 + 128], PXB[0:3, 0:128]), sV, 1)
                sc.w("tensor", sV, vevq)

            if debug:
                dv = sc.o("vector", lambda e: e.engine_nop(), sV, 1)
                sc.w("sync", sV, dv)
                sc.o("sync", lambda e, p=p: e.dma_start(out=dnm[p], in_=NMall[:]), sR, 16)
                sc.o("sync", lambda e, p=p: e.dma_start(out=dsg[p], in_=SGall[:]), sR, 16)
                sc.o("sync", lambda e, p=p: e.dma_start(out=dqrow[p], in_=QROW3[:]), sR, 16)
                sc.o("sync", lambda e, p=p: e.dma_start(out=dksc[p], in_=KSC2[:]), sR, 16)
                sc.o("sync", lambda e, p=p: e.dma_start(out=dckr[p], in_=CKR[:]), sR, 16)
                sc.w("sync", sR, sR.n)

            for qg in range(8):
                kcmax = qg * 4 + 4
                for kc in range(kcmax):
                    tb = kc % 3
                    sc.o("tensor", lambda e, kc=kc, qg=qg, tb=tb: e.matmul(
                        PT3[:, tb * 512:(tb + 1) * 512], KF[:, kc * 128:kc * 128 + 128],
                        QF[:, qg * 512:(qg + 1) * 512], start=True, stop=False))
                    tmm = sc.o("tensor", lambda e, qg=qg, tb=tb: e.matmul(
                        PT3[:, tb * 512:(tb + 1) * 512], ONES5[0:3, :],
                        QROW3[0:3, qg * 512:(qg + 1) * 512], start=False, stop=True), sT, 1)
                    c2 = kc * 128 - qg * 512
                    if c2 >= 0:
                        sc.w("vector", sT, tmm)
                        vm = sc.o("vector", lambda e, tb=tb, c2=c2: e.tensor_tensor(
                            PT3[:, tb * 512:(tb + 1) * 512], PT3[:, tb * 512:(tb + 1) * 512],
                            TRIT[:, 384 - c2:384 - c2 + 512], op=ALU.add), sV, 1)
                        sc.w("scalar", sV, vm)
                    else:
                        sc.w("scalar", sT, tmm)
                    sex = sc.o("scalar", lambda e, tb=tb, kc=kc: e.activation(
                        PTS[tb][:], PT3[:, tb * 512:(tb + 1) * 512], AF.Exp,
                        bias=KSC2[:, kc:kc + 1], scale=1.0), sE2, 1)
                    sc.w("tensor", sE2, sex)
                    sc.o("tensor", lambda e, kc=kc, tb=tb, st=(kc == 0), sp=(kc == kcmax - 1): e.matmul(
                        PROW[:, 0:512], VSB[:, kc * 128:kc * 128 + 128], PTS[tb][:],
                        start=st, stop=sp), sPV, 1)
                sc.w("vector", sPV, sPV.n)
                ob = qg % 2
                if sO[ob].n > 0:
                    sc.w("vector", sO[ob], sO[ob].n)
                vo_ = sc.o("vector", lambda e, ob=ob: e.tensor_copy(OEV[ob][:], PROW[:, 0:512]), sV, 1)
                sc.w("sync", sV, vo_)
                sc.o("sync", lambda e, p=p, qg=qg, ob=ob: e.dma_start(
                    out=oo[p][:, qg * 512:(qg + 1) * 512], in_=OEV[ob][:]), sO[ob], 16)
                sc.w("tensor", sV, vo_)
            sc.w("sync", sPV, sPV.n)
        sc.w("sync", sO[0], sO[0].n)
        sc.w("sync", sO[1], sO[1].n)
        sc.emit(blk)
    return nc



def build_proj():
    nc = bass.Bass("TRN2", target_bir_lowering=False)
    aT = nc.dram_tensor("aT", [D, TOK], BF16, kind="ExternalInput").ap()
    wT = nc.dram_tensor("wT", [D, D], BF16, kind="ExternalInput").ap()
    br = nc.dram_tensor("br", [1, D], F32, kind="ExternalInput").ap()
    resid = nc.dram_tensor("resid", [TOK, D], F32, kind="ExternalInput").ap()
    xo = nc.dram_tensor("xo", [TOK, D], F32, kind="ExternalOutput").ap()
    xob = nc.dram_tensor("xob", [TOK, D], BF16, kind="ExternalOutput").ap()
    Y = nc.dram_tensor("Y", [TOK, D], F32, kind="Internal").ap()

    with contextlib.ExitStack() as st:
        E = st.enter_context
        A16 = E(nc.sbuf_tensor("A16", [128, 32 * 1024], BF16))
        W16 = [E(nc.sbuf_tensor(f"W16{i}", [128, 32 * 512], BF16)) for i in range(2)]
        ONES1 = E(nc.sbuf_tensor("ONES1", [1, 128], BF16))
        B16 = E(nc.sbuf_tensor("B16", [1, D], BF16))
        EVF = [E(nc.sbuf_tensor(f"EVF{i}", [128, 512], F32)) for i in range(4)]
        YC = E(nc.sbuf_tensor("YC", [128, D], F32))
        RC = E(nc.sbuf_tensor("RC", [128, D], F32))
        XOB = E(nc.sbuf_tensor("XOB", [128, D], BF16))
        Y2 = YC
        XO = RC
        ST6 = E(nc.sbuf_tensor("ST6", [128, 48], F32))
        MV = E(nc.sbuf_tensor("MV", [128, 2], F32))
        SD = E(nc.sbuf_tensor("SD", [128, 1], F32))
        RS = E(nc.sbuf_tensor("RS", [128, 1], F32))
        EPSC = E(nc.sbuf_tensor("EPSC", [128, 1], F32))
        PS = [E(nc.psum_tensor(f"PS{i}", [128, 512], F32)) for i in range(4)]
        sX = _Sem(E(nc.semaphore("sX")))
        sW = [_Sem(E(nc.semaphore(f"sW{i}"))) for i in range(2)]
        sMM = _Sem(E(nc.semaphore("sMM")))
        sEV = _Sem(E(nc.semaphore("sEV")))
        sST = [_Sem(E(nc.semaphore(f"sST{i}"))) for i in range(4)]
        sL = _Sem(E(nc.semaphore("sL")))
        sV = _Sem(E(nc.semaphore("sV")))
        sSc = _Sem(E(nc.semaphore("sSc")))
        sO = _Sem(E(nc.semaphore("sO")))
        cV = _Sem(E(nc.semaphore("cV")))
        cS = _Sem(E(nc.semaphore("cS")))
        blk = E(nc.Block())
        sc = Sched(chains={"vector": cV, "scalar": cS})

        sc.o("vector", lambda e: e.memset(ONES1[:], 1.0))
        sc.o("vector", lambda e: e.memset(EPSC[:], EPS))
        sc.o("gpsimd", lambda e: e.dma_start(out=B16[:], in_=br), sX, 16)
        sc.o("gpsimd", lambda e: e.dma_start(
            out=A16[:].rearrange("p (i t) -> p i t", i=32),
            in_=aT.rearrange("(i p) t -> p i t", p=128)), sX, 16)

        def wload(w):
            src = wT[:, w * 512:(w + 1) * 512].rearrange("(i p) e -> p i e", p=128)
            dst = W16[w % 2][:].rearrange("p (i e) -> p i e", i=32)
            if w >= 2:
                sc.w("sync", sMM, (w - 1) * 8)
            sc.o("sync", lambda e, s=src, d=dst: e.dma_start(out=d, in_=s), sW[w % 2], 16)

        wload(0)
        wload(1)
        n = 0
        for et in range(8):
            sc.w("tensor", sX, 32)
            sc.w("tensor", sW[et % 2], 16 * (et // 2 + 1))
            for tc in range(8):
                n += 1
                bank = (n - 1) % 4
                if n > 4:
                    sc.w("tensor", sEV, n - 4)
                for d in range(32):
                    sc.o("tensor", lambda e, d=d, b=bank, tc=tc, wb=et % 2: e.matmul(
                        PS[b][:], A16[:, d * 1024 + tc * 128: d * 1024 + tc * 128 + 128],
                        W16[wb][:, d * 512:(d + 1) * 512], start=(d == 0), stop=False))
                sc.o("tensor", lambda e, b=bank, c=et * 512: e.matmul(
                    PS[b][:], ONES1[:], B16[0:1, c:c + 512], start=False, stop=True), sMM, 1)
                sc.w("vector", sMM, n)
                if n > 4:
                    sc.w("vector", sST[bank], 16 * ((n - 4 - 1) // 4 + 1))
                sc.o("vector", lambda e, b=bank: e.tensor_copy(EVF[b][:], PS[b][:]), sEV, 1)
                sc.w("sync", sEV, sEV.n)
                sc.o("sync", lambda e, t0=tc * 128, c=et * 512, b=bank: e.dma_start(
                    out=Y[t0:t0 + 128, c:c + 512], in_=EVF[b][:]), sST[bank], 16)
            if et + 2 < 8:
                wload(et + 2)
        for tc in range(8):
            t0 = tc * 128
            if tc == 0:
                for b in range(4):
                    sc.w("sync", sST[b], sST[b].n)
            else:
                sc.w("sync", sO, 32 * tc)
            sc.o("sync", lambda e, a=Y[t0:t0 + 128, :]: e.dma_start(out=YC[:], in_=a), sL, 16)
            sc.o("sync", lambda e, a=resid[t0:t0 + 128, :]: e.dma_start(out=RC[:], in_=a), sL, 16)
            sc.w("vector", sL, sL.n)
            if tc > 0:
                sc.w("vector", sO, 32 * tc)
            sc.o("vector", lambda e: e.tensor_tensor(Y2[:], YC[:], RC[:], op=ALU.add))
            for cc in range(8):
                sc.o("vector", lambda e, cc=cc: e.bn_stats(ST6[:, cc * 6:(cc + 1) * 6], Y2[:, cc * 512:(cc + 1) * 512]))
            sc.o("vector", lambda e: e.bn_aggr(MV[:], ST6[:]), sV, 1)
            sc.w("scalar", sV, sV.n)
            sc.o("scalar", lambda e: e.activation(SD[:], MV[:, 1:2], AF.Sqrt, bias=EPSC[:], scale=1.0), sSc, 1)
            sc.w("vector", sSc, sSc.n)
            sc.o("vector", lambda e: e.reciprocal(RS[:], SD[:]))
            sc.o("vector", lambda e: e.tensor_scalar(XO[:], Y2[:], MV[:, 0:1], RS[:, 0:1], op0=ALU.subtract, op1=ALU.mult))
            sc.o("vector", lambda e: e.tensor_copy(XOB[:], XO[:]), sV, 1)
            sc.w("sync", sV, sV.n)
            sc.o("sync", lambda e, o=xo[t0:t0 + 128, :]: e.dma_start(out=o, in_=XO[:]), sO, 16)
            sc.o("sync", lambda e, o=xob[t0:t0 + 128, :]: e.dma_start(out=o, in_=XOB[:]), sO, 16)
        sc.w("sync", sO, sO.n)
        sc.emit(blk)
    return nc



def _split3(a):
    a = a.astype(np.float64)
    hi = a.astype(BF)
    r = a - hi.astype(np.float64)
    mid = r.astype(BF)
    lo = (r - mid.astype(np.float64)).astype(BF)
    return hi, mid, lo


def _run(nc, in_maps):
    res = run_bass_kernel_spmd(nc, in_maps, core_ids=list(range(NCORE)))
    return res.results


_PROGS = {}

def _prog(name):
    if name not in _PROGS:
        _PROGS[name] = {"prep": build_prep, "qkv0": lambda: build_qkv(0),
                        "qkv1": lambda: build_qkv(1), "qkln": build_qkln,
                        "attn": build_attn, "attn_dbg": lambda: build_attn(True),
                        "proj": build_proj}[name]()
    return _PROGS[name]


_CONSTS = None

def _consts():
    global _CONSTS
    if _CONSTS is None:
        slopes = np.array([2.0 ** (-(h + 1) / 4.0) for h in range(H)], dtype=np.float64)
        kidx = np.arange(S, dtype=np.float64)
        ident = np.eye(128, dtype=np.float32)
        identb = np.eye(128, dtype=np.float32).astype(BF)
        ii, jj = np.meshgrid(np.arange(128), np.arange(128), indexing="ij")
        negd = np.where(jj > ii, np.float32(NEG), np.float32(0.0))
        iu, uu = np.meshgrid(np.arange(128), np.arange(896), indexing="ij")
        trit = np.where(iu <= uu - 384, np.float32(0.0), np.float32(NEG))
        AHI = np.empty((H, S), BF); AMI = np.empty((H, S), BF); ALO = np.empty((H, S), BF)
        ACOLT = np.empty((H, 128, 32), np.float32)
        for h in range(H):
            a = slopes[h] * kidx
            AHI[h], AMI[h], ALO[h] = _split3(a)
            ACOLT[h] = (slopes[h] * (np.arange(128)[:, None] + 128.0 * np.arange(32)[None, :])).astype(np.float32)
        _CONSTS = dict(ident=ident, identb=identb, negd=negd, trit=trit,
                       AHI=AHI, AMI=AMI, ALO=ALO, ACOLT=ACOLT)
    return _CONSTS


def run_prep(wqi, zz, ss, s2, mki, q3i, wni, castcat):
    maps = []
    for c in range(NCORE):
        r0, r1 = c * 512, (c + 1) * 512
        maps.append(dict(
            wq=wqi[r0:r1], zz=zz[r0:r1], ss=ss[r0:r1], s2=s2[r0:r1], mk=mki[r0:r1],
            q3=q3i[r0 * 128:r1 * 128], wn=wni[r0 * 128:r1 * 128],
            cin=np.ascontiguousarray(castcat[:, c * 4096:(c + 1) * 4096])))
    rp = _run(_prog("prep"), maps)
    W0 = np.concatenate([np.asarray(rp[c]["w0d"]) for c in range(NCORE)], axis=0)
    W1 = np.concatenate([np.asarray(rp[c]["w1d"]) for c in range(NCORE)], axis=0)
    castb = np.concatenate([np.asarray(rp[c]["cout"]) for c in range(NCORE)], axis=1)
    return W0, W1, castb


def run_qkv(layer, xbatches, Wfull, bias):
    maps = []
    for c in range(NCORE):
        b = c // 4
        hs = (c % 4) * 8
        wsl = np.concatenate([Wfull[:, hs * 128:hs * 128 + EH],
                              Wfull[:, D + hs * 128:D + hs * 128 + EH],
                              Wfull[:, 2 * D + hs * 128:2 * D + hs * 128 + EH]], axis=1)
        m = dict(xT=xbatches[b], wT=np.ascontiguousarray(wsl))
        if layer == 1:
            bsl = np.concatenate([bias[hs * 128:hs * 128 + EH],
                                  bias[D + hs * 128:D + hs * 128 + EH],
                                  bias[2 * D + hs * 128:2 * D + hs * 128 + EH]])[None, :]
            m["b3"] = np.ascontiguousarray(bsl.astype(np.float32))
        maps.append(m)
    rr = _run(_prog("qkv0" if layer == 0 else "qkv1"), maps)
    dt = np.float32 if layer == 0 else BF
    Qf = np.empty((2 * S, D), dt); Kf = np.empty((2 * S, D), dt); Vf = np.empty((2 * S, D), BF)
    for c in range(NCORE):
        b = c // 4
        hs = (c % 4) * 8
        Qf[b * S:(b + 1) * S, hs * 128:hs * 128 + EH] = np.asarray(rr[c]["qo"])
        Kf[b * S:(b + 1) * S, hs * 128:hs * 128 + EH] = np.asarray(rr[c]["ko"])
        Vf[b * S:(b + 1) * S, hs * 128:hs * 128 + EH] = np.asarray(rr[c]["vo"])
    return Qf, Kf, Vf


def run_qkln(Qr, Kr, gqv, gkv):
    gq = gqv[None, :].astype(np.float32)
    gk = gkv[None, :].astype(np.float32)
    maps = [dict(q=np.ascontiguousarray(Qr[c * TOK:(c + 1) * TOK]),
                 k=np.ascontiguousarray(Kr[c * TOK:(c + 1) * TOK]),
                 gq=gq, gk=gk) for c in range(NCORE)]
    rr = _run(_prog("qkln"), maps)
    Q2 = np.concatenate([np.asarray(rr[c]["q2"]) for c in range(NCORE)], axis=0)
    K2 = np.concatenate([np.asarray(rr[c]["k2"]) for c in range(NCORE)], axis=0)
    return Q2, K2


def run_attn(Q2, K2, Vf, ub, uw):
    C = _consts()
    maps = []
    for c in range(NCORE):
        b = c // 4
        hs = (c % 4) * 8
        qfs = np.empty((8, HD, S), BF); kfs = np.empty((8, HD, S), BF); vts = np.empty((8, S, HD), BF)
        for p in range(8):
            h = hs + p
            qfs[p] = Q2[b * S:(b + 1) * S, h * 128:(h + 1) * 128].T
            kfs[p] = K2[b * S:(b + 1) * S, h * 128:(h + 1) * 128].T
            vts[p] = Vf[b * S:(b + 1) * S, h * 128:(h + 1) * 128]
        maps.append(dict(
            qf=qfs, kf=kfs, vt=vts,
            ahi=C["AHI"][hs:hs + 8], ami=C["AMI"][hs:hs + 8], alo=C["ALO"][hs:hs + 8],
            acolt=C["ACOLT"][hs:hs + 8],
            ub8=np.ascontiguousarray(ub[hs * 128:(hs + 8) * 128].reshape(8, 128)),
            uw8=np.ascontiguousarray(uw[hs * 128:(hs + 8) * 128].reshape(8, 128)),
            ident=C["ident"], identb=C["identb"], negd=C["negd"], trit=C["trit"]))
    rr = _run(_prog("attn"), maps)
    Ofm = [np.empty((D, S), BF) for _ in range(2)]
    for c in range(NCORE):
        b = c // 4
        hs = (c % 4) * 8
        oc = np.asarray(rr[c]["oo"])
        for p in range(8):
            Ofm[b][(hs + p) * 128:(hs + p + 1) * 128, :] = oc[p]
    return Ofm


def run_proj(Ofm_or_aTs, wTb, bias, residfull, is_ofm=True):
    maps = []
    for c in range(NCORE):
        if is_ofm:
            b = c // 4
            aTc = np.ascontiguousarray(Ofm_or_aTs[b][:, (c % 4) * TOK:(c % 4 + 1) * TOK])
        else:
            aTc = Ofm_or_aTs[c]
        maps.append(dict(aT=aTc, wT=wTb, br=bias[None, :].astype(np.float32),
                         resid=np.ascontiguousarray(residfull[c * TOK:(c + 1) * TOK])))
    rr = _run(_prog("proj"), maps)
    xn = np.concatenate([np.asarray(rr[c]["xo"]) for c in range(NCORE)], axis=0)
    xnb = np.concatenate([np.asarray(rr[c]["xob"]) for c in range(NCORE)], axis=0)
    return xn, xnb


def kernel_unfused(L0_attn_hidden_states, L0_attn_Wqkv, L0_attn_q_ln_w, L0_attn_q_ln_b,
           L0_attn_k_ln_w, L0_attn_k_ln_b, L0_attn_Wout, L0_ffn_W_q, L0_ffn_scales,
           L0_ffn_zeros, L0_ffn_mask, L0_ffn_scale2, L0_ffn_bias, L1_attn_w_attn,
           L1_attn_b_attn, L1_attn_w_proj, L1_attn_b_proj, L1_ffn_weight_q3,
           L1_ffn_weight_norm, L1_ffn_bias):
    x = np.asarray(L0_attn_hidden_states, dtype=np.float32).reshape(2 * S, D)

    WqT = np.ascontiguousarray(np.asarray(L0_attn_Wqkv, np.float32).T)
    WoT = np.ascontiguousarray(np.asarray(L0_attn_Wout, np.float32).T)
    WA = np.asarray(L1_attn_w_attn, np.float32)
    WP = np.asarray(L1_attn_w_proj, np.float32)
    castcat = np.concatenate([WqT, WA, WoT, WP], axis=1)
    W0, W1, castb = run_prep(
        np.asarray(L0_ffn_W_q, np.int32),
        np.asarray(L0_ffn_zeros, np.float32).reshape(D, 32),
        np.asarray(L0_ffn_scales, np.float32).reshape(D, 32),
        np.asarray(L0_ffn_scale2, np.float32).reshape(D, 1),
        np.asarray(L0_ffn_mask, np.int32),
        np.asarray(L1_ffn_weight_q3, np.int8),
        np.asarray(L1_ffn_weight_norm, np.float32),
        castcat)
    WqTb = castb[:, :12288]
    WAb = castb[:, 12288:24576]
    WoTb = np.ascontiguousarray(castb[:, 24576:28672])
    WPb = np.ascontiguousarray(castb[:, 28672:])
    W0T = np.ascontiguousarray(W0.T)
    W1T = np.ascontiguousarray(W1.T)

    zeros_bias = np.zeros(D, np.float32)

    xb = [np.ascontiguousarray(x[b * S:(b + 1) * S].T) for b in range(2)]
    Qr, Kr, Vf = run_qkv(0, xb, WqTb, None)
    Q2, K2 = run_qkln(Qr, Kr, np.asarray(L0_attn_q_ln_w, np.float32),
                      np.asarray(L0_attn_k_ln_w, np.float32))
    Ofm = run_attn(Q2, K2, Vf,
                   np.asarray(L0_attn_q_ln_b, np.float32),
                   np.asarray(L0_attn_k_ln_w, np.float32))
    x1, x1b = run_proj(Ofm, WoTb, zeros_bias, x)
    aTs = [np.ascontiguousarray(x1b[c * TOK:(c + 1) * TOK].T) for c in range(NCORE)]
    x2, x2b = run_proj(aTs, W0T, np.asarray(L0_ffn_bias, np.float32), x1, is_ofm=False)

    xb2 = [np.ascontiguousarray(x2b[b * S:(b + 1) * S].T) for b in range(2)]
    Q1, K1, V1 = run_qkv(1, xb2, WAb, np.asarray(L1_attn_b_attn, np.float32))
    zz8 = np.zeros(D, np.float32)
    Ofm1 = run_attn(Q1, K1, V1, zz8, zz8)
    x3, x3b = run_proj(Ofm1, WPb, np.asarray(L1_attn_b_proj, np.float32), x2)
    aTs = [np.ascontiguousarray(x3b[c * TOK:(c + 1) * TOK].T) for c in range(NCORE)]
    x4, _ = run_proj(aTs, W1T, np.asarray(L1_ffn_bias, np.float32), x3, is_ofm=False)

    return x4.reshape(2, S, D).astype(np.float32)




GROUPS_B = [[0, 1, 2, 3], [4, 5, 6, 7]]
GROUPS_ALL = [[0, 1, 2, 3, 4, 5, 6, 7]]


class _Ctx:
    pass


def _mk_sems(nc, E, names):
    return {n: _Sem(E(nc.semaphore(n))) for n in names}


SEM_NAMES = ["sL", "sR", "sT", "sV", "sSc", "sE2", "sPV", "sO0", "sO1", "sMM",
             "sEV", "sW0", "sW1", "sS0", "sS1", "sS2", "sS3", "sX", "sC",
             "gW0", "gW1", "sAGW", "sAGX", "sAR", "sAGO", "cV", "cS",
             "sL2", "sR2", "sVa", "sVb"]


def _barrier(cx, exclude=()):
    snap = [(m.h, m.n) for nm, m in cx.sm.items() if m.n > 0 and nm not in exclude]
    for eng in ENGS:
        for h, v in snap:
            cx.sc.q[eng].append(("w", h, v))


def emit_cast_list(cx, items):
    nc, sc, E, sm = cx.nc, cx.sc, cx.E, cx.sm
    CW = [E(nc.sbuf_tensor(f"{cx.tag}CW{i}", [128, 32 * 512], BF16)) for i in range(2)]
    b_st = [sm["sS0"].n, sm["sS1"].n]
    for i, (src, dst) in enumerate(items):
        pr = i % 2
        sw = sm["gW0"] if pr == 0 else sm["gW1"]
        ss = sm["sS0"] if pr == 0 else sm["sS1"]
        if i >= 2:
            sc.w("gpsimd", ss, b_st[pr] + 16 * (i // 2))
        sc.o("gpsimd", lambda e, s=src, d=CW[pr]: e.dma_start(
            out=d[:].rearrange("p (i t) -> p i t", i=32),
            in_=s.rearrange("(i p) t -> p i t", p=128)), sw, 16)
        sc.w("sync", sw, sw.n)
        sc.o("sync", lambda e, d=dst, b=CW[pr]: e.dma_start(
            out=d.rearrange("(i p) t -> p i t", p=128),
            in_=b[:].rearrange("p (i t) -> p i t", i=32)), ss, 16)


def emit_dequant_fused(cx, io, do_tr=True, do_dq=True, do_st=True, do_pe=True):
    nc, sc, E, sm = cx.nc, cx.sc, cx.E, cx.sm
    t = cx.tag
    QF = E(nc.sbuf_tensor(t + "QF", [128, D], F32))
    MF = E(nc.sbuf_tensor(t + "MF", [128, D], F32))
    T0 = E(nc.sbuf_tensor(t + "T0", [128, D], F32))
    WD0 = E(nc.sbuf_tensor(t + "WD0", [128, D], BF16))
    ZS = E(nc.sbuf_tensor(t + "ZS", [128, 32], F32))
    SC_ = E(nc.sbuf_tensor(t + "SC", [128, 32], F32))
    S2b = E(nc.sbuf_tensor(t + "S2b", [128, 1], F32))
    SCC = E(nc.sbuf_tensor(t + "SCC", [128, 32], F32))
    BY = E(nc.sbuf_tensor(t + "BY", [128, 1536], I32))
    Q1 = E(nc.sbuf_tensor(t + "Q1", [128, D], I32))
    QF2 = E(nc.sbuf_tensor(t + "QF2", [128, D], F32))
    NR = E(nc.sbuf_tensor(t + "NR", [128, 128], F32))
    NE = E(nc.sbuf_tensor(t + "NE", [128, D], F32))
    TA = E(nc.sbuf_tensor(t + "TA", [128, 512], I32))
    TB = E(nc.sbuf_tensor(t + "TB", [128, 512], I32))
    T1 = E(nc.sbuf_tensor(t + "T1", [128, D], F32))
    WD1 = E(nc.sbuf_tensor(t + "WD1", [128, D], BF16))
    sL, sV, sT, sEV, sC = sm["sL"], sm["sV"], sm["sT"], sm["sEV"], sm["sC"]
    wq, zz, ss, s2, mk, q3, wn = (io["wq"], io["zz"], io["ss"], io["s2"],
                                  io["mk"], io["q3"], io["wn"])
    agw_in = io["agw_in"]

    vbase = sV.n
    import os as _os
    for r in range(int(_os.environ.get('DEQR', '4'))):
        r0, r1 = r * 128, (r + 1) * 128
        g0 = r * 16384
        if r > 0:
            sc.w("gpsimd", sV, vbase + 2 * r)
            sc.w("sync", sV, vbase + 2 * r)
        sX_ = sm["sX"]
        sc.o("gpsimd", lambda e, a=wq[r0:r1, :], o=QF: e.dma_start(out=o[:], in_=a), sX_, 16)
        sc.o("gpsimd", lambda e, a=mk[r0:r1, :], o=MF: e.dma_start(out=o[:], in_=a), sX_, 16)
        sc.o("gpsimd", lambda e, a=q3.bitcast(U8)[g0:g0 + 16384, :].rearrange("(p g) b -> p (g b)", p=128), o=BY: e.dma_start(out=o[:], in_=a), sX_, 16)
        sc.o("sync", lambda e, a=zz[r0:r1, :], o=ZS: e.dma_start(out=o[:], in_=a), sL, 16)
        sc.o("sync", lambda e, a=ss[r0:r1, :], o=SC_: e.dma_start(out=o[:], in_=a), sL, 16)
        sc.o("sync", lambda e, a=s2[r0:r1, :], o=S2b: e.dma_start(out=o[:], in_=a), sL, 16)
        sc.o("sync", lambda e, a=wn[g0:g0 + 16384, :].rearrange("(p g) o -> p (g o)", p=128), o=NR: e.dma_start(out=o[:], in_=a), sL, 16)
        sc.w("vector", sL, sL.n)
        sc.w("vector", sX_, sX_.n)
        if r > 0:
            sc.w("vector", sm["sC"], sm["sC"].n)
        if not do_dq:
            sc.o("vector", lambda e: e.tensor_copy(WD0[:], QF[:]), sV, 1)
            sc.o("vector", lambda e: e.tensor_copy(WD1[:], MF[:]), sV, 1)
        else:
            _dq_marker = 1
        if not do_dq:
            pass
        if do_dq:
            sc.o("vector", lambda e: e.tensor_scalar(SCC[:], SC_[:], S2b[:, 0:1], None, op0=ALU.mult))
        for g in range(32 if do_dq else 0):
            sc.o("vector", lambda e, g=g: e.tensor_scalar(
                T0[:, g * 128:(g + 1) * 128], QF[:, g * 128:(g + 1) * 128],
                ZS[:, g:g + 1], SCC[:, g:g + 1], op0=ALU.subtract, op1=ALU.mult))
        if do_dq:
            sc.o("vector", lambda e: e.tensor_tensor(WD0[:], T0[:], MF[:], op=ALU.mult), sV, 1)
        b0_, b1_, b2_ = BY[:, 0:1536:3], BY[:, 1:1536:3], BY[:, 2:1536:3]
        qs = lambda v: Q1[:, v:4096:8]
        if not do_dq:
            qs = None
        if do_dq:
            sc.o("vector", lambda e: e.tensor_single_scalar(qs(0), b0_, 7, op=ALU.bitwise_and))
            sc.o("vector", lambda e: e.tensor_scalar(qs(1), b0_, 3, 7, op0=ALU.logical_shift_right, op1=ALU.bitwise_and))
            sc.o("vector", lambda e: e.tensor_scalar(TA[:], b0_, 6, 3, op0=ALU.logical_shift_right, op1=ALU.bitwise_and))
            sc.o("vector", lambda e: e.tensor_scalar(TB[:], b1_, 1, 2, op0=ALU.bitwise_and, op1=ALU.logical_shift_left))
            sc.o("vector", lambda e: e.tensor_tensor(qs(2), TA[:], TB[:], op=ALU.add))
            sc.o("vector", lambda e: e.tensor_scalar(qs(3), b1_, 1, 7, op0=ALU.logical_shift_right, op1=ALU.bitwise_and))
            sc.o("vector", lambda e: e.tensor_scalar(qs(4), b1_, 4, 7, op0=ALU.logical_shift_right, op1=ALU.bitwise_and))
            sc.o("vector", lambda e: e.tensor_scalar(TA[:], b1_, 7, 1, op0=ALU.logical_shift_right, op1=ALU.bitwise_and))
            sc.o("vector", lambda e: e.tensor_scalar(TB[:], b2_, 3, 1, op0=ALU.bitwise_and, op1=ALU.logical_shift_left))
            sc.o("vector", lambda e: e.tensor_tensor(qs(5), TA[:], TB[:], op=ALU.add))
            sc.o("vector", lambda e: e.tensor_scalar(qs(6), b2_, 2, 7, op0=ALU.logical_shift_right, op1=ALU.bitwise_and))
            sc.o("vector", lambda e: e.tensor_scalar(qs(7), b2_, 5, 7, op0=ALU.logical_shift_right, op1=ALU.bitwise_and))
            sc.o("vector", lambda e: e.tensor_copy(QF2[:], Q1[:]))
            sc.o("vector", lambda e: e.tensor_copy(
                NE[:].rearrange("p (g j) -> p g j", j=32),
                NR[:].unsqueeze(2).broadcast_to((128, 128, 32))))
            sc.o("vector", lambda e: e.scalar_tensor_tensor(T1[:], QF2[:], 2.0 / 7.0, NE[:], op0=ALU.mult, op1=ALU.mult))
            sc.o("vector", lambda e: e.tensor_tensor(WD1[:], T1[:], NE[:], op=ALU.subtract), sV, 1)
        sc.w("sync", sV, sV.n)
        sc.o("sync", lambda e, d=io["wdq0"][r0:r1, :]: e.dma_start(out=d, in_=WD0[:]), sC, 16)
        sc.o("sync", lambda e, d=io["wdq1"][r0:r1, :]: e.dma_start(out=d, in_=WD1[:]), sC, 16)

    XT = [E(nc.sbuf_tensor(t + f"XT{i}", [128, 512], BF16)) for i in range(2)]
    sc.w("sync", sC, sC.n)
    sW = [sm["sW0"], sm["sW1"]]
    sS = [sm["sS0"], sm["sS1"]]
    sb = [sm["sS0"].n, sm["sS1"].n]
    idx = 0
    for mi, wsrc in ((0, io["wdq0"]), (1, io["wdq1"])):
        for ncb in range(32):
            pr = idx % 2
            if idx >= 2:
                sc.w("sync", sS[pr], sb[pr] + 16 * (idx // 2))
            sc.o("sync", lambda e, a=wsrc[0:512, ncb * 128:(ncb + 1) * 128], o=XT[pr]:
                 e.dma_start(out=o[:], in_=a, transpose=True), sW[pr], 16)
            sc.w("sync", sW[pr], sW[pr].n)
            dstp = agw_in[(2 + mi) * 4096 + ncb * 128:(2 + mi) * 4096 + (ncb + 1) * 128, :]
            sc.o("sync", lambda e, d=dstp, o=XT[pr]: e.dma_start(out=d, in_=o[:]), sS[pr], 16)
            idx += 1


def emit_tm2fm(cx, src, dst, R, C, cast, pre_bounced=False, bounce_only=False):
    nc, sc, E, sm = cx.nc, cx.sc, cx.E, cx.sm
    t = cx.tag
    nch = R // 128
    ndc = C // 128
    XB = [E(nc.sbuf_tensor(t + f"XB{i}", [128, C], BF16)) for i in range(2)]
    XT = [E(nc.sbuf_tensor(t + f"XT{i}", [128, R], BF16)) for i in range(2)]
    sX, sC = sm["sX"], sm["sC"]
    sW = [sm["sW0"], sm["sW1"]]
    sS = [sm["sS0"], sm["sS1"]]
    bm = cx.tmb
    cb0 = sC.n
    if pre_bounced:
        nch = 0
    for tc in range(nch):
        pr = tc % 2
        if tc >= 2:
            sc.w("gpsimd", sC, cb0 + 16 * (tc - 1))
        sc.o("gpsimd", lambda e, a=src[tc * 128:(tc + 1) * 128, :], o=XB[pr]:
             e.dma_start(out=o[:], in_=a), sX, 16)
        sc.w("sync", sX, sX.n)
        sc.o("sync", lambda e, d=bm[tc * 128:(tc + 1) * 128, 0:C], o=XB[pr]:
             e.dma_start(out=d, in_=o[:]), sC, 16)
    sc.w("sync", sC, sC.n)
    if bounce_only:
        ndc = 0
    sb = [sm["sS0"].n, sm["sS1"].n]
    for dc in range(ndc):
        pr = dc % 2
        if dc >= 2:
            sc.w("sync", sS[pr], sb[pr] + 16 * (dc // 2))
        sc.o("sync", lambda e, a=bm[0:R, dc * 128:(dc + 1) * 128], o=XT[pr]:
             e.dma_start(out=o[:, 0:R], in_=a, transpose=True), sW[pr], 16)
        sc.w("sync", sW[pr], sW[pr].n)
        sc.o("sync", lambda e, d=dst[dc * 128:(dc + 1) * 128, 0:R], o=XT[pr]:
             e.dma_start(out=d, in_=o[:, 0:R]), sS[pr], 16)


def emit_qkv_fused(cx, layer, io):
    nc, sc, E, sm = cx.nc, cx.sc, cx.E, cx.sm
    t = cx.tag
    XH = [E(nc.sbuf_tensor(t + f"XH{i}", [128, 16 * 1024], BF16)) for i in range(3)]
    W16 = [E(nc.sbuf_tensor(t + f"W16{i}", [128, 32 * 512], BF16)) for i in range(2)]
    B16 = E(nc.sbuf_tensor(t + "B16", [1, 3 * EH], BF16))
    EVF = [E(nc.sbuf_tensor(t + f"EVF{i}", [128, 512], F32)) for i in range(4)]
    EVB = [E(nc.sbuf_tensor(t + f"EVB{i}", [128, 512], BF16)) for i in range(4)]
    PS = [E(nc.psum_tensor(t + f"PS{i}", [128, 512], F32)) for i in range(4)]
    sX, sMM, sEV = sm["sX"], sm["sMM"], sm["sEV"]
    sW = [sm["sW0"], sm["sW1"]]
    sST = [sm["sS0"], sm["sS1"], sm["sS2"], sm["sS3"]]
    sLq = [sm["sL"], sm["sL2"]]
    agxt, wT = io["agxt"], io["wT"]
    outs = io["outs"]
    stats = layer == 0
    if stats:
        STATS = E(nc.sbuf_tensor(t + "STATS", [128, 128], F32))
        SQS = E(nc.sbuf_tensor(t + "SQS", [128, 512], F32))
        ACC1 = E(nc.sbuf_tensor(t + "ACC1", [128, 1], F32))
        ACC2 = E(nc.sbuf_tensor(t + "ACC2", [128, 1], F32))
        sc.o("vector", lambda e: e.memset(STATS[:], 0.0))

    xb = sX.n
    bias_off = 0
    if layer == 1:
        sc.o("gpsimd", lambda e: e.dma_start(out=B16[:], in_=io["b3"]), sX, 16)
        bias_off = 16
    wb_base = [sm["gW0"].n, sm["gW1"].n]
    mmb = sMM.n
    evb = sEV.n
    stb = [x.n for x in sST]
    scb = sm["sSc"].n

    gW = [sm["gW0"], sm["gW1"]]

    def wload(w):
        tg, et = divmod(w, 6)
        src = wT[:, et * 512:(et + 1) * 512].rearrange("(i p) e -> p i e", p=128)
        dstw = W16[w % 2][:].rearrange("p (i e) -> p i e", i=32)
        if w >= 2:
            sc.w("gpsimd", sMM, mmb + (w - 1) * 8)
        sc.o("gpsimd", lambda e, s=src, d=dstw: e.dma_start(out=d, in_=s), gW[w % 2], 16)

    xcnt = {}

    def xload(s):
        tg_, h_ = divmod(s, 2)
        buf = s % 3
        if s >= 3:
            sc.w("sync", sMM, mmb + 48 * ((s - 3) // 2 + 1))
        for dd in range(16):
            d_ = h_ * 16 + dd
            sc.o("sync", lambda e, tg_=tg_, d_=d_, buf=buf, dd=dd: e.dma_start(
                out=XH[buf][:, dd * 1024:(dd + 1) * 1024],
                in_=agxt[tg_ * 1024:(tg_ + 1) * 1024, d_ * 128:(d_ + 1) * 128],
                transpose=True), sLq[s % 2], 16)
        xcnt[s] = sLq[s % 2].n

    wload(0)
    wload(1)
    xload(0)
    xload(1)
    xload(2)
    n = 0
    for tg in range(4):
        for et in range(6):
            w = tg * 6 + et
            if bias_off:
                sc.w("tensor", sX, xb + bias_off)
            sc.w("tensor", sLq[(2 * tg) % 2], xcnt[2 * tg])
            if 2 * tg + 1 in xcnt:
                sc.w("tensor", sLq[(2 * tg + 1) % 2], xcnt[2 * tg + 1])
            sc.w("tensor", gW[w % 2], wb_base[w % 2] + 16 * (w // 2 + 1))
            if et == 4 and 2 * tg + 2 <= 7:
                xload(2 * tg + 2)
            if et == 5 and 2 * tg + 3 <= 7:
                xload(2 * tg + 3)
            sec = et // 2
            ecol = (et % 2) * 512
            for tc in range(8):
                n += 1
                bank = (n - 1) % 4
                if n > 4:
                    sc.w("tensor", sEV, evb + n - 4)
                for d in range(32):
                    last = (d == 31) and (layer == 0)
                    hb = (2 * tg + d // 16) % 3
                    sc.o("tensor", lambda e, d=d, b=bank, tc=tc, st=(d == 0), sp=last, wp=w % 2, hb=hb:
                         e.matmul(PS[b][:], XH[hb][:, (d % 16) * 1024 + tc * 128: (d % 16) * 1024 + tc * 128 + 128],
                                  W16[wp][:, d * 512:(d + 1) * 512], start=st, stop=sp),
                         sMM if last else None, 1)
                if layer == 1:
                    sc.o("tensor", lambda e, b=bank, c=et * 512:
                         e.matmul(PS[b][:], cx.ONESB[0:1, :], B16[0:1, c:c + 512], start=False, stop=True),
                         sMM, 1)
                sc.w("vector", sMM, mmb + n)
                if n > 4:
                    sc.w("vector", sST[bank], stb[bank] + 16 * ((n - 4 - 1) // 4 + 1))
                    if stats:
                        sc.w("vector", sm["sSc"], scb + max(0, _nsq(n - 4)))
                if layer == 0:
                    tgt = EVF[bank] if sec < 2 else EVB[bank]
                    sc.o("vector", lambda e, b=bank, tt=tgt: e.tensor_scalar(
                        tt[:], PS[b][:], -8.0, 8.0, op0=ALU.max, op1=ALU.min), sEV, 1)
                    if sec < 2 and stats:
                        sc.o("vector", lambda e, tt=tgt: e.reduce_sum(
                            ACC1[:], tt[:], axis=AX.X))
                else:
                    tgt = EVB[bank]
                    if sec == 0:
                        sc.o("vector", lambda e, b=bank, tt=tgt: e.tensor_scalar(
                            tt[:], PS[b][:], 1.0 / 128.0, None, op0=ALU.mult), sEV, 1)
                    else:
                        sc.o("vector", lambda e, b=bank, tt=tgt: e.tensor_copy(tt[:], PS[b][:]), sEV, 1)
                if stats and sec < 2:
                    tcg = tg * 8 + tc
                    c0 = tcg * 4 + (0 if sec == 0 else 2)
                    sc.o("vector", lambda e, c0=c0: e.tensor_tensor(
                        STATS[:, c0:c0 + 1], STATS[:, c0:c0 + 1], ACC1[:], op=ALU.add))
                    sc.w("scalar", sEV, evb + n)
                    sq = sc.o("scalar", lambda e, b=bank, tt=tgt: e.activation(
                        SQS[:], tt[:], AF.Square, accum_out=ACC2[:]), sm["sSc"], 1)
                    sc.w("vector", sm["sSc"], sq)
                    sc.o("vector", lambda e, c0=c0: e.tensor_tensor(
                        STATS[:, c0 + 1:c0 + 2], STATS[:, c0 + 1:c0 + 2], ACC2[:], op=ALU.add))
                sc.w("sync", sEV, sEV.n)
                t0 = tg * 1024 + tc * 128
                sc.o("sync", lambda e, o=outs[sec][t0:t0 + 128, ecol:ecol + 512], tt=tgt:
                     e.dma_start(out=o, in_=tt[:]), sST[bank], 16)
            if w + 2 < 24:
                wload(w + 2)
    if stats:
        vs = sc.o("vector", lambda e: e.engine_nop(), sm["sV"], 1)
        sc.w("sync", sm["sV"], vs)
        sc.o("sync", lambda e: e.dma_start(out=io["ar_in"], in_=STATS[:]), sm["sL"], 16)
        sc.w("gpsimd", sm["sL"], sm["sL"].n)
        sc.o("gpsimd", lambda e: e.collective_compute(
            "AllReduce", ALU.add, GROUPS_B, ins=[io["ar_in"]], outs=[io["ar_out"]]),
            sm["sAR"], 1)


def _nsq(k):
    full, rem = divmod(k, 48)
    cnt = full * 32
    cnt += min(rem, 32)
    return cnt


def emit_qkfin_fused(cx, layer, io):
    nc, sc, E, sm = cx.nc, cx.sc, cx.E, cx.sm
    t = cx.tag
    sL, sT, sV, sEV = sm["sL"], sm["sT"], sm["sV"], sm["sEV"]
    sW = [sm["sW0"], sm["sW1"]]
    sS = [sm["sS0"], sm["sS1"]]
    sC = sm["sC"]
    qtm, ktm, q2fm, k2fm = io["qtm"], io["ktm"], io["q2fm"], io["k2fm"]

    if layer == 0:
        ARS = E(nc.sbuf_tensor(t + "ARS", [128, 128], F32))
        GQ = E(nc.sbuf_tensor(t + "GQ", [1, EH], F32))
        GK = E(nc.sbuf_tensor(t + "GK", [1, EH], F32))
        G1 = E(nc.sbuf_tensor(t + "G1", [1, EH], F32))
        G16 = E(nc.sbuf_tensor(t + "G16", [1, EH], BF16))
        GBC = E(nc.sbuf_tensor(t + "GBC", [128, EH], F32))
        MU = E(nc.sbuf_tensor(t + "MU", [128, 1], F32))
        VAR = E(nc.sbuf_tensor(t + "VAR", [128, 1], F32))
        TT1 = E(nc.sbuf_tensor(t + "TT1", [128, 1], F32))
        SD = E(nc.sbuf_tensor(t + "SD", [128, 1], F32))
        RSD = E(nc.sbuf_tensor(t + "RSD", [128, 1], F32))
        MUK = E(nc.sbuf_tensor(t + "MUK", [128, 1], F32))
        RSDK = E(nc.sbuf_tensor(t + "RSDK", [128, 1], F32))
        QC = E(nc.sbuf_tensor(t + "QC", [128, EH], F32))
        KC = E(nc.sbuf_tensor(t + "KC", [128, EH], F32))
        TAP = E(nc.sbuf_tensor(t + "TAP", [128, EH], F32))
        Q2C = E(nc.sbuf_tensor(t + "Q2C", [128, EH], BF16))
        K2C = E(nc.sbuf_tensor(t + "K2C", [128, EH], BF16))
        PSG = E(nc.psum_tensor(t + "PSG", [128, 512], F32))
        qbm, kbm = io["qbm"], io["kbm"]

        sc.w("sync", sm["sAR"], sm["sAR"].n)
        sc.o("sync", lambda e: e.dma_start(out=ARS[:], in_=io["ar_out"]), sL, 16)
        sc.o("sync", lambda e: e.dma_start(out=GQ[:], in_=io["gq"]), sL, 16)
        sc.o("sync", lambda e: e.dma_start(out=GK[:], in_=io["gk"]), sL, 16)
        sc.w("vector", sL, sL.n)
        sc.o("vector", lambda e: e.scalar_tensor_tensor(G1[:], GQ[:], ISQ, GK[:], op0=ALU.mult, op1=ALU.mult))
        sc.o("vector", lambda e: e.tensor_copy(G16[:], G1[:]), sV, 1)
        for c_ in range(2):
            sc.w("tensor", sV, sV.n)
            sc.o("tensor", lambda e, c_=c_: e.matmul(PSG[:], cx.ONESB[0:1, :], G16[0:1, c_ * 512:(c_ + 1) * 512],
                                                     start=True, stop=True), sT, 1)
            sc.w("vector", sT, sT.n)
            sc.o("vector", lambda e, c_=c_: e.tensor_copy(GBC[:, c_ * 512:(c_ + 1) * 512], PSG[:]), sV, 1)

        MU32 = E(nc.sbuf_tensor(t + "MU32", [128, 32], F32))
        RS32 = E(nc.sbuf_tensor(t + "RS32", [128, 32], F32))
        MK32 = E(nc.sbuf_tensor(t + "MK32", [128, 32], F32))
        RK32 = E(nc.sbuf_tensor(t + "RK32", [128, 32], F32))
        VT32 = E(nc.sbuf_tensor(t + "VT32", [128, 32], F32))
        SD32 = E(nc.sbuf_tensor(t + "SD32", [128, 32], F32))
        for mu_, rs_, c0_ in ((MU32, RS32, 0), (MK32, RK32, 2)):
            sc.o("vector", lambda e, mu_=mu_, c0_=c0_: e.tensor_scalar(
                mu_[:], ARS[:, c0_:128:4], 1.0 / 4096.0, None, op0=ALU.mult))
            sc.o("vector", lambda e, c0_=c0_: e.tensor_scalar(
                VT32[:], ARS[:, c0_ + 1:128:4], 1.0 / 4096.0, None, op0=ALU.mult))
            sc.o("vector", lambda e, mu_=mu_: e.tensor_tensor(SD32[:], mu_[:], mu_[:], op=ALU.mult))
            vv = sc.o("vector", lambda e: e.tensor_tensor(VT32[:], VT32[:], SD32[:], op=ALU.subtract), sV, 1)
            sc.w("scalar", sV, vv)
            sq = sc.o("scalar", lambda e: e.activation(SD32[:], VT32[:], AF.Sqrt, bias=cx.EPSC[:], scale=1.0), sm["sSc"], 1)
            sc.w("vector", sm["sSc"], sq)
            sc.o("vector", lambda e, rs_=rs_: e.reciprocal(rs_[:], SD32[:]), sV, 1)

        cb0 = sC.n
        prev_vq = None
        for tcg in range(32):
            r0 = tcg * 128
            if prev_vq is not None:
                sc.w("sync", sV, prev_vq)
            if tcg >= 1:
                sc.w("sync", sC, cb0 + 32 * tcg)
            sc.o("sync", lambda e, a=qtm[r0:r0 + 128, :]: e.dma_start(out=QC[:], in_=a), sL, 16)
            sc.o("sync", lambda e, a=ktm[r0:r0 + 128, :]: e.dma_start(out=KC[:], in_=a), sL, 16)
            sc.w("vector", sL, sL.n)
            sc.o("vector", lambda e, tcg=tcg: e.tensor_scalar(
                TAP[:], QC[:], MU32[:, tcg:tcg + 1], RS32[:, tcg:tcg + 1], op0=ALU.subtract, op1=ALU.mult))
            sc.o("vector", lambda e: e.tensor_tensor(Q2C[:], TAP[:], GBC[:], op=ALU.mult))
            vq = sc.o("vector", lambda e, tcg=tcg: e.tensor_scalar(
                K2C[:], KC[:], MK32[:, tcg:tcg + 1], RK32[:, tcg:tcg + 1], op0=ALU.subtract, op1=ALU.mult), sV, 1)
            prev_vq = vq
            sc.w("sync", sV, vq)
            sc.o("sync", lambda e, d=qbm[r0:r0 + 128, :]: e.dma_start(out=d, in_=Q2C[:]), sC, 16)
            sc.o("sync", lambda e, d=kbm[r0:r0 + 128, :]: e.dma_start(out=d, in_=K2C[:]), sC, 16)
        sc.w("sync", sC, sC.n)


def emit_heat(cx, ps, n):
    for _ in range(n):
        cx.sc.o("tensor", lambda e, ps=ps: e.matmul(ps, cx.IDB[:], cx.HEATR[:], start=True, stop=True))


def emit_attn_fused(cx, layer, io):
    nc, sc, E, sm = cx.nc, cx.sc, cx.E, cx.sm
    t = cx.tag
    QF = [E(nc.sbuf_tensor(t + f"QF{i}", [128, S], BF16)) for i in range(3)]
    KF = [E(nc.sbuf_tensor(t + f"KF{i}", [128, S], BF16)) for i in range(3)]
    VSB = [E(nc.sbuf_tensor(t + f"VSB{i}", [128, S], BF16)) for i in range(2)]
    KS5 = [E(nc.sbuf_tensor(t + f"KS5{i}", [5, S], BF16)) for i in range(2)]
    QROW3 = [E(nc.sbuf_tensor(t + f"QROW3{i}", [3, S], BF16)) for i in range(2)]
    KSCA = [E(nc.sbuf_tensor(t + f"KSCA{i}", [128, 32], F32)) for i in range(2)]
    KSC2 = [E(nc.sbuf_tensor(t + f"KSC2{i}", [128, 32], F32)) for i in range(3)]
    KSCF = [E(nc.sbuf_tensor(t + f"KSCF{i}", [128, 32], F32)) for i in range(2)]
    CKHc = [E(nc.sbuf_tensor(t + f"CKHc{i}", [128, 32], BF16)) for i in range(2)]
    CKLc = [E(nc.sbuf_tensor(t + f"CKLc{i}", [128, 32], BF16)) for i in range(2)]
    THs = [E(nc.sbuf_tensor(t + f"THs{i}", [128, 128], BF16)) for i in range(2)]
    TLs = [E(nc.sbuf_tensor(t + f"TLs{i}", [128, 128], BF16)) for i in range(2)]
    UB = E(nc.sbuf_tensor(t + "UB", [8, 128], F32))
    UW = E(nc.sbuf_tensor(t + "UW", [8, 128], F32))
    U8 = E(nc.sbuf_tensor(t + "U8", [8, 128], F32))
    UCF = E(nc.sbuf_tensor(t + "UCF", [128, 8], F32))
    UCB = E(nc.sbuf_tensor(t + "UCB", [128, 8], BF16))
    NMB = [E(nc.sbuf_tensor(t + f"NMB{i}", [128, 4], F32)) for i in range(2)]
    SGB = [E(nc.sbuf_tensor(t + f"SGB{i}", [128, 4], F32)) for i in range(2)]
    SGW = [E(nc.sbuf_tensor(t + f"SGW{i}", [128, 4], F32)) for i in range(2)]
    EXB = [E(nc.sbuf_tensor(t + f"EXB{i}", [128, 4], F32)) for i in range(2)]
    NMc = [E(nc.sbuf_tensor(t + f"NMc{i}", [128, 1], F32)) for i in range(2)]
    SGc = [E(nc.sbuf_tensor(t + f"SGc{i}", [128, 1], F32)) for i in range(2)]
    LNS = [E(nc.sbuf_tensor(t + f"LNS{i}", [128, 1], F32)) for i in range(2)]
    QAB = [E(nc.sbuf_tensor(t + f"QAB{i}", [128, 4], F32)) for i in range(2)]
    QA3G = [E(nc.sbuf_tensor(t + f"QA3G{i}", [128, 12], F32)) for i in range(2)]
    QH4 = [E(nc.sbuf_tensor(t + f"QH4{i}", [128, 4], BF16)) for i in range(2)]
    QR4 = [E(nc.sbuf_tensor(t + f"QR4{i}", [128, 4], F32)) for i in range(2)]
    ESC = E(nc.sbuf_tensor(t + "ESC", [128, 1024], BF16))
    Z4 = E(nc.sbuf_tensor(t + "Z4", [128, 4], F32))
    ZPAD = E(nc.sbuf_tensor(t + "ZPAD", [128, 96], BF16))
    PTS = [E(nc.sbuf_tensor(t + f"PTS{i}", [128, 512], BF16)) for i in range(4)]
    OEV = [E(nc.sbuf_tensor(t + f"OEV{i}", [128, 512], BF16)) for i in range(2)]
    PALL = E(nc.psum_tensor(t + "PALL", [128, 4096], F32))
    sT, sV, sSc = sm["sT"], sm["sV"], sm["sSc"]
    sE2, sPV = sm["sE2"], sm["sPV"]
    sO = [sm["sO0"], sm["sO1"]]
    sLq = [sm["sL"], sm["sL2"]]
    sRr = [sm["sR"], sm["sR2"]]
    sVv = [sm["sVa"], sm["sVb"]]
    q2fm, k2fm, vtm = io["q2fm"], io["k2fm"], io["vtm"]
    agoi, ago8 = io["agoi"], io["ago8"]
    ckd = io["ckd"]
    ahi, ami, alo, acolt = io["ahi"], io["ami"], io["alo"], io["acolt"]

    b7 = {"v": 0}
    ldQ = [0] * 8
    ldR = [0] * 8
    ldV = [0] * 8
    ckst = [0] * 8
    ckst2 = [0] * 8
    ksAdd = [0] * 8
    exps_last = [0] * 8
    oa = [[0, 0] for _ in range(8)]
    pv_after = [0] * 8
    e2_after = [0] * 8
    qrow_ev = [[0] * 8 for _ in range(8)]

    sc.o("vector", lambda e: e.memset(Z4[:], 0.0))
    zv = sc.o("vector", lambda e: e.memset(ZPAD[:], 0.0), sV, 1)
    sc.w("sync", sV, zv)
    for pi_ in range(2):
        sc.o("sync", lambda e, pi_=pi_: e.dma_start(out=ckd[pi_]["h1"][:, 32:128], in_=ZPAD[:]), sRr[pi_], 16)
        sc.o("sync", lambda e, pi_=pi_: e.dma_start(out=ckd[pi_]["l1"][:, 32:128], in_=ZPAD[:]), sRr[pi_], 16)
    sc.o("sync", lambda e: e.dma_start(out=UB[:], in_=io["ub8"]), sLq[0], 16)
    sc.o("sync", lambda e: e.dma_start(out=UW[:], in_=io["uw8"]), sLq[0], 16)
    sc.w("vector", sLq[0], sLq[0].n)
    sc.o("vector", lambda e: e.scalar_tensor_tensor(
        U8[:], UB[:], ISQ, UW[:], op0=ALU.mult, op1=ALU.mult), sV, 1)
    sc.w("tensor", sV, sV.n)
    sc.o("tensor", lambda e: e.transpose(PALL[0:128, 3584:3592], U8[:], cx.IDF[0:8, 0:8]), sT, 1)
    sc.w("vector", sT, sT.n)
    sc.o("vector", lambda e: e.tensor_copy(UCF[:], PALL[0:128, 3584:3592]))
    b7["v"] = sc.o("vector", lambda e: e.tensor_copy(UCB[:], UCF[:]), sV, 1)

    def loads_qk(p, gates=()):
        pi = p % 2
        for s_, v_ in gates:
            sc.w("sync", s_, v_)
        sc.o("sync", lambda e, p=p: e.dma_start(
            out=QF[p % 3][:], in_=q2fm[0:S, p * 128:(p + 1) * 128], transpose=True), sLq[pi], 16)
        sc.o("sync", lambda e, p=p: e.dma_start(
            out=KF[p % 3][:], in_=k2fm[0:S, p * 128:(p + 1) * 128], transpose=True), sLq[pi], 16)
        sc.o("sync", lambda e, p=p: e.dma_start(out=KSCA[pi][:], in_=acolt[p]), sLq[pi], 16)
        ldQ[p] = sLq[pi].n
        sc.o("sync", lambda e, p=p: e.dma_start(out=KS5[pi][0:1, :], in_=ahi[p:p + 1, :]), sRr[pi], 16)
        sc.o("sync", lambda e, p=p: e.dma_start(out=KS5[pi][1:2, :], in_=ami[p:p + 1, :]), sRr[pi], 16)
        sc.o("sync", lambda e, p=p: e.dma_start(out=KS5[pi][2:3, :], in_=alo[p:p + 1, :]), sRr[pi], 16)

    def loads_v(p, gates=()):
        pi = p % 2
        for s_, v_ in gates:
            sc.w("sync", s_, v_)
        sc.o("sync", lambda e, p=p: e.dma_start(
            out=VSB[pi][:].rearrange("p (kc d) -> p kc d", kc=32),
            in_=vtm[:, p * 128:(p + 1) * 128].rearrange("(kc p) d -> p kc d", p=128)), sVv[pi], 16)
        ldV[p] = sVv[pi].n

    def setup(p):
        pi = p % 2
        sc.w("tensor", sLq[pi], ldQ[p])
        sc.w("tensor", sV, b7["v"])
        for kc in range(32):
            sc.o("tensor", lambda e, kc=kc, p=p: e.matmul(
                PALL[0:128, 3840 + kc:3841 + kc], KF[p % 3][:, kc * 128:(kc + 1) * 128],
                UCB[:, p:p + 1], start=True, stop=True),
                sT if kc == 31 else None, 1)
        tpe = sT.n
        sc.w("vector", sT, tpe)
        if p >= 2:
            sc.w("vector", sRr[pi], ckst[p - 2])
        sc.o("vector", lambda e, pi=pi: e.tensor_copy(KSCF[pi][:], PALL[0:128, 3840:3872]))
        if p >= 3:
            sc.w("vector", sE2, e2_after[p - 3])
        vts = sc.o("vector", lambda e, p=p, pi=pi: e.tensor_tensor(
            KSC2[p % 3][:], PALL[0:128, 3840:3872], KSCA[pi][:], op=ALU.add), sV, 1)
        b7["v"] = vts
        ksAdd[p] = vts
        sc.o("vector", lambda e, pi=pi: e.tensor_copy(CKHc[pi][:], KSCF[pi][:]))
        vck = sc.o("vector", lambda e, pi=pi: e.scalar_tensor_tensor(
            CKLc[pi][:], KSCF[pi][:], 1.0, CKHc[pi][:], op0=ALU.mult, op1=ALU.subtract), sV, 1)
        sc.w("sync", sV, vck)
        if p >= 2:
            sc.w("sync", sRr[pi], ckst2[p - 2])
        sc.o("sync", lambda e, pi=pi: e.dma_start(out=ckd[pi]["h1"][:, 0:32], in_=CKHc[pi][:]), sRr[pi], 16)
        sc.o("sync", lambda e, pi=pi: e.dma_start(out=ckd[pi]["l1"][:, 0:32], in_=CKLc[pi][:]), sRr[pi], 16)
        ckst[p] = sRr[pi].n
        sc.w("sync", sRr[pi], sRr[pi].n)
        sc.o("sync", lambda e, pi=pi: e.dma_start(out=THs[pi][:], in_=ckd[pi]["h1"], transpose=True), sRr[pi], 16)
        sc.o("sync", lambda e, pi=pi: e.dma_start(out=TLs[pi][:], in_=ckd[pi]["l1"], transpose=True), sRr[pi], 16)
        sc.w("sync", sRr[pi], sRr[pi].n)
        sc.o("sync", lambda e, pi=pi: e.dma_start(out=ckd[pi]["h2"], in_=THs[pi][0:32, :]), sRr[pi], 16)
        sc.o("sync", lambda e, pi=pi: e.dma_start(out=ckd[pi]["l2"], in_=TLs[pi][0:32, :]), sRr[pi], 16)
        sc.w("sync", sRr[pi], sRr[pi].n)
        sc.o("sync", lambda e, pi=pi: e.dma_start(
            out=KS5[pi][3:4, :], in_=ckd[pi]["h2"].rearrange("(o c) w -> o (c w)", o=1)), sRr[pi], 16)
        sc.o("sync", lambda e, pi=pi: e.dma_start(
            out=KS5[pi][4:5, :], in_=ckd[pi]["l2"].rearrange("(o c) w -> o (c w)", o=1)), sRr[pi], 16)
        ldR[p] = sRr[pi].n
        ckst2[p] = sRr[pi].n

    p2 = {"t": 0, "pvc": [], "e2c": [], "vo": 0, "pend": None, "pvq": []}

    def p2_evac_flush():
        if p2["pend"] is None:
            return
        ph, qg, ob, pvend = p2["pend"]
        p2["pend"] = None
        sc.w("vector", sPV, pvend)
        if sO[ob].n > 0:
            sc.w("vector", sO[ob], sO[ob].n)
        vo = sc.o("vector", lambda e, ob=ob: e.tensor_copy(OEV[ob][:], PALL[:, 3072:3584]), sV, 1)
        p2["vo"] = vo
        sc.w("sync", sV, vo)
        sc.o("sync", lambda e, ph=ph, qg=qg, ob=ob: e.dma_start(
            out=agoi[ph][:, qg * 512:(qg + 1) * 512], in_=OEV[ob][:]), sO[ob], 16)
        oa[ph] = [sO[0].n, sO[1].n]

    def p2_score(ph, qg, kc):
        if kc == 0:
            sc.w("tensor", sV, qrow_ev[ph][qg])
            if qg == 0:
                sc.w("scalar", sV, ksAdd[ph])
        tg = p2["t"]
        p2["t"] += 1
        slot = tg % 2
        pslot = tg % 4
        po = 2048 + slot * 512
        if tg >= 2:
            sc.w("tensor", sE2, p2["e2c"][tg - 2])
        bq, bk, br = ph % 3, ph % 3, ph % 2
        sc.o("tensor", lambda e, kc=kc, qg=qg, po=po, bk=bk, bq=bq: e.matmul(
            PALL[:, po:po + 512], KF[bk][:, kc * 128:kc * 128 + 128],
            QF[bq][:, qg * 512:(qg + 1) * 512], start=True, stop=False))
        c2 = kc * 128 - qg * 512
        if c2 >= 0:
            sc.o("tensor", lambda e, po=po, c2=c2: e.matmul(
                PALL[:, po:po + 512], cx.IDB[:],
                cx.TRITB[:, 384 - c2:384 - c2 + 512], start=False, stop=False))
        tmm = sc.o("tensor", lambda e, qg=qg, po=po, br=br: e.matmul(
            PALL[:, po:po + 512], cx.ONESB[0:3, :],
            QROW3[br][0:3, qg * 512:(qg + 1) * 512], start=False, stop=True), sT, 1)
        sc.w("scalar", sT, tmm)
        if tg >= 4:
            sc.w("scalar", sPV, p2["pvc"][tg - 4])
        sex = sc.o("scalar", lambda e, pslot=pslot, po=po, kc=kc, ph=ph: e.activation(
            PTS[pslot][:], PALL[:, po:po + 512], AF.Exp,
            bias=KSC2[ph % 3][:, kc:kc + 1], scale=1.0), sE2, 1)
        p2["e2c"].append(sex)
        p2["pvq"].append((ph, qg, kc, pslot, sex))

    def p2_pop_pv():
        ph, qg, kc, pslot, e2c = p2["pvq"].pop(0)
        kcmax = qg * 4 + 4
        if kc == 0:
            p2_evac_flush()
            if p2["vo"]:
                sc.w("tensor", sV, p2["vo"])
            sc.w("tensor", sVv[ph % 2], ldV[ph])
        sc.w("tensor", sE2, e2c)
        p2["pvc"].append(sc.o("tensor", lambda e, kc=kc, pslot=pslot, bv2=ph % 2,
                              st=(kc == 0), sp=(kc == kcmax - 1): e.matmul(
            PALL[:, 3072:3584], VSB[bv2][:, kc * 128:kc * 128 + 128],
            PTS[pslot][:], start=st, stop=sp), sPV, 1))
        if kc == kcmax - 1:
            p2["pend"] = (ph, qg, qg % 2, sPV.n)
            if qg == 7:
                pv_after[ph] = sPV.n
                e2_after[ph] = sE2.n

    def fire_ag(ph):
        sc.w("gpsimd", sO[0], sO[0].n)
        sc.w("gpsimd", sO[1], sO[1].n)
        sc.o("gpsimd", lambda e, ph=ph: e.collective_compute(
            "AllGather", ALU.bypass, GROUPS_B, ins=[agoi[ph]], outs=[ago8[ph]]),
            sm["sAGO"], 1)

    loads_qk(0)
    loads_v(0)
    setup(0)
    nmb_scgate = [0, 0]
    cmb_dvgate = [0, 0]
    qa3_pe = [0, 0]
    for p in range(9):
        if p < 8:
            if p + 1 < 8:
                g_ = []
                if p - 2 >= 0:
                    g_ += [(sO[0], oa[p - 2][0]), (sO[1], oa[p - 2][1]),
                           (sSc, exps_last[p - 2])]
                if p - 1 >= 0:
                    g_ += [(sSc, exps_last[p - 1]), (sV, ksAdd[p - 1])]
                loads_qk(p + 1, g_)
            if p >= 1:
                loads_v(p, [(sPV, pv_after[p - 2])] if p >= 2 else ())
            exps = []
            u = 0
            pendg = None
            tlist = [(qg_, kc_) for qg_ in range(8) for kc_ in range(qg_ * 4 + 4)] if p >= 1 else []
            tidx = [0]
            for qt in range(32):
                pi2 = qt % 2
                kext = (qt + 1) * 128
                blocks = [(o2, min(o2 + 1024, kext)) for o2 in range(0, kext, 1024)]
                nb = len(blocks)
                if nmb_scgate[pi2]:
                    sc.w("vector", sSc, nmb_scgate[pi2])
                if cmb_dvgate[pi2]:
                    sc.w("scalar", sV, cmb_dvgate[pi2])
                for bi, (b0, b1) in enumerate(blocks):
                    wb = b1 - b0
                    base = (u % 2) * 1024
                    if u >= 2:
                        s_, c_ = exps[u - 2]
                        sc.w("tensor", s_, c_)
                    if u == 0:
                        sc.w("tensor", sLq[p % 2], ldQ[p])
                        sc.w("tensor", sRr[p % 2], ldR[p])
                    off = b0
                    tmm = None
                    while off < b1:
                        wch = min(512, b1 - off)
                        diag = b1 == kext and off + wch >= b1
                        sc.o("tensor", lambda e, qt=qt, off=off, wch=wch, b0=b0, base=base, p=p: e.matmul(
                            PALL[:, base + off - b0:base + off - b0 + wch],
                            QF[p % 3][:, qt * 128:qt * 128 + 128],
                            KF[p % 3][:, off:off + wch], start=True, stop=False))
                        lastch = off + wch >= b1
                        if diag:
                            dloc = qt * 128 - b0
                            sc.o("tensor", lambda e, dloc=dloc, base=base: e.matmul(
                                PALL[:, base + dloc:base + dloc + 128],
                                cx.IDB[:], cx.NEGDB[:], start=False, stop=False))
                        tmm = sc.o("tensor", lambda e, off=off, wch=wch, b0=b0, base=base, p=p: e.matmul(
                            PALL[:, base + off - b0:base + off - b0 + wch],
                            cx.ONESB[0:5, :], KS5[p % 2][0:5, off:off + wch],
                            start=False, stop=True), sT if lastch else None, 1)
                        off += wch
                    sc.w("vector", sT, tmm)
                    vred = sc.o("vector", lambda e, wb=wb, bi=bi, base=base, pi2=pi2: e.reduce_max(
                        NMB[pi2][:, bi:bi + 1], PALL[:, base:base + wb],
                        axis=AX.X, negate=True), sV, 1)
                    sc.w("scalar", sV, vred)
                    sexp = sc.o("scalar", lambda e, wb=wb, bi=bi, base=base, pi2=pi2: e.activation(
                        ESC[:, 0:wb], PALL[:, base:base + wb], AF.Exp,
                        bias=NMB[pi2][:, bi:bi + 1], scale=1.0,
                        accum_out=SGB[pi2][:, bi:bi + 1]), sSc, 1)
                    exps.append((sSc, sexp))
                    u += 1
                    if p >= 1:
                        want = min(len(tlist), (u * len(tlist)) // 76)
                        while tidx[0] < want:
                            qg_, kc_ = tlist[tidx[0]]
                            tidx[0] += 1
                            p2_score(p - 1, qg_, kc_)
                            while len(p2["pvq"]) > 2:
                                p2_pop_pv()
                gb = (qt // 4) % 2
                qcol = qt % 4
                if nb == 1:
                    sln = sc.o("scalar", lambda e, pi2=pi2: e.activation(
                        LNS[pi2][:], SGB[pi2][:, 0:1], AF.Ln), sSc, 1)
                    sc.w("vector", sSc, sln)
                    cmb_dvgate[pi2] = sc.o("vector", lambda e, gb=gb, qcol=qcol, pi2=pi2: e.tensor_tensor(
                        QAB[gb][:, qcol:qcol + 1], NMB[pi2][:, 0:1], LNS[pi2][:],
                        op=ALU.subtract), sV, 1)
                else:
                    sc.o("vector", lambda e, nb=nb, pi2=pi2: e.tensor_tensor(
                        SGW[pi2][:, 0:nb], Z4[:, 0:nb], NMB[pi2][:, 0:nb], op=ALU.subtract))
                    vmn = sc.o("vector", lambda e, nb=nb, pi2=pi2: e.reduce_max(
                        NMc[pi2][:], SGW[pi2][:, 0:nb], axis=AX.X, negate=True), sV, 1)
                    sc.w("scalar", sV, vmn)
                    sx = sc.o("scalar", lambda e, nb=nb, pi2=pi2: e.activation(
                        EXB[pi2][:, 0:nb], NMB[pi2][:, 0:nb], AF.Exp,
                        bias=NMc[pi2][:], scale=-1.0), sSc, 1)
                    sc.w("vector", sSc, sx)
                    sc.o("vector", lambda e, nb=nb, pi2=pi2: e.tensor_tensor(
                        SGW[pi2][:, 0:nb], SGB[pi2][:, 0:nb], EXB[pi2][:, 0:nb], op=ALU.mult))
                    vsg = sc.o("vector", lambda e, nb=nb, pi2=pi2: e.reduce_sum(
                        SGc[pi2][:], SGW[pi2][:, 0:nb], axis=AX.X), sV, 1)
                    sc.w("scalar", sV, vsg)
                    sln = sc.o("scalar", lambda e, pi2=pi2: e.activation(
                        LNS[pi2][:], SGc[pi2][:], AF.Ln), sSc, 1)
                    sc.w("vector", sSc, sln)
                    cmb_dvgate[pi2] = sc.o("vector", lambda e, gb=gb, qcol=qcol, pi2=pi2: e.tensor_tensor(
                        QAB[gb][:, qcol:qcol + 1], NMc[pi2][:], LNS[pi2][:],
                        op=ALU.subtract), sV, 1)
                nmb_scgate[pi2] = sln

                def emit_qaug_group(g, gbuf, p=p):
                    if qa3_pe[gbuf]:
                        sc.w("vector", sT, qa3_pe[gbuf])
                    l0 = QA3G[gbuf][:, 0:12:3]
                    l1 = QA3G[gbuf][:, 1:12:3]
                    l2 = QA3G[gbuf][:, 2:12:3]
                    sc.o("vector", lambda e: e.tensor_copy(QH4[gbuf][:], QAB[gbuf][:]))
                    sc.o("vector", lambda e: e.tensor_copy(l0, QH4[gbuf][:]))
                    sc.o("vector", lambda e: e.tensor_tensor(QR4[gbuf][:], QAB[gbuf][:], l0, op=ALU.subtract))
                    sc.o("vector", lambda e: e.tensor_copy(QH4[gbuf][:], QR4[gbuf][:]))
                    sc.o("vector", lambda e: e.tensor_copy(l1, QH4[gbuf][:]))
                    sc.o("vector", lambda e: e.tensor_tensor(QR4[gbuf][:], QR4[gbuf][:], l1, op=ALU.subtract))
                    sc.o("vector", lambda e: e.tensor_copy(QH4[gbuf][:], QR4[gbuf][:]))
                    vsp = sc.o("vector", lambda e: e.tensor_copy(l2, QH4[gbuf][:]), sV, 1)
                    sc.w("tensor", sV, vsp)
                    sc.w("tensor", sV, b7["v"])
                    ttr = 0
                    for ql in range(4):
                        ttr = sc.o("tensor", lambda e, ql=ql, gbuf=gbuf: e.transpose(
                            PALL[0:3, 3584 + ql * 128:3712 + ql * 128],
                            QA3G[gbuf][:, ql * 3:ql * 3 + 3], cx.IDF[:]), sT, 1)
                    qa3_pe[gbuf] = ttr
                    sc.w("vector", sT, ttr)
                    if g == 0 and p >= 2:
                        sc.w("vector", sPV, pv_after[p - 2])
                    vv = 0
                    for ql in range(4):
                        vv = sc.o("vector", lambda e, ql=ql, g=g, p=p: e.tensor_copy(
                            QROW3[p % 2][0:3, (g * 4 + ql) * 128:(g * 4 + ql + 1) * 128],
                            PALL[0:3, 3584 + ql * 128:3712 + ql * 128]), sV, 1)
                    b7["v"] = vv
                    qrow_ev[p][g] = vv

                if qt % 4 == 3:
                    if pendg is not None:
                        emit_qaug_group(*pendg)
                    pendg = (qt // 4, gb)
                if qt == 10 and p + 1 < 8:
                    setup(p + 1)
            while tidx[0] < len(tlist):
                qg_, kc_ = tlist[tidx[0]]
                tidx[0] += 1
                p2_score(p - 1, qg_, kc_)
                while len(p2["pvq"]) > 2:
                    p2_pop_pv()
            emit_qaug_group(*pendg)
            pendg = None
            exps_last[p] = exps[-1][1]
            if p >= 1:
                while p2["pvq"]:
                    p2_pop_pv()
                p2_evac_flush()
                fire_ag(p - 1)
        else:
            tlist = [(qg_, kc_) for qg_ in range(8) for kc_ in range(qg_ * 4 + 4)]
            for qg_, kc_ in tlist:
                p2_score(7, qg_, kc_)
                while len(p2["pvq"]) > 2:
                    p2_pop_pv()
            while p2["pvq"]:
                p2_pop_pv()
            p2_evac_flush()
            fire_ag(7)
    sc.w("sync", sPV, sPV.n)



def emit_proj_fused(cx, io):
    nc, sc, E, sm = cx.nc, cx.sc, cx.E, cx.sm
    t = cx.tag
    A16 = E(nc.sbuf_tensor(t + "A16", [128, 32 * 1024], BF16))
    W16 = [E(nc.sbuf_tensor(t + f"W16{i}", [128, 32 * 512], BF16)) for i in range(2)]
    B16 = E(nc.sbuf_tensor(t + "B16", [1, D], BF16))
    EVF = [E(nc.sbuf_tensor(t + f"EVF{i}", [128, 512], F32)) for i in range(4)]
    YC = E(nc.sbuf_tensor(t + "YC", [128, D], F32))
    RC = E(nc.sbuf_tensor(t + "RC", [128, D], F32))
    XOB = E(nc.sbuf_tensor(t + "XOB", [128, D], BF16))
    ST6 = E(nc.sbuf_tensor(t + "ST6", [128, 48], F32))
    MV = E(nc.sbuf_tensor(t + "MV", [128, 2], F32))
    SD = E(nc.sbuf_tensor(t + "SD", [128, 1], F32))
    RS = E(nc.sbuf_tensor(t + "RS", [128, 1], F32))
    PS = [E(nc.psum_tensor(t + f"PS{i}", [128, 512], F32)) for i in range(4)]
    Y2 = YC
    XO = RC
    sX, sMM, sEV, sL, sV = sm["sX"], sm["sMM"], sm["sEV"], sm["sL"], sm["sV"]
    sW = [sm["sW0"], sm["sW1"]]
    sST = [sm["sS0"], sm["sS1"], sm["sS2"], sm["sS3"]]
    xb = sX.n
    sc.o("gpsimd", lambda e: e.dma_start(out=B16[:], in_=io["br"]), sX, 16)
    ret = io["a16_load"](sc, A16, sX)
    asem, xa = ret if ret is not None else (sX, sX.n)
    wbb = [sm["sW0"].n, sm["sW1"].n]
    mmb = sMM.n
    evb = sEV.n
    stb = [x.n for x in sST]

    def wload(w):
        src = io["wtile"](w)
        dstw = W16[w % 2][:].rearrange("p (i e) -> p i e", i=32)
        if w >= 2:
            sc.w("sync", sMM, mmb + (w - 1) * 8)
        sc.o("sync", lambda e, s=src, d=dstw: e.dma_start(out=d, in_=s), sW[w % 2], 16)

    wload(0)
    wload(1)
    n = 0
    for et in range(8):
        sc.w("tensor", sX, xb + 16)
        sc.w("tensor", asem, xa)
        sc.w("tensor", sW[et % 2], wbb[et % 2] + 16 * (et // 2 + 1))
        for tc in range(8):
            n += 1
            bank = (n - 1) % 4
            if n > 4:
                sc.w("tensor", sEV, evb + n - 4)
            for d in range(32):
                sc.o("tensor", lambda e, d=d, b=bank, tc=tc, wp=et % 2: e.matmul(
                    PS[b][:], A16[:, d * 1024 + tc * 128: d * 1024 + tc * 128 + 128],
                    W16[wp][:, d * 512:(d + 1) * 512], start=(d == 0), stop=False))
            sc.o("tensor", lambda e, b=bank, c=et * 512: e.matmul(
                PS[b][:], cx.ONESB[0:1, :], B16[0:1, c:c + 512], start=False, stop=True), sMM, 1)
            sc.w("vector", sMM, mmb + n)
            if n > 4:
                sc.w("vector", sST[bank], stb[bank] + 16 * ((n - 4 - 1) // 4 + 1))
            sc.o("vector", lambda e, b=bank: e.tensor_copy(EVF[b][:], PS[b][:]), sEV, 1)
            sc.w("sync", sEV, sEV.n)
            sc.o("sync", lambda e, t0=tc * 128, c=et * 512, b=bank: e.dma_start(
                out=io["Y"][t0:t0 + 128, c:c + 512], in_=EVF[b][:]), sST[bank], 16)
        if et + 2 < 8:
            wload(et + 2)
    ob = sm["sO0"].n
    prev_vo = None
    for tc in range(8):
        t0 = tc * 128
        if tc == 0:
            for b in range(4):
                sc.w("sync", sST[b], sST[b].n)
        else:
            sc.w("sync", sm["sO0"], ob + 16 * tc)
            sc.w("sync", sV, prev_vo)
        sc.o("sync", lambda e, a=io["Y"][t0:t0 + 128, :]: e.dma_start(out=YC[:], in_=a), sL, 16)
        sc.o("sync", lambda e, a=io["resid"][t0:t0 + 128, :]: e.dma_start(out=RC[:], in_=a), sL, 16)
        sc.w("vector", sL, sL.n)
        if tc > 0:
            sc.w("vector", sm["sO0"], ob + 16 * tc)
        sc.o("vector", lambda e: e.tensor_tensor(Y2[:], YC[:], RC[:], op=ALU.add))
        for cc in range(8):
            sc.o("vector", lambda e, cc=cc: e.bn_stats(ST6[:, cc * 6:(cc + 1) * 6], Y2[:, cc * 512:(cc + 1) * 512]))
        sc.o("vector", lambda e: e.bn_aggr(MV[:], ST6[:]), sV, 1)
        sc.w("scalar", sV, sV.n)
        sq = sc.o("scalar", lambda e: e.activation(SD[:], MV[:, 1:2], AF.Sqrt, bias=cx.EPSC[:], scale=1.0), sm["sSc"], 1)
        sc.w("vector", sm["sSc"], sq)
        sc.o("vector", lambda e: e.reciprocal(RS[:], SD[:]))
        vo = sc.o("vector", lambda e: e.tensor_scalar(XO[:], Y2[:], MV[:, 0:1], RS[:, 0:1], op0=ALU.subtract, op1=ALU.mult), sV, 1)
        if io.get("bounce") is not None:
            if tc > 0:
                sc.w("vector", sm["sC"], sm["sC"].n)
            vo = sc.o("vector", lambda e: e.tensor_copy(XOB[:], XO[:]), sV, 1)
            sc.w("sync", sV, vo)
            sc.o("sync", lambda e, d=io["bounce"][t0:t0 + 128, :]: e.dma_start(out=d, in_=XOB[:]), sm["sC"], 16)
        prev_vo = vo
        sc.w("sync", sV, vo)
        sc.o("sync", lambda e, o=io["xo"][t0:t0 + 128, :]: e.dma_start(out=o, in_=XO[:]), sm["sO0"], 16)
    hn = io.get("heat_n", 0)
    if hn:
        PSH = E(nc.psum_tensor(t + "PSH", [128, 512], F32))
        emit_heat(cx, PSH[:], hn)

def build_fused(stop_after=None):
    import os
    DEQR = int(os.environ.get('DEQR', '4'))
    SKIPA = os.environ.get('SKIPA', '0') == '1'
    DEQM = int(os.environ.get('DEQM', '2'))
    nc = bass.Bass("TRN2", target_bir_lowering=False)
    x_tok = nc.dram_tensor("x_tok", [TOK, D], F32, kind="ExternalInput").ap()
    wq_sl = nc.dram_tensor("wq_sl", [D, 3 * EH], F32, kind="ExternalInput").ap()
    wa_sl = nc.dram_tensor("wa_sl", [D, 3 * EH], F32, kind="ExternalInput").ap()
    shc = nc.dram_tensor("shc", [D, 1024], F32, kind="ExternalInput").ap()
    wqd = nc.dram_tensor("wqd", [512, D], I32, kind="ExternalInput").ap()
    zzd = nc.dram_tensor("zzd", [512, 32], F32, kind="ExternalInput").ap()
    ssd = nc.dram_tensor("ssd", [512, 32], F32, kind="ExternalInput").ap()
    s2d = nc.dram_tensor("s2d", [512, 1], F32, kind="ExternalInput").ap()
    mkd = nc.dram_tensor("mkd", [512, D], I32, kind="ExternalInput").ap()
    q3d = nc.dram_tensor("q3d", [65536, 12], I8, kind="ExternalInput").ap()
    wnd = nc.dram_tensor("wnd", [65536, 1], F32, kind="ExternalInput").ap()
    b3 = nc.dram_tensor("b3", [1, 3 * EH], F32, kind="ExternalInput").ap()
    bpr = nc.dram_tensor("bpr", [1, D], F32, kind="ExternalInput").ap()
    bf0 = nc.dram_tensor("bf0", [1, D], F32, kind="ExternalInput").ap()
    bf1 = nc.dram_tensor("bf1", [1, D], F32, kind="ExternalInput").ap()
    zrow = nc.dram_tensor("zrow", [1, D], F32, kind="ExternalInput").ap()
    gq_sl = nc.dram_tensor("gq_sl", [1, EH], F32, kind="ExternalInput").ap()
    gk_sl = nc.dram_tensor("gk_sl", [1, EH], F32, kind="ExternalInput").ap()
    ub2 = nc.dram_tensor("ub2", [2, 8, 128], F32, kind="ExternalInput").ap()
    uw2 = nc.dram_tensor("uw2", [2, 8, 128], F32, kind="ExternalInput").ap()
    ident = nc.dram_tensor("ident", [128, 128], F32, kind="ExternalInput").ap()
    identb = nc.dram_tensor("identb", [128, 128], BF16, kind="ExternalInput").ap()
    negd = nc.dram_tensor("negd", [128, 128], F32, kind="ExternalInput").ap()
    trit = nc.dram_tensor("trit", [128, 896], F32, kind="ExternalInput").ap()
    ahi = nc.dram_tensor("ahi", [8, S], BF16, kind="ExternalInput").ap()
    ami = nc.dram_tensor("ami", [8, S], BF16, kind="ExternalInput").ap()
    alo = nc.dram_tensor("alo", [8, S], BF16, kind="ExternalInput").ap()
    acolt = nc.dram_tensor("acolt", [8, 128, 32], F32, kind="ExternalInput").ap()
    tofs = nc.dram_tensor("tofs", [1, 1], I32, kind="ExternalInput").ap()
    xout = nc.dram_tensor("xout", [TOK, D], F32, kind="ExternalOutput").ap()
    DBGF = os.environ.get("DBGF", "0") == "1"
    if DBGF:
        dago = nc.dram_tensor("dago", [4096, S], BF16, kind="ExternalOutput").ap()
        dagx = nc.dram_tensor("dagx", [4 * D, TOK], BF16, kind="ExternalOutput").ap()
        _dqdt = F32 if os.environ.get("DBGL", "0") == "0" else BF16
        dqtm = nc.dram_tensor("dqtm", [S, EH], _dqdt, kind="ExternalOutput").ap()
        dktm = nc.dram_tensor("dktm", [S, EH], _dqdt, kind="ExternalOutput").ap()
        dvtm = nc.dram_tensor("dvtm", [S, EH], BF16, kind="ExternalOutput").ap()
        dq2 = nc.dram_tensor("dq2", [EH, S], BF16, kind="ExternalOutput").ap()
        dk2 = nc.dram_tensor("dk2", [EH, S], BF16, kind="ExternalOutput").ap()
        dar = nc.dram_tensor("dar", [128, 128], F32, kind="ExternalOutput").ap()
        ds1 = nc.dram_tensor("ds1", [TOK, D], F32, kind="ExternalOutput").ap()
        ds2 = nc.dram_tensor("ds2", [TOK, D], F32, kind="ExternalOutput").ap()
        dagw = nc.dram_tensor("dagw", [32 * D, 512], BF16, kind="ExternalOutput").ap()
        dagxin = nc.dram_tensor("dagxin", [D, TOK], BF16, kind="ExternalOutput").ap()

    WQ16 = nc.dram_tensor("WQ16", [D, 3 * EH], BF16).ap()
    WA16 = nc.dram_tensor("WA16", [D, 3 * EH], BF16).ap()
    agw_in = nc.dram_tensor("agw_in", [4 * D, 512], BF16).ap()
    agw_out = nc.dram_tensor("agw_out", [32 * D, 512], BF16).ap()
    agx_in = nc.dram_tensor("agx_in", [D, TOK], BF16).ap()
    agx_out = nc.dram_tensor("agx_out", [4 * D, TOK], BF16).ap()
    agxt = nc.dram_tensor("agxt", [4 * TOK, D], BF16).ap()
    ar_in = nc.dram_tensor("ar_in", [128, 128], F32).ap()
    ar_out = nc.dram_tensor("ar_out", [128, 128], F32).ap()
    agoi = [nc.dram_tensor(f"agoi{p}", [128, S], BF16).ap() for p in range(8)]
    ago8 = [nc.dram_tensor(f"ago8_{p}", [512, S], BF16).ap() for p in range(8)]
    ckd = [dict(h1=nc.dram_tensor(f"ckdh1{i}", [128, 128], BF16).ap(),
                l1=nc.dram_tensor(f"ckdl1{i}", [128, 128], BF16).ap(),
                h2=nc.dram_tensor(f"ckdh2{i}", [32, 128], BF16).ap(),
                l2=nc.dram_tensor(f"ckdl2{i}", [32, 128], BF16).ap()) for i in range(2)]
    qtm0 = nc.dram_tensor("qtm0", [S, EH], F32).ap()
    ktm0 = nc.dram_tensor("ktm0", [S, EH], F32).ap()
    vtm0 = nc.dram_tensor("vtm0", [S, EH], BF16).ap()
    qtm1 = nc.dram_tensor("qtm1", [S, EH], BF16).ap()
    ktm1 = nc.dram_tensor("ktm1", [S, EH], BF16).ap()
    vtm1 = nc.dram_tensor("vtm1", [S, EH], BF16).ap()
    q2fm = nc.dram_tensor("q2fm", [EH, S], BF16).ap()
    k2fm = nc.dram_tensor("k2fm", [EH, S], BF16).ap()
    xfm = nc.dram_tensor("xfm", [D, TOK], BF16).ap()
    s1 = nc.dram_tensor("s1", [TOK, D], F32).ap()
    s2 = nc.dram_tensor("s2", [TOK, D], F32).ap()
    s3 = nc.dram_tensor("s3", [TOK, D], F32).ap()
    Y = nc.dram_tensor("Ybuf", [TOK, D], F32).ap()
    tmb = nc.dram_tensor("tmb", [TOK, D], BF16).ap()
    wdq0 = nc.dram_tensor("wdq0", [512, D], BF16).ap()
    wdq1 = nc.dram_tensor("wdq1", [512, D], BF16).ap()
    qbm = nc.dram_tensor("qbm", [S, EH], BF16).ap()
    kbm = nc.dram_tensor("kbm", [S, EH], BF16).ap()

    with contextlib.ExitStack() as st0:
        E0 = st0.enter_context
        cx = _Ctx()
        cx.nc = nc
        cx.sm = _mk_sems(nc, E0, SEM_NAMES)
        cx.sc = Sched(chains={"vector": cx.sm["cV"], "scalar": cx.sm["cS"]})
        sc, sm = cx.sc, cx.sm
        cx.IDF = E0(nc.sbuf_tensor("gIDF", [128, 128], F32))
        cx.IDB = E0(nc.sbuf_tensor("gIDB", [128, 128], BF16))
        cx.NEGD = E0(nc.sbuf_tensor("gNEGD", [128, 128], F32))
        cx.TRIT = E0(nc.sbuf_tensor("gTRIT", [128, 896], F32))
        cx.ONESB = E0(nc.sbuf_tensor("gONESB", [8, 128], BF16))
        cx.EPSC = E0(nc.sbuf_tensor("gEPSC", [128, 1], F32))
        cx.HEATR = E0(nc.sbuf_tensor("gHEATR", [128, 512], BF16))
        cx.NEGDB = E0(nc.sbuf_tensor("gNEGDB", [128, 128], BF16))
        cx.TRITB = E0(nc.sbuf_tensor("gTRITB", [128, 896], BF16))
        cx.tmb = tmb
        TOFS = E0(nc.sbuf_tensor("gTOFS", [1, 1], I32))
        blk = E0(nc.Block())

        sc.o("sync", lambda e: e.dma_start(out=cx.IDF[:], in_=ident), sm["sL"], 16)
        sc.o("sync", lambda e: e.dma_start(out=cx.IDB[:], in_=identb), sm["sL"], 16)
        sc.o("sync", lambda e: e.dma_start(out=cx.NEGD[:], in_=negd), sm["sL"], 16)
        sc.o("sync", lambda e: e.dma_start(out=cx.TRIT[:], in_=trit), sm["sL"], 16)
        sc.o("vector", lambda e: e.memset(cx.ONESB[:], 1.0))
        sc.o("vector", lambda e: e.memset(cx.HEATR[:], 0.0))
        sc.w("vector", sm["sL"], sm["sL"].n)
        sc.o("vector", lambda e: e.tensor_copy(cx.NEGDB[:], cx.NEGD[:]))
        sc.o("vector", lambda e: e.tensor_copy(cx.TRITB[:], cx.TRIT[:]), sm["sV"], 1)
        sc.o("vector", lambda e: e.memset(cx.EPSC[:], EPS), sm["sV"], 1)
        sc.o("gpsimd", lambda e: e.dma_start(out=TOFS[:], in_=tofs), sm["sX"], 16)
        sc.w("gpsimd", sm["sX"], sm["sX"].n)

        def _initreg(e):
            cx.treg = e.alloc_register("tofsreg")
            return e.reg_load(cx.treg, TOFS[0:1, 0:1])
        sc.o("gpsimd", _initreg)

        if stop_after == "pre":
            def _finish0():
                sc.o("sync", lambda e: e.dma_start(out=xout, in_=x_tok), sm["sO1"], 16)
                _barrier(cx)
            _finish0()
            sc.emit(blk)
            return nc

        with contextlib.ExitStack() as st:
            cx.E = st.enter_context
            cx.tag = "l0p1"
            emit_tm2fm(cx, x_tok, agx_in, TOK, D, cast=True, bounce_only=True)
            sc.w("gpsimd", cx.sm["sC"], cx.sm["sC"].n)
            sc.o("gpsimd", lambda e: e.collective_compute(
                "AllGather", ALU.bypass, GROUPS_B, ins=[tmb], outs=[agxt]),
                cx.sm["sAGX"], 1)
            _barrier(cx, exclude=("sAGW", "sAGX"))

        with contextlib.ExitStack() as st:
            cx.E = st.enter_context
            cx.tag = "p0a"
            items = [(shc[:, 0:512], agw_in[0:D, :]),
                     (shc[:, 512:1024], agw_in[D:2 * D, :])]
            if SKIPA:
                items = []
            emit_cast_list(cx, items)
            _barrier(cx, exclude=("sAGW", "sAGX"))

        if SKIPA or stop_after == "p0a":
            pass
        if stop_after == "p0a":
            def _finish0():
                sc.o("sync", lambda e: e.dma_start(out=xout, in_=x_tok), sm["sO1"], 16)
                _barrier(cx)
            _finish0()
            sc.emit(blk)
            return nc

        with contextlib.ExitStack() as st:
            cx.E = st.enter_context
            cx.tag = "p0b"
            emit_dequant_fused(cx, dict(wq=wqd, zz=zzd, ss=ssd, s2=s2d, mk=mkd,
                                        q3=q3d, wn=wnd, agw_in=agw_in,
                                        wdq0=wdq0, wdq1=wdq1),
                               do_tr=stop_after not in ("p0dq",),
                               do_dq=stop_after not in ("p0tr", "p0tr3"),
                               do_st=stop_after != "p0tr2",
                               do_pe=stop_after != "p0tr3")
            sc.w("gpsimd", sm["sS0"], sm["sS0"].n)
            sc.w("gpsimd", sm["sS1"], sm["sS1"].n)
            sc.w("gpsimd", sm["sC"], sm["sC"].n)
            if stop_after in ("p0nc", "p0dq", "p0tr", "p0tr2", "p0tr3"):
                def _finish0():
                    sc.o("sync", lambda e: e.dma_start(out=xout, in_=x_tok), sm["sO1"], 16)
                    _barrier(cx)
                _finish0()
                sc.emit(blk)
                return nc
            sc.o("gpsimd", lambda e: e.collective_compute(
                "AllGather", ALU.bypass, GROUPS_ALL, ins=[agw_in], outs=[agw_out]),
                sm["sAGW"], 1)
            _barrier(cx, exclude=("sAGW", "sAGX"))

        def _finish():
            sc.o("sync", lambda e: e.dma_start(out=xout, in_=x_tok), sm["sO1"], 16)
            _barrier(cx)

        if stop_after == "p0":
            _finish()
            sc.emit(blk)
            return nc

        layer_io = [
            dict(xsrc=x_tok, xcast=True, wT=wq_sl, qtm=qtm0, ktm=ktm0, vtm=vtm0,
                 resid_p=x_tok, s_attn=s1, s_ffn=s2, wm_out=0, wm_ffn=2,
                 br_out=zrow, br_ffn=bf0),
            dict(xsrc=s2, xcast=True, wT=wa_sl, qtm=qtm1, ktm=ktm1, vtm=vtm1,
                 resid_p=s2, s_attn=s3, s_ffn=xout, wm_out=1, wm_ffn=3,
                 br_out=bpr, br_ffn=bf1),
        ]

        for L in (0, 1):
            io = layer_io[L]
            if L == 1:
                with contextlib.ExitStack() as st:
                    cx.E = st.enter_context
                    cx.tag = f"l{L}p1"
                    sc.w("gpsimd", sm["sC"], sm["sC"].n)
                    sc.o("gpsimd", lambda e: e.collective_compute(
                        "AllGather", ALU.bypass, GROUPS_B, ins=[tmb], outs=[agxt]),
                        sm["sAGX"], 1)
                    _barrier(cx, exclude=("sAGW",))

            if stop_after == "p1" and L == 0:
                _finish()
                sc.emit(blk)
                return nc

            with contextlib.ExitStack() as st:
                cx.E = st.enter_context
                cx.tag = f"l{L}p2"
                sc.w("sync", sm["sAGX"], sm["sAGX"].n)
                qio = dict(agxt=agxt, wT=io["wT"],
                           outs=[io["qtm"], io["ktm"], io["vtm"]],
                           ar_in=ar_in, ar_out=ar_out)
                if L == 1:
                    qio["b3"] = b3
                emit_qkv_fused(cx, L, qio)
                _barrier(cx, exclude=("sAGW",))

            if stop_after == "p2" and L == 0:
                _finish()
                sc.emit(blk)
                return nc

            if L == 0:
                with contextlib.ExitStack() as st:
                    cx.E = st.enter_context
                    cx.tag = f"l{L}p3"
                    emit_qkfin_fused(cx, L, dict(qtm=io["qtm"], ktm=io["ktm"],
                                                 q2fm=q2fm, k2fm=k2fm, qbm=qbm, kbm=kbm,
                                                 ar_out=ar_out, gq=gq_sl, gk=gk_sl))
                    _barrier(cx, exclude=("sAGW",))

            if stop_after == "p3" and L == 0:
                _finish()
                sc.emit(blk)
                return nc

            with contextlib.ExitStack() as st:
                cx.E = st.enter_context
                cx.tag = f"l{L}p4"
                emit_attn_fused(cx, L, dict(
                    q2fm=qbm if L == 0 else io["qtm"],
                    k2fm=kbm if L == 0 else io["ktm"],
                    vtm=io["vtm"], agoi=agoi, ago8=ago8,
                    ckd=ckd, ahi=ahi, ami=ami, alo=alo, acolt=acolt,
                    ub8=ub2[L], uw8=uw2[L]))
                _barrier(cx, exclude=("sAGW",))

            if stop_after == "p4" and L == 0:
                _finish()
                sc.emit(blk)
                return nc

            with contextlib.ExitStack() as st:
                cx.E = st.enter_context
                cx.tag = f"l{L}p5"
                if L == 0:
                    sc.w("sync", sm["sAGW"], 1)
                    sc.w("gpsimd", sm["sAGW"], 1)
                sc.w("gpsimd", sm["sAGO"], sm["sAGO"].n)

                def a16_load(sc_, A16, sX, _cx=cx, _ago8=ago8):
                    for sp in range(8):
                        static = _ago8[sp].rearrange("(c q) t -> q c t", q=128)[:, :, 0:1024]

                        def _do(e, sp=sp, static=static):
                            src = AP(_ago8[sp].tensor, _cx.treg, static.ap)
                            dst = A16[:].rearrange("q (g t) -> q g t", g=32)[:, sp:32:8, :]
                            return e.dma_start(out=dst, in_=src)
                        sc_.o("gpsimd", _do, sm["sX"], 16)
                    return None

                m = io["wm_out"]
                emit_proj_fused(cx, dict(
                    a16_load=a16_load,
                    wtile=lambda w, m=m: agw_out[w * 16384 + m * 4096:w * 16384 + (m + 1) * 4096, 0:512].rearrange("(i p) e -> p i e", p=128),
                    br=io["br_out"], resid=io["resid_p"], Y=Y, xo=io["s_attn"],
                    bounce=tmb))
                _barrier(cx, exclude=("sAGW",) if L == 0 else ())

            if stop_after == "p5" and L == 0:
                _finish()
                sc.emit(blk)
                return nc

            with contextlib.ExitStack() as st:
                cx.E = st.enter_context
                cx.tag = f"l{L}p6b"

                def a16s(sc_, A16, sX, _x=tmb):
                    for d_ in range(32):
                        sc_.o("sync", lambda e, d_=d_: e.dma_start(
                            out=A16[:, d_ * 1024:(d_ + 1) * 1024],
                            in_=_x[0:1024, d_ * 128:(d_ + 1) * 128], transpose=True),
                            sm["sL"], 16)
                    return (sm["sL"], sm["sL"].n)

                m = io["wm_ffn"]
                emit_proj_fused(cx, dict(
                    a16_load=a16s,
                    wtile=lambda w, m=m: agw_out[w * 16384 + m * 4096:w * 16384 + (m + 1) * 4096, 0:512].rearrange("(i p) e -> p i e", p=128),
                    br=io["br_ffn"], resid=io["s_attn"], Y=Y, xo=io["s_ffn"],
                    bounce=tmb if L == 0 else None))
                _barrier(cx)
            if stop_after == "p6b" and L == 0:
                _finish()
                sc.emit(blk)
                return nc

            if DBGF and L == int(os.environ.get("DBGL", "0")):
                qt_, kt_, vt_ = (qtm0, ktm0, vtm0) if L == 0 else (qtm1, ktm1, vtm1)
                for p_ in range(8):
                    sc.o("sync", lambda e, a=ago8[p_], b=dago[p_ * 512:(p_ + 1) * 512, :]: e.dma_start(out=b, in_=a), sm["sO1"], 16)
                for dsrc, ddst in [(agx_in, dagxin), (agx_out, dagx), (qt_, dqtm),
                                   (kt_, dktm), (vt_, dvtm), (q2fm, dq2),
                                   (k2fm, dk2), (ar_out, dar), (s1, ds1), (s2, ds2),
                                   (agw_out, dagw)]:
                    sc.o("sync", lambda e, a=dsrc, b=ddst: e.dma_start(out=b, in_=a), sm["sO1"], 16)
                sc.w("sync", sm["sO1"], sm["sO1"].n)

        sc.emit(blk)
    return nc


def kernel(L0_attn_hidden_states, L0_attn_Wqkv, L0_attn_q_ln_w, L0_attn_q_ln_b,
           L0_attn_k_ln_w, L0_attn_k_ln_b, L0_attn_Wout, L0_ffn_W_q, L0_ffn_scales,
           L0_ffn_zeros, L0_ffn_mask, L0_ffn_scale2, L0_ffn_bias, L1_attn_w_attn,
           L1_attn_b_attn, L1_attn_w_proj, L1_attn_b_proj, L1_ffn_weight_q3,
           L1_ffn_weight_norm, L1_ffn_bias):
    x = np.asarray(L0_attn_hidden_states, dtype=np.float32).reshape(2 * S, D)
    WqT = np.ascontiguousarray(np.asarray(L0_attn_Wqkv, np.float32).T)
    WoT = np.ascontiguousarray(np.asarray(L0_attn_Wout, np.float32).T)
    WA = np.asarray(L1_attn_w_attn, np.float32)
    WP = np.asarray(L1_attn_w_proj, np.float32)
    zz = np.asarray(L0_ffn_zeros, np.float32).reshape(D, 32)
    ss = np.asarray(L0_ffn_scales, np.float32).reshape(D, 32)
    s2v = np.asarray(L0_ffn_scale2, np.float32).reshape(D, 1)
    wqi = np.asarray(L0_ffn_W_q, np.int32)
    mki = np.asarray(L0_ffn_mask, np.int32)
    q3i = np.asarray(L1_ffn_weight_q3, np.int8)
    wni = np.asarray(L1_ffn_weight_norm, np.float32)
    gq = np.asarray(L0_attn_q_ln_w, np.float32)
    gk = np.asarray(L0_attn_k_ln_w, np.float32)
    bq = np.asarray(L0_attn_q_ln_b, np.float32)
    b_attn = np.asarray(L1_attn_b_attn, np.float32)
    C = _consts()

    def _slice3(W, hs):
        return np.ascontiguousarray(np.concatenate(
            [W[:, hs * 128:hs * 128 + EH],
             W[:, D + hs * 128:D + hs * 128 + EH],
             W[:, 2 * D + hs * 128:2 * D + hs * 128 + EH]], axis=1))

    nc = build_fused()
    maps = []
    for c in range(NCORE):
        hs = (c % 4) * 8
        r0, r1 = c * 512, (c + 1) * 512
        zz8 = np.zeros((8, 128), np.float32)
        maps.append(dict(
            x_tok=np.ascontiguousarray(x[c * TOK:(c + 1) * TOK]),
            wq_sl=_slice3(WqT, hs), wa_sl=_slice3(WA, hs),
            shc=np.ascontiguousarray(np.concatenate(
                [WoT[:, c * 512:(c + 1) * 512], WP[:, c * 512:(c + 1) * 512]], axis=1)),
            wqd=wqi[r0:r1], zzd=zz[r0:r1], ssd=ss[r0:r1], s2d=s2v[r0:r1], mkd=mki[r0:r1],
            q3d=q3i[r0 * 128:r1 * 128], wnd=wni[r0 * 128:r1 * 128],
            b3=np.ascontiguousarray(np.concatenate(
                [b_attn[hs * 128:hs * 128 + EH], b_attn[D + hs * 128:D + hs * 128 + EH],
                 b_attn[2 * D + hs * 128:2 * D + hs * 128 + EH]])[None, :]),
            bpr=np.asarray(L1_attn_b_proj, np.float32)[None, :],
            bf0=np.asarray(L0_ffn_bias, np.float32)[None, :],
            bf1=np.asarray(L1_ffn_bias, np.float32)[None, :],
            zrow=np.zeros((1, D), np.float32),
            gq_sl=np.ascontiguousarray(gq[hs * 128:hs * 128 + EH][None, :]),
            gk_sl=np.ascontiguousarray(gk[hs * 128:hs * 128 + EH][None, :]),
            ub2=np.stack([bq[hs * 128:(hs + 8) * 128].reshape(8, 128), zz8]),
            uw2=np.stack([gk[hs * 128:(hs + 8) * 128].reshape(8, 128), zz8]),
            ident=C["ident"], identb=C["identb"], negd=C["negd"], trit=C["trit"],
            ahi=np.ascontiguousarray(C["AHI"][hs:hs + 8]),
            ami=np.ascontiguousarray(C["AMI"][hs:hs + 8]),
            alo=np.ascontiguousarray(C["ALO"][hs:hs + 8]),
            acolt=np.ascontiguousarray(C["ACOLT"][hs:hs + 8]),
            tofs=np.array([[(c % 4) * 1024]], np.int32)))
    rr = _run(nc, maps)
    out = np.concatenate([np.asarray(rr[c]["xout"]) for c in range(NCORE)], axis=0)
    return out.reshape(2, S, D).astype(np.float32)
